# version 04
# baseline (speedup 1.0000x reference)
import numpy as np
from contextlib import ExitStack
import concourse.bass as bass
import concourse.mybir as mybir
from concourse.bass_utils import run_bass_kernel_spmd

F32 = mybir.dt.float32
BF16 = mybir.dt.bfloat16
I8 = mybir.dt.int8
AF = mybir.ActivationFunctionType
ALU = mybir.AluOpType

S = 2048
D = 1024
P2 = 2048
HID = 512
H = 8
HD = 64
NE = 32
NEX = 34
INTER = 512
SB_BYTES = 206 * 1024


def _host_consts():
    c = {}
    c["cident"] = np.eye(128, dtype=np.float32)
    ind8 = np.zeros((8, 1024), np.float32)
    for h in range(8):
        ind8[h, h * 128:(h + 1) * 128] = 1.0
    c["cind8"] = ind8
    s = np.arange(128)
    m1 = np.where(s[:, None] <= s[None, :], 0.0, -1e30).astype(np.float32)
    c["cmask"] = np.tile(m1, (1, 8)).astype(np.float32)
    par2 = np.zeros((2, 128), np.float32)
    par2[0, :64] = 1.0
    par2[1, 64:] = 1.0
    c["cpar2"] = par2
    dd = np.arange(128)
    bm = np.zeros((4, 128, 128), np.float32)
    b2m = np.zeros((4, 128, 128), np.float32)
    for m in range(4):
        bm[m] = (dd[:, None] == (dd[None, :] - 3 + m)).astype(np.float32)
        b2m[m] = (dd[:, None] == (dd[None, :] - 3 + m + 128)).astype(np.float32)
    c["cband"] = bm
    c["cband2"] = b2m
    return c


class Buf:
    __slots__ = ("name", "writers", "readers")

    def __init__(self, name):
        self.name = name
        self.writers = []
        self.readers = {}


class Prog:
    STREAMS = ("pe", "dve", "act", "pool", "sync")

    def __init__(self):
        self.steps = []
        self.stream_steps = {s: [] for s in self.STREAMS}
        self.n_lanes = 17
        self.lane_count = [0] * self.n_lanes
        self.lane_rr = 0
        self.lane_rr_sw = 0

    def stepw(self, stream, fn, reads=(), writes=(), dma=False, tag="",
              tpose=False):
        sid = len(self.steps)
        deps = set()
        for b in reads:
            deps.update(b.writers)
        for b in writes:
            deps.update(b.readers.values())
            deps.update(b.writers)
        deps.discard(sid)
        if dma:
            if tpose:
                lane = 16
            elif stream == "pool":
                lane = 12 + self.lane_rr_sw % 4
                self.lane_rr_sw += 1
            else:
                lane = self.lane_rr % 12
                self.lane_rr += 1
            self.lane_count[lane] += 1
            pos = self.lane_count[lane]
        else:
            lane, pos = None, None
        st = dict(sid=sid, stream=stream, fn=fn, deps=sorted(deps), dma=dma,
                  lane=lane, lanepos=pos, tag=tag, marked=False, semval=None)
        self.steps.append(st)
        self.stream_steps[stream].append(sid)
        for b in reads:
            b.readers[stream] = sid
        for b in writes:
            b.writers = [sid]
            b.readers = {}
        return sid

    NEAR = 6

    def finalize(self):
        order = {}
        for s in self.STREAMS:
            for i, sid in enumerate(self.stream_steps[s]):
                order[sid] = i
        for st in self.steps:
            for d in st["deps"]:
                dst = self.steps[d]
                if dst["dma"] or dst["stream"] != st["stream"]:
                    dst["marked"] = True
                elif (st["stream"] in ("dve", "act", "pool")
                      and order[st["sid"]] - order[d] < self.NEAR):
                    dst["marked"] = True
                    st.setdefault("near", []).append(d)
        self.stream_totals = {}
        for s in self.STREAMS:
            v = 0
            for sid in self.stream_steps[s]:
                st = self.steps[sid]
                if st["dma"]:
                    continue
                if st["marked"]:
                    v += 1
                    st["semval"] = v
            self.stream_totals[s] = v
        know = {s: {} for s in self.STREAMS}
        snap = [None] * len(self.steps)
        for st in self.steps:
            s = st["stream"]
            k = know[s]
            waits = []
            near = st.get("near", ())
            for d in st["deps"]:
                dst = self.steps[d]
                if dst["dma"]:
                    key = ("l", dst["lane"])
                    val = dst["lanepos"] * 16
                else:
                    if dst["stream"] == s and d not in near:
                        dk = snap[d]
                        for kk, vv in dk.items():
                            if k.get(kk, 0) < vv:
                                k[kk] = vv
                        continue
                    key = ("e", dst["stream"])
                    val = dst["semval"]
                if k.get(key, 0) >= val:
                    continue
                waits.append((key, val))
                k[key] = val
                dk = snap[d]
                for kk, vv in dk.items():
                    if k.get(kk, 0) < vv:
                        k[kk] = vv
            st["waits"] = waits
            if st["dma"]:
                k2 = dict(k)
                k2[("l", st["lane"])] = st["lanepos"] * 16
                snap[st["sid"]] = k2
            elif st["marked"]:
                k2 = dict(k)
                k2[("e", s)] = st["semval"]
                snap[st["sid"]] = k2
            else:
                snap[st["sid"]] = k


def _emit(prog, stream, eng, esems, lsems):
    for sid in prog.stream_steps[stream]:
        st = prog.steps[sid]
        for (key, val) in st["waits"]:
            if key[0] == "e":
                eng.wait_ge(esems[key[1]], val)
            else:
                eng.wait_ge(lsems[key[1]], val)
        if st["dma"] and st["lanepos"] > 1:
            eng.wait_ge(lsems[st["lane"]], (st["lanepos"] - 1) * 16)
        inst = st["fn"](eng)
        if st["dma"]:
            inst.then_inc(lsems[st["lane"]], 16)
        elif st["marked"]:
            inst.then_inc(esems[stream], 1)


def build_program(debug_dumps=(), stop_after=None):
    nc = bass.Bass("TRN2", target_bir_lowering=False,
                   detect_race_conditions=False)

    def inp(name, shape, dtype=F32):
        return nc.dram_tensor(name, list(shape), dtype,
                              kind="ExternalInput").ap()

    d_x = inp("x", (S, D))
    d_lng = inp("ln_g", (D,)); d_lnb = inp("ln_b", (D,))
    d_mhg = inp("mh_g", (H, HD)); d_mhb = inp("mh_b", (H, HD))
    d_wupl = inp("wupl", (P2, D)); d_bupl = inp("bupl", (P2,))
    d_wupr = inp("wupr", (HID, D)); d_bupr = inp("bupr", (HID,))
    d_wdown = inp("wdown", (D, HID)); d_bdown = inp("bdown", (D,))
    d_convw = inp("convw", (4,)); d_convb = inp("convb", (1,))
    d_wskip = inp("wskip", (HID, P2)); d_bskip = inp("bskip", (HID,))
    d_wi = inp("wi", (H, P2)); d_bi = inp("bi", (H, 1))
    d_wf = inp("wf", (H, P2)); d_bf = inp("bf", (H, 1))
    d_wo = inp("wo", (HID, P2)); d_bo = inp("bo", (HID,))
    d_wq = inp("wq", (HID, P2)); d_bq = inp("bq", (HID,))
    d_wk = inp("wk", (HID, P2)); d_bk = inp("bk", (HID,))
    d_wv = inp("wv", (HID, P2)); d_bv = inp("bv", (HID,))
    d_gw = inp("gw", (D, NE)); d_gb = inp("gb", (NE,))
    d_w1 = inp("w1", (NE, D, INTER)); d_b1 = inp("b1", (NE, INTER))
    d_w3 = inp("w3", (NE, D, INTER)); d_b3 = inp("b3", (NE, INTER))
    d_w2 = inp("w2", (NE, INTER, D)); d_b2 = inp("b2", (NE, D))
    d_sw1 = inp("sw1", (D, 2 * INTER)); d_sb1 = inp("sb1", (1, 2 * INTER))
    d_sw3 = inp("sw3", (D, 2 * INTER)); d_sb3 = inp("sb3", (1, 2 * INTER))
    d_sw2 = inp("sw2", (2 * INTER, D)); d_sb2 = inp("sb2", (D,))
    d_cident = inp("cident", (128, 128))
    d_cind8 = inp("cind8", (8, 1024))
    d_cmask = inp("cmask", (128, 1024))
    d_cpar2 = inp("cpar2", (2, 128))
    d_cband = inp("cband", (4, 128, 128))
    d_cband2 = inp("cband2", (4, 128, 128))

    d_final = nc.dram_tensor("x2", [S, D], F32, kind="ExternalOutput").ap()

    def scratch(name, shape, dtype):
        return nc.dram_tensor(name, list(shape), dtype, kind="Internal").ap()

    d_xn = scratch("s_xn", (S, D), BF16)
    d_wuplb = scratch("s_wuplb", (P2, D), BF16)
    d_wuprb = scratch("s_wuprb", (HID, D), BF16)
    d_wvb = scratch("s_wvb", (HID, P2), BF16)
    d_wob = scratch("s_wob", (HID, P2), BF16)
    d_wqb = scratch("s_wqb", (HID, P2), BF16)
    d_wkb = scratch("s_wkb", (HID, P2), BF16)
    d_wskipb = scratch("s_wskipb", (HID, P2), BF16)
    d_wib = scratch("s_wib", (H, P2), BF16)
    d_wfb = scratch("s_wfb", (H, P2), BF16)
    d_wdownb = scratch("s_wdownb", (D, HID), BF16)
    d_k = scratch("s_k", (HID, S), BF16)
    d_o = scratch("s_o", (S, HID), BF16)
    d_skip = scratch("s_skip", (S, HID), BF16)
    d_v = scratch("s_v", (S, HID), BF16)
    d_rt = scratch("s_rt", (S, HID), BF16)
    d_outp = scratch("s_outp", (S, HID), BF16)
    d_x1 = scratch("s_x1", (S, D), F32)
    d_x1b = scratch("s_x1b", (S, D), BF16)
    d_negw3 = scratch("s_negw3", (16, 8, 128), F32)
    d_u3 = scratch("s_u3", (16, 8, 128), F32)
    d_mprev2 = scratch("s_mprev2", (16, 8), F32)
    d_negwl2 = scratch("s_negwl2", (16, 8), F32)
    d_phi2 = scratch("s_phi2", (16, 8), F32)
    d_cw = scratch("s_cw", (NE, S), BF16)

    dumps = {}
    def dumpt(name, shape, dtype=F32):
        dumps[name] = nc.dram_tensor("dump_" + name, list(shape), dtype,
                                     kind="ExternalOutput").ap()
        return dumps[name]

    P = Prog()
    bufs = {}
    def B(name):
        if name not in bufs:
            bufs[name] = Buf(name)
        return bufs[name]

    stack = ExitStack()
    arena = stack.enter_context(nc.sbuf_tensor("arena", [128, SB_BYTES], I8))
    arena_ap = arena[:, :]
    free_ranges = [(0, SB_BYTES)]
    allocs = {}
    graveyard = []

    def newv(name, parts, elems, dtype, bufs_=(), side="lo"):
        esz = 2 if dtype == BF16 else (1 if dtype == I8 else 4)
        sz = elems * esz
        align = 64
        sz = (sz + align - 1) // align * align
        off = None
        idxs = (range(len(free_ranges)) if side == "lo"
                else range(len(free_ranges) - 1, -1, -1))
        for i in idxs:
            a, b = free_ranges[i]
            if side == "lo":
                a2 = (a + align - 1) // align * align
            else:
                a2 = (b - sz) // align * align
                if a2 < a:
                    continue
            if b - a2 >= sz and a2 >= a:
                off = a2
                rest = []
                if a2 > a:
                    rest.append((a, a2))
                if a2 + sz < b:
                    rest.append((a2 + sz, b))
                free_ranges[i:i + 1] = rest
                break
        if off is None:
            used = sum(v[1] for v in allocs.values())
            lay = sorted((v[0], v[1], k) for k, v in allocs.items())
            raise MemoryError(f"SBUF full for {name} ({sz}B); used={used}; "
                              f"free={free_ranges}; layout={lay}")
        bufnames = [name] + list(bufs_)
        allocs[name] = (off, sz, bufnames)
        old = set()
        for (ga, gb_, sids) in graveyard:
            if not (gb_ <= off or ga >= off + sz):
                old.update(sids)
        if old:
            for bn in bufnames:
                B(bn).writers = sorted(set(B(bn).writers) | old)
        return arena_ap[0:parts, off:off + elems * esz].bitcast(dtype)

    def sfree(name):
        off, sz, bufnames = allocs.pop(name)
        sids = set()
        for bn in bufnames:
            sids.update(B(bn).writers)
            sids.update(B(bn).readers.values())
        graveyard.append((off, off + sz, sorted(sids)))
        free_ranges.append((off, off + sz))
        free_ranges.sort()
        merged = []
        for r in free_ranges:
            if merged and merged[-1][1] >= r[0]:
                merged[-1] = (merged[-1][0], max(merged[-1][1], r[1]))
            else:
                merged.append(list(r))
        free_ranges[:] = [tuple(r) for r in merged]

    pa = stack.enter_context(nc.psum_tensor("pa", [128, 4096], F32))
    def bank(k, cols=512):
        return pa[:, k * 512:k * 512 + cols]
    pbuf = [B(f"psum{k}") for k in range(8)]


    def dma(stream, dst, src, reads, writes, tag="", **kw):
        return P.stepw(stream,
                       (lambda eng, dst=dst, src=src, kw=kw:
                        eng.dma_start(out=dst, in_=src, **kw)),
                       reads=reads, writes=writes, dma=True, tag=tag,
                       tpose=bool(kw.get("transpose")))

    def mm(out, lhsT, rhs, start, stop, reads, writes, tag=""):
        return P.stepw("pe",
                       (lambda eng, out=out, lhsT=lhsT, rhs=rhs, s1=start,
                        s2=stop: eng.matmul(out, lhsT, rhs, start=s1, stop=s2)),
                       reads=reads, writes=writes, tag=tag)

    def op(stream, fn, reads, writes, tag=""):
        return P.stepw(stream, fn, reads=reads, writes=writes, tag=tag)

    rrb = [0]
    def nextbank(lo=0, hi=6):
        rrb[0] += 1
        return lo + (rrb[0] % (hi - lo))

    op("dve", lambda e: e.memset(pa[:, :], 0.0), [], pbuf, tag="psum0")

    for src, dst, nm in ((d_wupl, d_wuplb, "wuplb"), (d_wupr, d_wuprb, "wuprb"),
                         (d_wv, d_wvb, "wvb"), (d_wo, d_wob, "wob"),
                         (d_wq, d_wqb, "wqb"), (d_wk, d_wkb, "wkb"),
                         (d_wskip, d_wskipb, "wskipb"),
                         (d_wdown, d_wdownb, "wdownb")):
        dma("pool", dst, src, [], [B(nm)], tag="cast_" + nm)

    c_ident = newv("c_ident", 128, 128, F32)
    c_ind8 = newv("c_ind8", 8, 1024, F32)
    c_mask = newv("c_mask", 128, 1024, F32)
    c_par2 = newv("c_par2", 2, 128, F32)
    c_band = newv("c_band", 128, 4 * 128, F32)
    c_band2 = newv("c_band2", 128, 4 * 128, F32)
    ones_r = newv("ones_r", 1, 128, F32)
    ones_rb = newv("ones_rb", 1, 128, BF16)
    ones_cb = newv("ones_cb", 128, 1, BF16)
    dma("sync", c_ident, d_cident, [], [B("c_ident")])
    dma("sync", c_ind8, d_cind8, [], [B("c_ind8")])
    dma("sync", c_mask, d_cmask, [], [B("c_mask")])
    dma("sync", c_par2, d_cpar2, [], [B("c_par2")])
    cbv = c_band.rearrange("p (m j) -> p m j", m=4)
    cb2v = c_band2.rearrange("p (m j) -> p m j", m=4)
    dma("sync", cbv, d_cband.rearrange("m p j -> p m j"), [], [B("c_band")])
    dma("sync", cb2v, d_cband2.rearrange("m p j -> p m j"), [], [B("c_band2")])
    op("dve", lambda e: e.memset(ones_r, 1.0), [], [B("ones_r")])
    op("dve", lambda e: e.memset(ones_rb, 1.0), [], [B("ones_rb")])
    op("dve", lambda e: e.memset(ones_cb, 1.0), [], [B("ones_cb")])
    eps6_c = newv("eps6_c", 128, 1, F32)
    eps5_c = newv("eps5_c", 128, 1, F32)
    op("dve", lambda e: e.memset(eps6_c, 1e-6), [], [B("eps6_c")])
    op("dve", lambda e: e.memset(eps5_c, 1e-5), [], [B("eps5_c")])

    bul_c = newv("bul_c", 128, 16, F32)
    dma("sync", bul_c, d_bupl.rearrange("(c p) -> p c", p=128), [], [B("bul_c")], allow_slow_non_contiguous=True)
    bq_c = newv("bq_c", 64, 8, F32)
    dma("sync", bq_c, d_bq.rearrange("(c p) -> p c", p=64), [], [B("bq_c")], allow_slow_non_contiguous=True)
    bk_c = newv("bk_c", 64, 8, F32)
    dma("sync", bk_c, d_bk.rearrange("(c p) -> p c", p=64), [], [B("bk_c")], allow_slow_non_contiguous=True)
    bk8_c = newv("bk8_c", 64, 8, F32)
    op("dve", lambda e: e.tensor_scalar_mul(bk8_c, bk_c, 0.125),
       [B("bk_c")], [B("bk8_c")])
    bi_c = newv("bi_c", 8, 1, F32)
    dma("sync", bi_c, d_bi, [], [B("bi_c")])
    bf_c = newv("bf_c", 8, 1, F32)
    dma("sync", bf_c, d_bf, [], [B("bf_c")])
    bi15 = newv("bi15", 8, 1, F32)
    op("dve", lambda e: e.tensor_scalar_mul(bi15, bi_c, 1.0 / 15.0),
       [B("bi_c")], [B("bi15")])
    bf15 = newv("bf15", 8, 1, F32)
    op("dve", lambda e: e.tensor_scalar_mul(bf15, bf_c, 1.0 / 15.0),
       [B("bf_c")], [B("bf15")])

    bupr_rb = newv("bupr_rb", 1, 512, BF16)
    dma("pool", bupr_rb, d_bupr[None, :], [], [B("bupr_rb")])
    bv_rb = newv("bv_rb", 1, 512, BF16)
    dma("pool", bv_rb, d_bv[None, :], [], [B("bv_rb")])
    bo_rb = newv("bo_rb", 1, 512, BF16)
    dma("pool", bo_rb, d_bo[None, :], [], [B("bo_rb")])
    bskip_rb = newv("bskip_rb", 1, 512, BF16)
    dma("pool", bskip_rb, d_bskip[None, :], [], [B("bskip_rb")])

    def bcast_from(name, dram_row, cols):
        rown = "row_" + name
        row_ap = newv(rown, 1, cols, F32, side="hi")
        dma("sync", row_ap, dram_row, [], [B(rown)], tag="ldr_" + name)
        t = newv(name, 128, cols, F32)
        nhalf = (cols + 511) // 512
        for hh in range(nhalf):
            w = min(512, cols - hh * 512)
            bk = nextbank()
            mm(bank(bk)[:, 0:w], ones_r, row_ap[0:1, hh * 512:hh * 512 + w],
               True, True, [B("ones_r"), B(rown)], [pbuf[bk]], tag="bc_" + name)
            op("act",
               (lambda e, dst=t[:, hh * 512:hh * 512 + w], src=bank(bk)[:, 0:w]:
                e.activation(dst, src, AF.Copy)),
               [pbuf[bk]], [B(name)], tag="bce_" + name)
        sfree(rown)
        return t

    lgb = bcast_from("lgb", d_lng[None, :], 1024)
    lbb = bcast_from("lbb", d_lnb[None, :], 1024)
    wcol4 = bcast_from("wcol4", d_convw[None, :], 4)
    convb_c = bcast_from("convb_c", d_convb[None, :], 1)

    bandA = newv("bandA", 128, 128, F32, side="hi")
    bandB = newv("bandB", 128, 128, F32, side="hi")
    band_bf = newv("band_bf", 128, 128, BF16)
    band2_bf = newv("band2_bf", 128, 128, BF16)
    for which, cb, outv, nm in ((0, cbv, band_bf, "band_bf"),
                                (1, cb2v, band2_bf, "band2_bf")):
        src_buf = B("c_band") if which == 0 else B("c_band2")
        op("dve", lambda e, cb=cb: e.tensor_scalar(
            bandA, cb[:, 0, :], wcol4[:, 0:1], None, ALU.mult),
            [src_buf, B("wcol4")], [B("bandA")])
        cur, oth = bandA, bandB
        curb, othb = B("bandA"), B("bandB")
        for m in range(1, 4):
            op("dve", lambda e, cb=cb, m=m, cur=cur, oth=oth:
               e.scalar_tensor_tensor(oth, cb[:, m, :], wcol4[:, m:m + 1], cur,
                                      ALU.mult, ALU.add),
               [src_buf, B("wcol4"), curb], [othb])
            cur, oth = oth, cur
            curb, othb = othb, curb
        op("dve", lambda e, cur=cur, outv=outv: e.tensor_copy(outv, cur),
           [curb], [B(nm)])
    for nm_ in ("bandA", "bandB", "c_band", "c_band2"):
        sfree(nm_)

    if stop_after == "consts":
        P.finalize()
        lf = {i: c for i, c in enumerate(P.lane_count) if c}
        return nc, P, dumps, lf, stack
    xt0 = newv("xt0", 128, 1024, F32)
    xt1 = newv("xt1", 128, 1024, F32)
    xts = [(xt0, B("xt0")), (xt1, B("xt1"))]
    st6 = newv("st6", 128, 24, F32)
    xn_s = newv("xn_s", 128, 1024, F32)

    def ln_stats(xap0, xap1, rd, eps_col, eps_buf):
        sb = B("st6")
        op("dve", lambda e: e.bn_stats(st6[:, 0:6], xap0), rd + [sb], [sb])
        op("dve", lambda e: e.bn_stats(st6[:, 6:12], xap1), rd + [sb], [sb])
        op("dve", lambda e: e.bn_aggr(st6[:, 12:14], st6[:, 0:6]), [sb], [sb])
        op("dve", lambda e: e.bn_aggr(st6[:, 14:16], st6[:, 6:12]), [sb], [sb])
        op("dve", lambda e: e.tensor_add(st6[:, 16:17], st6[:, 12:13],
                                         st6[:, 14:15]), [sb], [sb])
        op("dve", lambda e: e.tensor_scalar_mul(st6[:, 17:18], st6[:, 16:17],
                                                0.5), [sb], [sb])
        op("dve", lambda e: e.tensor_sub(st6[:, 18:19], st6[:, 12:13],
                                         st6[:, 14:15]), [sb], [sb])
        op("dve", lambda e: e.tensor_mul(st6[:, 19:20], st6[:, 18:19],
                                         st6[:, 18:19]), [sb], [sb])
        op("dve", lambda e: e.tensor_add(st6[:, 20:21], st6[:, 13:14],
                                         st6[:, 15:16]), [sb], [sb])
        op("dve", lambda e: e.tensor_scalar_mul(st6[:, 20:21], st6[:, 20:21],
                                                0.5), [sb], [sb])
        op("dve", lambda e: e.scalar_tensor_tensor(
            st6[:, 21:22], st6[:, 19:20], 0.25, st6[:, 20:21],
            ALU.mult, ALU.add), [sb], [sb])
        op("act", lambda e: e.activation(st6[:, 22:23], st6[:, 21:22], AF.Ln,
                                         bias=eps_col[:, 0:1]),
           [sb, eps_buf], [sb])
        op("act", lambda e: e.activation(st6[:, 23:24], st6[:, 22:23], AF.Exp,
                                         scale=-0.5), [sb], [sb])
    for tt in range(16):
        xv, xb_ = xts[tt % 2]
        dma("sync", xv, d_x[tt * 128:(tt + 1) * 128, :], [], [xb_], tag="ldx")
        ln_stats(xv[:, 0:512], xv[:, 512:1024], [xb_], eps6_c, B("eps6_c"))
        op("dve", lambda e, xv=xv: e.tensor_scalar(
            xn_s, xv, st6[:, 17:18], st6[:, 23:24], ALU.subtract, ALU.mult),
            [xb_, B("st6")], [B("xn_s")])
        op("dve", lambda e: e.tensor_mul(xn_s, xn_s, lgb),
           [B("xn_s"), B("lgb")], [B("xn_s")])
        op("dve", lambda e: e.tensor_add(xn_s, xn_s, lbb),
           [B("xn_s"), B("lbb")], [B("xn_s")])
        dma("pool", d_xn[tt * 128:(tt + 1) * 128, :], xn_s,
            [B("xn_s")], [B("d_xn")], tag="stxn")

    if "ln1dbg" in debug_dumps:
        t = dumpt("st6", (128, 16))
        dma("sync", t, st6, [B("st6")], [B("dump_st6")])
        t2 = dumpt("xns", (128, 1024))
        dma("sync", t2, xn_s, [B("xn_s")], [B("dump_xns")])
        t3 = dumpt("xtile", (128, 1024))
        dma("sync", t3, xts[15 % 2][0], [xts[15 % 2][1]], [B("dump_xtile")])
    if "xn" in debug_dumps:
        t = dumpt("xn", (S, D), BF16)
        dma("sync", t, d_xn, [B("d_xn")], [B("dump_xn")])
    sfree("lgb"); sfree("lbb")

    xn_fm = newv("xn_fm", 128, 8 * 2048, BF16, side="hi")
    xn_fmv = xn_fm.rearrange("p (c t) -> p c t", c=8)
    for dc in range(8):
        dma("sync", xn_fmv[:, dc, :], d_xn[:, dc * 128:(dc + 1) * 128],
            [B("d_xn")], [B("xn_fm")], tag="Txn", transpose=True)

    if stop_after == "xn":
        P.finalize()
        lf = {i: c for i, c in enumerate(P.lane_count) if c}
        return nc, P, dumps, lf, stack
    wuplT = newv("wuplT", 128, 8 * 2048, BF16, side="hi")
    wuplTv = wuplT.rearrange("p (c j) -> p c j", c=8)
    for dc in range(8):
        dma("act", wuplTv[:, dc, :], d_wuplb[:, dc * 128:(dc + 1) * 128],
            [B("wuplb")], [B("wuplT")], tag="Twupl", transpose=True)
    if "fmdbg" in debug_dumps:
        t = dumpt("xnfm0", (128, 2048), BF16)
        dma("sync", t, xn_fmv[:, 0, :], [B("xn_fm")], [B("dump_xnfm0")])
        t2 = dumpt("xnfm5", (128, 2048), BF16)
        dma("sync", t2, xn_fmv[:, 5, :], [B("xn_fm")], [B("dump_xnfm5")])
        t3 = dumpt("wuplT0", (128, 2048), BF16)
        dma("sync", t3, wuplTv[:, 0, :], [B("wuplT")], [B("dump_wuplT0")])
        t4 = dumpt("wuplT5", (128, 2048), BF16)
        dma("sync", t4, wuplTv[:, 5, :], [B("wuplT")], [B("dump_wuplT5")])
    xt_fm = newv("xt_fm", 128, 16 * 2048, BF16,
                 bufs_=[f"xt_fm{c}" for c in range(16)])
    xt_fmv = xt_fm.rearrange("p (c t) -> p c t", c=16)
    xtb = [B(f"xt_fm{c}") for c in range(16)]
    for jc in range(16):
        for tf in range(4):
            bk = nextbank()
            for dc in range(8):
                mm(bank(bk), wuplTv[:, dc, jc * 128:(jc + 1) * 128],
                   xn_fmv[:, dc, tf * 512:(tf + 1) * 512], dc == 0, dc == 7,
                   [B("wuplT"), B("xn_fm")], [pbuf[bk]], tag="xt_mm")
            op("act", lambda e, jc=jc, tf=tf, bk=bk: e.activation(
                xt_fmv[:, jc, tf * 512:(tf + 1) * 512], bank(bk), AF.Identity,
                bias=bul_c[:, jc:jc + 1]),
                [pbuf[bk], B("bul_c")], [xtb[jc]], tag="xt_ev")

    if stop_after == "xt":
        P.finalize()
        lf = {i: c for i, c in enumerate(P.lane_count) if c}
        return nc, P, dumps, lf, stack
    if "xtc" in debug_dumps:
        t = dumpt("xtc", (128, 2048), BF16)
        dma("sync", t, xt_fmv[:, 0, :], [xtb[0]], [B("dump_xtc")])
        t2 = dumpt("xtc5", (128, 2048), BF16)
        dma("sync", t2, xt_fmv[:, 5, :], [xtb[5]], [B("dump_xtc5")])
    wuprT = newv("wuprT", 128, 8 * 512, BF16, side="hi")
    wuprTv = wuprT.rearrange("p (c j) -> p c j", c=8)
    for dc in range(8):
        dma("act", wuprTv[:, dc, :], d_wuprb[:, dc * 128:(dc + 1) * 128],
            [B("wuprb")], [B("wuprT")], tag="Twupr", transpose=True)
    rt_t = newv("rt_t", 128, 2 * 512, BF16, bufs_=["rt_t0", "rt_t1"])
    rt_tv = rt_t.rearrange("p (c j) -> p c j", c=2)
    for tt in range(16):
        bk = nextbank()
        mm(bank(bk), ones_rb, bupr_rb, True, False,
           [B("ones_rb"), B("bupr_rb")], [pbuf[bk]], tag="rt_b")
        for dc in range(8):
            mm(bank(bk), xn_fmv[:, dc, tt * 128:(tt + 1) * 128], wuprTv[:, dc, :],
               False, dc == 7, [B("xn_fm"), B("wuprT")], [pbuf[bk]], tag="rt_mm")
        sl = rt_tv[:, tt % 2, :]
        op("act", lambda e, sl=sl, bk=bk: e.activation(sl, bank(bk), AF.Silu),
           [pbuf[bk]], [B(f"rt_t{tt % 2}")], tag="rt_ev")
        dma("sync", d_rt[tt * 128:(tt + 1) * 128, :], sl,
            [B(f"rt_t{tt % 2}")], [B("d_rt")], tag="st_rt")

    sfree("xn_fm"); sfree("wuplT"); sfree("wuprT")

    xc_fm = newv("xc_fm", 128, 16 * 2048, BF16,
                 bufs_=[f"xc_fm{c}" for c in range(16)])
    xc_fmv = xc_fm.rearrange("p (c t) -> p c t", c=16)
    xcb = [B(f"xc_fm{c}") for c in range(16)]
    for cc in range(16):
        for tf in range(4):
            bk = nextbank()
            mm(bank(bk), band_bf, xt_fmv[:, cc, tf * 512:(tf + 1) * 512],
               True, cc == 0, [B("band_bf"), xtb[cc]], [pbuf[bk]], tag="cv_mm")
            if cc > 0:
                mm(bank(bk), band2_bf, xt_fmv[:, cc - 1, tf * 512:(tf + 1) * 512],
                   False, True, [B("band2_bf"), xtb[cc - 1]], [pbuf[bk]],
                   tag="cv_mm2")
            op("act", lambda e, cc=cc, tf=tf, bk=bk: e.activation(
                xc_fmv[:, cc, tf * 512:(tf + 1) * 512], bank(bk), AF.Silu,
                bias=convb_c[:, 0:1]),
                [pbuf[bk], B("convb_c")], [xcb[cc]], tag="cv_ev")

    if "xcc" in debug_dumps:
        t = dumpt("xcc", (128, 2048), BF16)
        dma("sync", t, xc_fmv[:, 0, :], [xcb[0]], [B("dump_xcc")])
        t2 = dumpt("xcc5", (128, 2048), BF16)
        dma("sync", t2, xc_fmv[:, 5, :], [xcb[5]], [B("dump_xcc5")])
    wvT = newv("wvT", 128, 16 * 512, BF16, side="hi")
    wvTv = wvT.rearrange("p (c j) -> p c j", c=16)
    for pc in range(16):
        dma("act", wvTv[:, pc, :], d_wvb[:, pc * 128:(pc + 1) * 128],
            [B("wvb")], [B("wvT")], tag="Twv", transpose=True)
    v_s = newv("v_s", 128, 2 * 512, BF16, bufs_=["v_s0", "v_s1"])
    v_sv = v_s.rearrange("p (c j) -> p c j", c=2)
    for tt in range(16):
        bk = nextbank()
        mm(bank(bk), ones_rb, bv_rb, True, False,
           [B("ones_rb"), B("bv_rb")], [pbuf[bk]], tag="v_b")
        for pc in range(16):
            mm(bank(bk), xt_fmv[:, pc, tt * 128:(tt + 1) * 128], wvTv[:, pc, :],
               False, pc == 15, [xtb[pc], B("wvT")], [pbuf[bk]], tag="v_mm")
        sl = v_sv[:, tt % 2, :]
        op("act", lambda e, sl=sl, bk=bk: e.activation(sl, bank(bk), AF.Copy),
           [pbuf[bk]], [B(f"v_s{tt % 2}")], tag="v_ev")
        dma("sync", d_v[tt * 128:(tt + 1) * 128, :], sl,
            [B(f"v_s{tt % 2}")], [B("d_v")], tag="st_v")
    sfree("wvT"); sfree("v_s")

    woT = newv("woT", 128, 16 * 512, BF16, side="hi")
    woTv = woT.rearrange("p (c j) -> p c j", c=16)
    for pc in range(16):
        dma("act", woTv[:, pc, :], d_wob[:, pc * 128:(pc + 1) * 128],
            [B("wob")], [B("woT")], tag="Two", transpose=True)
    o_s = newv("o_s", 128, 2 * 512, BF16, bufs_=["o_s0", "o_s1"])
    o_sv = o_s.rearrange("p (c j) -> p c j", c=2)
    for tt in range(16):
        bk = nextbank()
        mm(bank(bk), ones_rb, bo_rb, True, False,
           [B("ones_rb"), B("bo_rb")], [pbuf[bk]], tag="o_b")
        for pc in range(16):
            mm(bank(bk), xt_fmv[:, pc, tt * 128:(tt + 1) * 128], woTv[:, pc, :],
               False, pc == 15, [xtb[pc], B("woT")], [pbuf[bk]], tag="o_mm")
        sl = o_sv[:, tt % 2, :]
        op("act", lambda e, sl=sl, bk=bk: e.activation(sl, bank(bk), AF.Sigmoid),
           [pbuf[bk]], [B(f"o_s{tt % 2}")], tag="o_ev")
        dma("sync", d_o[tt * 128:(tt + 1) * 128, :], sl,
            [B(f"o_s{tt % 2}")], [B("d_o")], tag="st_o")
    sfree("woT"); sfree("xt_fm")

    if stop_after == "conv":
        P.finalize()
        lf = {i: c for i, c in enumerate(P.lane_count) if c}
        return nc, P, dumps, lf, stack
    q8 = newv("q8", 64, 8 * 2048, BF16)
    q8v = q8.rearrange("p (h t) -> p h t", h=8)
    k8 = newv("k8", 64, 8 * 2048, BF16)
    k8v = k8.rearrange("p (h t) -> p h t", h=8)
    wiT = newv("wiT", 128, 16 * 8, BF16, side="hi")
    wiTv = wiT.rearrange("p (c j) -> p c j", c=16)
    wfT = newv("wfT", 128, 16 * 8, BF16, side="hi")
    wfTv = wfT.rearrange("p (c j) -> p c j", c=16)
    wi_sb = newv("wi_sb", 8, 2048, F32, side="hi")
    wf_sb = newv("wf_sb", 8, 2048, F32, side="hi")
    dma("sync", wi_sb, d_wi, [], [B("wi_sb")], tag="ld_wi")
    dma("sync", wf_sb, d_wf, [], [B("wf_sb")], tag="ld_wf")
    for (wsb, wsbb, wTv_, wtb_) in ((wi_sb, B("wi_sb"), wiTv, B("wiT")),
                                    (wf_sb, B("wf_sb"), wfTv, B("wfT"))):
        for pc in range(16):
            bk = nextbank()
            P.stepw("pe", lambda e, wsb=wsb, pc=pc, bk=bk: e.transpose(
                bank(bk)[:, 0:8], wsb[0:8, pc * 128:(pc + 1) * 128],
                c_ident[0:8, 0:8]),
                reads=[wsbb, B("c_ident")], writes=[pbuf[bk]], tag="Twi_pe")
            op("act", lambda e, wTv_=wTv_, pc=pc, bk=bk: e.activation(
                wTv_[:, pc, :], bank(bk)[:, 0:8], AF.Copy),
                [pbuf[bk]], [wtb_], tag="Twi_ev")
    sfree("wi_sb"); sfree("wf_sb")
    a_fm = newv("a_fm", 8, 2048, F32, side="hi")
    f_fm = newv("f_fm", 8, 2048, F32, side="hi")
    for (wTv_, wb_, gfm, gb_, bc, bcb) in (
            (wiTv, B("wiT"), a_fm, B("a_fm"), bi15, B("bi15")),
            (wfTv, B("wfT"), f_fm, B("f_fm"), bf15, B("bf15"))):
        for tf in range(4):
            bk = nextbank()
            for pc in range(16):
                mm(bank(bk)[0:8, :], wTv_[:, pc, :],
                   xc_fmv[:, pc, tf * 512:(tf + 1) * 512], pc == 0, pc == 15,
                   [wb_, xcb[pc]], [pbuf[bk]], tag="g_mm")
            op("act", lambda e, gfm=gfm, tf=tf, bk=bk, bc=bc: e.activation(
                gfm[0:8, tf * 512:(tf + 1) * 512], bank(bk)[0:8, :], AF.Tanh,
                bias=bc[0:8, 0:1], scale=1.0 / 15.0),
                [pbuf[bk], bcb], [gb_], tag="g_ev")
        op("dve", lambda e, gfm=gfm: e.tensor_scalar_mul(gfm, gfm, 15.0),
           [gb_], [gb_], tag="g_x15")
    sfree("wiT"); sfree("wfT")

    if "afm" in debug_dumps:
        t = dumpt("afm", (8, 2048))
        dma("sync", t, a_fm, [B("a_fm")], [B("dump_afm")])
        t2 = dumpt("ffm", (8, 2048))
        dma("sync", t2, f_fm, [B("f_fm")], [B("dump_ffm")])

    if stop_after == "gates":
        P.finalize()
        lf = {i: c for i, c in enumerate(P.lane_count) if c}
        return nc, P, dumps, lf, stack

    if "afm" in debug_dumps:
        t = dumpt("afm", (8, 2048))
        dma("sync", t, a_fm, [B("a_fm")], [B("dump_afm")])
        t2 = dumpt("ffm", (8, 2048))
        dma("sync", t2, f_fm, [B("f_fm")], [B("dump_ffm")])

    if stop_after == "gates":
        P.finalize()
        lf = {i: c for i, c in enumerate(P.lane_count) if c}
        return nc, P, dumps, lf, stack
    for (wsrc, wbuf, dstv, nm, bias_c, bias_buf, scale) in (
            (d_wqb, B("wqb"), q8v, "q8", bq_c, B("bq_c"), 1.0),
            (d_wkb, B("wkb"), k8v, "k8", bk8_c, B("bk8_c"), 0.125)):
        wT = newv("wT_" + nm, 128, 16 * 512, BF16)
        wTv = wT.rearrange("p (c j) -> p c j", c=16)
        for pc in range(16):
            dma("act", wTv[:, pc, :], wsrc[:, pc * 128:(pc + 1) * 128],
                [wbuf], [B("wT_" + nm)], tag="T" + nm, transpose=True)
        for h in range(8):
            for tf in range(4):
                bk = nextbank()
                for pc in range(16):
                    mm(bank(bk)[0:64, :], wTv[:, pc, h * 64:(h + 1) * 64],
                       xc_fmv[:, pc, tf * 512:(tf + 1) * 512], pc == 0, pc == 15,
                       [B("wT_" + nm), xcb[pc]], [pbuf[bk]], tag=nm + "_mm")
                op("act", lambda e, h=h, tf=tf, bk=bk, dstv=dstv,
                   bias_c=bias_c, scale=scale: e.activation(
                       dstv[:, h, tf * 512:(tf + 1) * 512], bank(bk)[0:64, :],
                       AF.Identity, bias=bias_c[0:64, h:h + 1], scale=scale),
                   [pbuf[bk], bias_buf], [B(nm)], tag=nm + "_ev")
        sfree("wT_" + nm)

    for h in range(8):
        dma("sync", d_k[h * 64:(h + 1) * 64, :], k8v[:, h, :],
            [B("k8")], [B("d_k")], tag="st_k")

    wskT = newv("wskT", 128, 16 * 512, BF16)
    wskTv = wskT.rearrange("p (c j) -> p c j", c=16)
    for pc in range(16):
        dma("act", wskTv[:, pc, :], d_wskipb[:, pc * 128:(pc + 1) * 128],
            [B("wskipb")], [B("wskT")], tag="Twsk", transpose=True)
    sk_s = newv("sk_s", 128, 2 * 512, BF16, bufs_=["sk_s0", "sk_s1"])
    sk_sv = sk_s.rearrange("p (c j) -> p c j", c=2)
    for tt in range(16):
        bk = nextbank()
        mm(bank(bk), ones_rb, bskip_rb, True, False,
           [B("ones_rb"), B("bskip_rb")], [pbuf[bk]], tag="sk_b")
        for pc in range(16):
            mm(bank(bk), xc_fmv[:, pc, tt * 128:(tt + 1) * 128], wskTv[:, pc, :],
               False, pc == 15, [xcb[pc], B("wskT")], [pbuf[bk]], tag="sk_mm")
        sl = sk_sv[:, tt % 2, :]
        op("act", lambda e, sl=sl, bk=bk: e.activation(sl, bank(bk), AF.Copy),
           [pbuf[bk]], [B(f"sk_s{tt % 2}")], tag="sk_ev")
        dma("sync", d_skip[tt * 128:(tt + 1) * 128, :], sl,
            [B(f"sk_s{tt % 2}")], [B("d_skip")], tag="st_sk")
    sfree("wskT"); sfree("sk_s")

    sfree("xc_fm")
    m_fm = newv("m_fm", 8, 2048, F32, side="hi")
    op("dve", lambda e: e.tensor_tensor_scan(m_fm, f_fm, a_fm, 0.0,
                                             ALU.add, ALU.max),
       [B("a_fm"), B("f_fm")], [B("m_fm")], tag="m_scan")
    F_fm = newv("F_fm", 8, 2048, F32, side="hi")
    for cc in range(16):
        sl = slice(cc * 128, (cc + 1) * 128)
        op("dve", lambda e, sl=sl: e.tensor_tensor_scan(
            F_fm[:, sl], f_fm[:, sl], f_fm[:, sl], 0.0, ALU.add, ALU.bypass),
            [B("f_fm"), B("F_fm")], [B("F_fm")], tag="F_scan")
    u_fm = newv("u_fm", 8, 2048, F32, side="hi")
    op("dve", lambda e: e.tensor_sub(u_fm, a_fm, F_fm),
       [B("a_fm"), B("F_fm")], [B("u_fm")])
    negw_fm = newv("negw_fm", 8, 2048, F32, side="hi")
    op("dve", lambda e: e.tensor_sub(negw_fm, F_fm, m_fm),
       [B("F_fm"), B("m_fm")], [B("negw_fm")])

    m_3 = m_fm.rearrange("p (c t) -> p c t", t=128)
    F_3 = F_fm.rearrange("p (c t) -> p c t", t=128)
    nw_3 = negw_fm.rearrange("p (c t) -> p c t", t=128)
    gsc = newv("gsc", 8, 80, F32, side="hi")
    mp = gsc[0:8, 0:16]; g_t1 = gsc[0:8, 16:32]; g_t2 = gsc[0:8, 32:48]
    phi_fm = gsc[0:8, 48:64]; negwl_fm = gsc[0:8, 64:80]
    op("dve", lambda e: e.memset(mp, 0.0), [B("gsc")], [B("gsc")])
    op("dve", lambda e: e.tensor_copy(mp[:, 1:16], m_3[0:8, 0:15, 127]),
       [B("m_fm"), B("gsc")], [B("gsc")])
    op("dve", lambda e: e.tensor_add(g_t1, mp, F_3[0:8, :, 127]),
       [B("gsc"), B("F_fm")], [B("gsc")])
    op("dve", lambda e: e.tensor_sub(g_t2, g_t1, m_3[0:8, :, 127]),
       [B("gsc"), B("m_fm")], [B("gsc")])
    op("act", lambda e: e.activation(phi_fm, g_t2, AF.Exp),
       [B("gsc")], [B("gsc")], tag="phi")
    op("dve", lambda e: e.tensor_copy(negwl_fm, nw_3[0:8, :, 127]),
       [B("negw_fm"), B("gsc")], [B("gsc")])

    dma("sync", d_negw3.rearrange("c h t -> h c t"),
        negw_fm.rearrange("h (c t) -> h c t", c=16),
        [B("negw_fm")], [B("d_negw3")], tag="bn_negw")
    dma("sync", d_u3.rearrange("c h t -> h c t"),
        u_fm.rearrange("h (c t) -> h c t", c=16),
        [B("u_fm")], [B("d_u3")], tag="bn_u")
    dma("sync", d_mprev2.rearrange("c h -> h c"), mp,
        [B("gsc")], [B("d_mprev2")], tag="bn_mp", allow_slow_non_contiguous=True)
    dma("sync", d_negwl2.rearrange("c h -> h c"), negwl_fm,
        [B("gsc")], [B("d_negwl2")], tag="bn_nwl", allow_slow_non_contiguous=True)
    dma("sync", d_phi2.rearrange("c h -> h c"), phi_fm,
        [B("gsc")], [B("d_phi2")], tag="bn_phi", allow_slow_non_contiguous=True)

    sfree("a_fm"); sfree("f_fm"); sfree("m_fm"); sfree("F_fm")
    sfree("negw_fm")
    wcols = newv("wcols", 128, 128, F32, side="hi")
    ucols = newv("ucols", 128, 128, F32, side="hi")
    l2scr = newv("l2scr", 128, 128, F32, side="hi")
    for (dsrc, dsb, outap, outb, nm_) in (
            (d_negw3, B("d_negw3"), wcols, B("wcols"), "wc"),
            (d_u3, B("d_u3"), ucols, B("ucols"), "uc")):
        dma("sync", l2scr, dsrc.rearrange("c h t -> (c h) t"),
            [dsb, B("l2scr")], [B("l2scr")], tag="ld2_" + nm_)
        bk = nextbank()
        P.stepw("pe", lambda e, bk=bk: e.transpose(
            bank(bk)[:, 0:128], l2scr, c_ident),
            reads=[B("l2scr"), B("c_ident")], writes=[pbuf[bk]],
            tag="T2_" + nm_)
        op("act", lambda e, outap=outap, bk=bk: e.activation(
            outap, bank(bk)[:, 0:128], AF.Copy),
            [pbuf[bk]], [outb], tag="T2e_" + nm_)
    sfree("l2scr")
    mprev_r = newv("mprev_r", 1, 128, F32, side="hi")
    dma("sync", mprev_r.rearrange("o (c h) -> o c h", c=16),
        d_mprev2[None, :, :], [B("d_mprev2")], [B("mprev_r")], tag="ld_mpr")
    negwl_r = newv("negwl_r", 1, 128, F32, side="hi")
    dma("sync", negwl_r.rearrange("o (c h) -> o c h", c=16),
        d_negwl2[None, :, :], [B("d_negwl2")], [B("negwl_r")], tag="ld_nwlr")
    phi_r = newv("phi_r", 1, 128, F32, side="hi")
    dma("sync", phi_r.rearrange("o (c h) -> o c h", c=16),
        d_phi2[None, :, :], [B("d_phi2")], [B("phi_r")], tag="ld_phir")

    egcols = newv("egcols", 128, 128, F32)
    eucols = newv("eucols", 128, 128, F32)
    phicols = newv("phicols", 64, 128, F32)
    gscr = newv("gscr", 128, 128, F32, side="hi")
    for (rowap, rowbuf, colsap, colsbuf, outap, outbuf, nm) in (
            (mprev_r, B("mprev_r"), wcols, B("wcols"), egcols, B("egcols"), "eg"),
            (negwl_r, B("negwl_r"), ucols, B("ucols"), eucols, B("eucols"), "eu")):
        bk = nextbank()
        mm(bank(bk)[:, 0:128], ones_r, rowap, True, True,
           [B("ones_r"), rowbuf], [pbuf[bk]], tag="bc_" + nm)
        op("dve", lambda e, colsap=colsap, bk=bk: e.tensor_tensor(
            gscr, bank(bk)[:, 0:128], colsap, ALU.add),
            [pbuf[bk], colsbuf, B("gscr")], [B("gscr")], tag="add_" + nm)
        op("act", lambda e, outap=outap: e.activation(outap, gscr, AF.Exp),
           [B("gscr")], [outbuf], tag="exp_" + nm)
    bk = nextbank()
    mm(bank(bk)[0:64, 0:128], ones_r[0:1, 0:64], phi_r, True, True,
       [B("ones_r"), B("phi_r")], [pbuf[bk]], tag="phicols_mm")
    op("dve", lambda e, bk=bk: e.tensor_copy(phicols, bank(bk)[0:64, 0:128]),
       [pbuf[bk]], [B("phicols")], tag="phicols_ev")

    if "gates" in debug_dumps:
        for nm2, ap2, bf2 in (("m_fm", m_fm, B("m_fm")), ("u_fm", u_fm, B("u_fm")),
                              ("negw_fm", negw_fm, B("negw_fm"))):
            t = dumpt(nm2, (8, 2048))
            dma("sync", t, ap2, [bf2], [B("dump_" + nm2)])
        t = dumpt("egcols", (128, 128))
        dma("sync", t, egcols, [B("egcols")], [B("dump_egcols")])
        t = dumpt("eucols", (128, 128))
        dma("sync", t, eucols, [B("eucols")], [B("dump_eucols")])
        t = dumpt("phicols", (128, 64))
        dma("sync", t, phicols, [B("phicols")], [B("dump_phicols")])

    if stop_after == "gatesprep":
        P.finalize()
        lf = {i: c for i, c in enumerate(P.lane_count) if c}
        return nc, P, dumps, lf, stack
    Wt = newv("Wt", 128, 16 * 1024, BF16, bufs_=[f"Wt{c}" for c in range(16)])
    Wtv = Wt.rearrange("p (c t) -> p c t", c=16)
    Wtb = [B(f"Wt{c}") for c in range(16)]
    nrow0 = newv("nrow0", 1, 1024, F32, side="hi")
    nrow1 = newv("nrow1", 1, 1024, F32, side="hi")
    nrows = [(nrow0, B("nrow0")), (nrow1, B("nrow1"))]
    pD = pa[:, 0:1024]
    for cc in range(16):
        nr, nrb = nrows[cc % 2]
        dma("sync", nr.rearrange("o (h t) -> o h t", h=8),
            d_negw3[None, cc, :, :], [B("d_negw3")], [nrb], tag="ld_nrow")
        for half in range(2):
            hs = slice(half * 512, (half + 1) * 512)
            mm(pD[:, hs], u_fm[:, cc * 128:(cc + 1) * 128], c_ind8[:, hs],
               True, False, [B("u_fm"), B("c_ind8")], [pbuf[half]], tag="D_u")
            mm(pD[:, hs], ones_r, nr[0:1, hs], False, False,
               [B("ones_r"), nrb], [pbuf[half]], tag="D_w")
            mm(pD[:, hs], c_ident, c_mask[:, hs], False, True,
               [B("c_ident"), B("c_mask")], [pbuf[half]], tag="D_m")
        op("act", lambda e, cc=cc: e.activation(Wtv[:, cc, :], pD, AF.Exp),
           [pbuf[0], pbuf[1]], [Wtb[cc]], tag="W_exp")
    sfree("u_fm")
    sfree("c_ind8"); sfree("c_mask"); sfree("c_par2")
    sfree("nrow0"); sfree("nrow1"); sfree("wcols"); sfree("ucols")
    sfree("mprev_r"); sfree("negwl_r"); sfree("phi_r"); sfree("gscr")
    sfree("gsc")

    if "Wt" in debug_dumps:
        t = dumpt("Wt", (128, 16 * 1024), BF16)
        dma("sync", t, Wt, Wtb, [B("dump_Wt")])

    k_tm = newv("k_tm", 128, 16 * 512, BF16)
    k_tmv = k_tm.rearrange("p (c j) -> p c j", c=16)
    for cc in range(16):
        dma("act", k_tmv[:, cc, :], d_k[:, cc * 128:(cc + 1) * 128],
            [B("d_k")], [B("k_tm")], tag="Tktm", transpose=True)

    if stop_after == "Wt":
        P.finalize()
        lf = {i: c for i, c in enumerate(P.lane_count) if c}
        return nc, P, dumps, lf, stack
    C8 = newv("C8", 64, 8 * 65, F32, bufs_=[f"C{h}" for h in range(8)])
    C8v = C8.rearrange("p (h e) -> p h e", h=8)
    Cb8 = newv("Cb8", 64, 8 * 65, BF16)
    Cb8v = Cb8.rearrange("p (h e) -> p h e", h=8)
    Cbuf = [B(f"C{h}") for h in range(8)]
    Cbbuf = B("Cb8")
    euk = newv("euk", 128, 8 * 64, BF16, bufs_=[f"euk{h}" for h in range(8)])
    eukv = euk.rearrange("p (h e) -> p h e", h=8)
    eukbuf = [B(f"euk{h}") for h in range(8)]
    h_tmA = newv("h_tmA", 128, 8 * 512, F32, bufs_=[f"h{c}" for c in range(8)])
    h_tmB = newv("h_tmB", 128, 8 * 512, F32,
                 bufs_=[f"h{c}" for c in range(8, 16)])
    h_tA = h_tmA.rearrange("p (c j) -> p c j", c=8)
    h_tB = h_tmB.rearrange("p (c j) -> p c j", c=8)
    def h_tmv_f(cc):
        return h_tA[:, cc, :] if cc < 8 else h_tB[:, cc - 8, :]
    htb = [B(f"h{c}") for c in range(16)]
    o_t0 = newv("o_t0", 128, 512, BF16)
    o_t1 = newv("o_t1", 128, 512, BF16)
    otiles = [(o_t0, B("o_t0")), (o_t1, B("o_t1"))]
    sw_s = newv("sw_s", 128, 2 * 512, BF16, bufs_=["sw0", "sw1"])
    sw_sv = sw_s.rearrange("p (g j) -> p g j", g=2)
    swbuf = [B("sw0"), B("sw1")]
    scrI = newv("scrI", 128, 2 * 67, F32, bufs_=["scrI0", "scrI1"])
    scrIv = scrI.rearrange("p (r j) -> p r j", r=2)
    scrIbuf = [B("scrI0"), B("scrI1")]
    scrN = newv("scrN", 128, 2 * 64, F32, bufs_=["scrN0", "scrN1"])
    scrNv = scrN.rearrange("p (r j) -> p r j", r=2)
    scrNbuf = [B("scrN0"), B("scrN1")]
    scrD = newv("scrD", 128, 2 * 3, F32, bufs_=["scrD0", "scrD1"])
    scrDv = scrD.rearrange("p (r j) -> p r j", r=2)
    scrDbuf = [B("scrD0"), B("scrD1")]
    statc = newv("statc", 128, 36, F32)
    rstdc = statc[:, 16:24]
    bnsc = statc[:, 24:36]
    statc_var = statc[:, 0:16].rearrange("p (h two) -> p h two", two=2)

    for h in range(8):
        op("dve", lambda e, h=h: e.memset(C8v[:, h, 0:64], 0.0),
           [Cbuf[h]], [Cbuf[h]])
        op("dve", lambda e, h=h: e.memset(C8v[:, h, 64:65], 1.0),
           [Cbuf[h]], [Cbuf[h]])

    v_t0 = newv("v_t0", 128, 512, BF16)
    v_t1 = newv("v_t1", 128, 512, BF16)
    vtiles = [(v_t0, B("v_t0")), (v_t1, B("v_t1"))]
    for cc in range(16):
        ov, ovb = otiles[cc % 2]
        dma("sync", ov, d_o[cc * 128:(cc + 1) * 128, :], [B("d_o")], [ovb],
            tag="ld_o")
        vv, vvb = vtiles[cc % 2]
        dma("sync", vv, d_v[cc * 128:(cc + 1) * 128, :], [B("d_v")], [vvb],
            tag="ld_v")
        op("dve", lambda e: e.tensor_copy(Cb8, C8),
           Cbuf, [Cbbuf], tag="cast_C")
        for h in range(8):
            op("dve", lambda e, h=h, cc=cc: e.tensor_scalar(
                eukv[:, h, :], k_tmv[:, cc, h * 64:(h + 1) * 64],
                eucols[:, cc * 8 + h:cc * 8 + h + 1], None, ALU.mult),
                [B("k_tm"), B("eucols")], [eukbuf[h]], tag="euk")
        for g in range(2):
            pSb = 2 + (g + cc * 2) % 2
            for hh in range(4):
                h = g * 4 + hh
                mm(bank(pSb)[:, hh * 128:(hh + 1) * 128],
                   k8v[:, h, cc * 128:(cc + 1) * 128],
                   q8v[:, h, cc * 128:(cc + 1) * 128],
                   True, True, [B("k8"), B("q8")], [pbuf[pSb]], tag="score")
            op("dve", lambda e, g=g, cc=cc, pSb=pSb: e.tensor_tensor(
                sw_sv[:, g, :], bank(pSb), Wtv[:, cc, g * 512:(g + 1) * 512],
                ALU.mult),
                [pbuf[pSb], Wtb[cc]], [swbuf[g]], tag="swmul")
            for hh in range(4):
                h = g * 4 + hh
                r = h % 2
                pW = 4 + (h + cc * 8) % 3
                colq = q8v[:, h, cc * 128:(cc + 1) * 128]
                mm(bank(pW)[:, 0:64], sw_sv[:, g, hh * 128:(hh + 1) * 128],
                   vv[:, h * 64:(h + 1) * 64], True, True,
                   [swbuf[g], vvb], [pbuf[pW]], tag="num")
                mm(bank(pW)[:, 64:128], colq, Cb8v[:, h, 0:64], True, True,
                   [B("q8"), Cbbuf], [pbuf[pW]], tag="inter")
                mm(bank(pW)[:, 128:129], sw_sv[:, g, hh * 128:(hh + 1) * 128],
                   ones_cb, True, True, [swbuf[g], B("ones_cb")], [pbuf[pW]],
                   tag="rowsum")
                mm(bank(pW)[:, 130:131], colq, Cb8v[:, h, 64:65], True, True,
                   [B("q8"), Cbbuf], [pbuf[pW]], tag="nq")
                egc = egcols[:, cc * 8 + h:cc * 8 + h + 1]
                op("act", lambda e, r=r, pW=pW: e.activation(
                    scrIv[:, r, :], bank(pW)[:, 64:131], AF.Copy),
                    [pbuf[pW]], [scrIbuf[r]], tag="evI")
                op("dve", lambda e, r=r, pW=pW, egc=egc: e.scalar_tensor_tensor(
                    scrNv[:, r, :], scrIv[:, r, 0:64], egc, bank(pW)[:, 0:64],
                    ALU.mult, ALU.add),
                    [scrIbuf[r], B("egcols"), pbuf[pW]], [scrNbuf[r]],
                    tag="numc")
                op("dve", lambda e, r=r, egc=egc: e.scalar_tensor_tensor(
                    scrDv[:, r, 0:1], scrIv[:, r, 66:67], egc,
                    scrIv[:, r, 64:65], ALU.mult, ALU.add),
                    [scrIbuf[r], B("egcols")], [scrDbuf[r]], tag="denc")
                op("dve", lambda e, r=r: e.tensor_scalar_max(
                    scrDv[:, r, 1:2], scrDv[:, r, 0:1], 1.0),
                    [scrDbuf[r]], [scrDbuf[r]], tag="denm")
                op("dve", lambda e, r=r: e.reciprocal(
                    scrDv[:, r, 2:3], scrDv[:, r, 1:2]),
                    [scrDbuf[r]], [scrDbuf[r]], tag="denr")
                op("dve", lambda e, r=r, cc=cc, h=h, ov=ov:
                   e.scalar_tensor_tensor(
                       h_tmv_f(cc)[:, h * 64:(h + 1) * 64], scrNv[:, r, :],
                       scrDv[:, r, 2:3], ov[:, h * 64:(h + 1) * 64],
                       ALU.mult, ALU.mult),
                   [scrNbuf[r], scrDbuf[r], ovb], [htb[cc]], tag="hwr")
        for h in range(8):
            pC = 7
            op("dve", lambda e, h=h, cc=cc: e.tensor_scalar(
                C8v[:, h, :], C8v[:, h, :],
                phicols[:, cc * 8 + h:cc * 8 + h + 1], None, ALU.mult),
                [B("phicols"), Cbuf[h]], [Cbuf[h]], tag="Cphi")
            mm(bank(pC)[0:64, 0:64], eukv[:, h, :],
               vv[:, h * 64:(h + 1) * 64], True, True,
               [eukbuf[h], vvb], [pbuf[pC]], tag="Cmm")
            mm(bank(pC)[0:64, 64:65], eukv[:, h, :], ones_cb, True, True,
               [eukbuf[h], B("ones_cb")], [pbuf[pC]], tag="Cn")
            op("dve", lambda e, h=h, pC=pC: e.tensor_add(
                C8v[:, h, :], bank(pC)[0:64, 0:65], C8v[:, h, :]),
                [pbuf[pC], Cbuf[h]], [Cbuf[h]], tag="Cadd")

    if "hs" in debug_dumps:
        t = dumpt("hs", (128, 16 * 512))
        dma("sync", t[:, 0:8 * 512], h_tmA, htb[:8], [B("dump_hs")])
        dma("sync", t[:, 8 * 512:], h_tmB, htb[8:], [B("dump_hs")])

    if stop_after == "scan":
        P.finalize()
        lf = {i: c for i, c in enumerate(P.lane_count) if c}
        return nc, P, dumps, lf, stack
    mhgb = bcast_from("mhgb", d_mhg.rearrange("h d -> (h d)")[None, :], 512)
    mhbb = bcast_from("mhbb", d_mhb.rearrange("h d -> (h d)")[None, :], 512)
    rt_l0 = newv("rt_l0", 128, 512, BF16)
    rt_l1 = newv("rt_l1", 128, 512, BF16)
    rtl = [(rt_l0, B("rt_l0")), (rt_l1, B("rt_l1"))]
    sk_l0 = newv("sk_l0", 128, 512, BF16)
    sk_l1 = newv("sk_l1", 128, 512, BF16)
    skl = [(sk_l0, B("sk_l0")), (sk_l1, B("sk_l1"))]
    out_s0 = newv("out_s0", 128, 512, BF16)
    out_s1 = newv("out_s1", 128, 512, BF16)
    outs_ = [(out_s0, B("out_s0")), (out_s1, B("out_s1"))]
    for cc in range(16):
        for h in range(8):
            op("dve", lambda e, cc=cc, h=h: e.bn_stats(
                bnsc[:, 0:6], h_tmv_f(cc)[:, h * 64:(h + 1) * 64]),
                [htb[cc], B("statc")], [B("statc")], tag="bn1")
            op("dve", lambda e, h=h: e.bn_aggr(
                statc[:, 2 * h:2 * h + 2], bnsc[:, 0:6]),
                [B("statc")], [B("statc")], tag="bn2")
        op("act", lambda e: e.activation(rstdc, statc_var[:, :, 1], AF.Ln,
                                         bias=eps6_c[:, 0:1]),
           [B("statc")], [B("statc")], tag="lnv")
        op("act", lambda e: e.activation(rstdc, rstdc, AF.Exp, scale=-0.5),
           [B("statc")], [B("statc")], tag="lnr")
        for h in range(8):
            op("dve", lambda e, cc=cc, h=h: e.tensor_scalar(
                h_tmv_f(cc)[:, h * 64:(h + 1) * 64],
                h_tmv_f(cc)[:, h * 64:(h + 1) * 64],
                statc[:, 2 * h:2 * h + 1], rstdc[:, h:h + 1],
                ALU.subtract, ALU.mult),
                [B("statc"), htb[cc]], [htb[cc]], tag="hnorm")
        op("pool", lambda e, cc=cc: e.tensor_mul(h_tmv_f(cc),
                                                 h_tmv_f(cc), mhgb),
           [B("mhgb"), htb[cc]], [htb[cc]], tag="hg")
        op("pool", lambda e, cc=cc: e.tensor_add(h_tmv_f(cc),
                                                 h_tmv_f(cc), mhbb),
           [B("mhbb"), htb[cc]], [htb[cc]], tag="hb")
        rv, rvb = rtl[cc % 2]
        dma("sync", rv, d_rt[cc * 128:(cc + 1) * 128, :], [B("d_rt")], [rvb],
            tag="ld_rt")
        sv_, svb = skl[cc % 2]
        dma("sync", sv_, d_skip[cc * 128:(cc + 1) * 128, :], [B("d_skip")],
            [svb], tag="ld_sk")
        op("pool", lambda e, cc=cc, sv_=sv_: e.tensor_add(
            h_tmv_f(cc), h_tmv_f(cc), sv_),
            [svb, htb[cc]], [htb[cc]], tag="hskip")
        osl, oslb = outs_[cc % 2]
        op("dve", lambda e, cc=cc, rv=rv, osl=osl: e.tensor_tensor(
            osl, h_tmv_f(cc), rv, ALU.mult),
            [htb[cc], rvb], [oslb], tag="hout")
        dma("sync", d_outp[cc * 128:(cc + 1) * 128, :], osl, [oslb],
            [B("d_outp")], tag="st_out")

    sfree("Wt"); sfree("k_tm"); sfree("k8"); sfree("q8")
    sfree("h_tmA"); sfree("h_tmB"); sfree("euk")
    sfree("mhgb"); sfree("mhbb")
    for nm_ in ("o_t0", "o_t1", "sw_s", "scrI", "scrN", "scrD", "statc",
                "egcols", "eucols", "phicols", "C8", "Cb8",
                "rt_l0", "rt_l1", "sk_l0", "sk_l1", "out_s0", "out_s1", "v_t0", "v_t1",
                "rt_t", "o_s",
                "band_bf", "band2_bf", "bupr_rb", "bv_rb", "bo_rb",
                "bskip_rb", "bul_c", "bq_c", "bk_c", "bk8_c", "bi_c",
                "bf_c", "bi15", "bf15", "xn_s"):
        sfree(nm_)

    bdb = bcast_from("bdb", d_bdown[None, :], 1024)
    out_fm = newv("out_fm", 128, 4 * 2048, BF16, side="hi")
    out_fmv = out_fm.rearrange("p (c t) -> p c t", c=4)
    for ic in range(4):
        dma("act", out_fmv[:, ic, :], d_outp[:, ic * 128:(ic + 1) * 128],
            [B("d_outp")], [B("out_fm")], tag="Tout", transpose=True)
    wdT = newv("wdT", 128, 4 * 1024, BF16, side="hi")
    wdTv = wdT.rearrange("p (c j) -> p c j", c=4)
    for ic in range(4):
        dma("act", wdTv[:, ic, :], d_wdownb[:, ic * 128:(ic + 1) * 128],
            [B("wdownb")], [B("wdT")], tag="Twd", transpose=True)

    y_t = newv("y_t", 128, 1024, F32)
    x1_s = newv("x1_s", 128, 1024, F32)
    for tt in range(16):
        xv, xb_ = xts[tt % 2]
        dma("sync", xv, d_x[tt * 128:(tt + 1) * 128, :], [xb_], [xb_],
            tag="ldx2")
        for dh in range(2):
            bk = nextbank()
            for ic in range(4):
                mm(bank(bk), out_fmv[:, ic, tt * 128:(tt + 1) * 128],
                   wdTv[:, ic, dh * 512:(dh + 1) * 512], ic == 0, ic == 3,
                   [B("out_fm"), B("wdT")], [pbuf[bk]], tag="y_mm")
            sl = slice(dh * 512, (dh + 1) * 512)
            op("dve", lambda e, sl=sl, bk=bk, xv=xv: e.scalar_tensor_tensor(
                y_t[:, sl], xv[:, sl], 2.0, bank(bk), ALU.mult, ALU.add),
                [xb_, pbuf[bk], B("y_t")], [B("y_t")], tag="y_ev")
        op("pool", lambda e: e.tensor_add(y_t, y_t, bdb),
           [B("y_t"), B("bdb")], [B("y_t")], tag="y_bd")
        ln_stats(y_t[:, 0:512], y_t[:, 512:1024], [B("y_t")], eps5_c,
                 B("eps5_c"))
        op("dve", lambda e: e.tensor_scalar(
            x1_s, y_t, st6[:, 17:18], st6[:, 23:24], ALU.subtract, ALU.mult),
            [B("y_t"), B("st6")], [B("x1_s")], tag="x1")
        dma("sync", d_x1[tt * 128:(tt + 1) * 128, :], x1_s,
            [B("x1_s")], [B("d_x1")], tag="st_x1")
        dma("pool", d_x1b[tt * 128:(tt + 1) * 128, :], x1_s,
            [B("x1_s")], [B("d_x1b")], tag="st_x1b")

    sfree("out_fm"); sfree("wdT"); sfree("bdb"); sfree("y_t")
    sfree("xt0"); sfree("xt1"); sfree("x1_s")

    if "x1" in debug_dumps:
        t = dumpt("x1", (S, D))
        dma("sync", t, d_x1, [B("d_x1")], [B("dump_x1")])

    if stop_after == "x1":
        P.finalize()
        lf = {i: c for i, c in enumerate(P.lane_count) if c}
        return nc, P, dumps, lf, stack
    gbb = bcast_from("gbb", d_gb[None, :], 32)
    x1_fm = newv("x1_fm", 128, 8 * 2048, BF16)
    x1_fmv = x1_fm.rearrange("p (c t) -> p c t", c=8)
    for dc in range(8):
        dma("act", x1_fmv[:, dc, :], d_x1b[:, dc * 128:(dc + 1) * 128],
            [B("d_x1b")], [B("x1_fm")], tag="Tx1", transpose=True)

    gwsb = newv("gwsb", 128, 8 * 32, BF16)
    gwv = gwsb.rearrange("p (c e) -> p c e", c=8)
    dma("pool", gwv, d_gw.rearrange("(c p) e -> p c e", p=128),
        [], [B("gwsb")], tag="ld_gw")
    cw_tm = newv("cw_tm", 128, 16 * 32, F32,
                 bufs_=[f"cw{t}" for t in range(16)])
    cwv = cw_tm.rearrange("p (c e) -> p c e", c=16)
    cwb = [B(f"cw{t}") for t in range(16)]
    cw_fm = newv("cw_fm", 32, 2048, BF16)
    gt = newv("gt", 128, 192, F32)
    g_p = gt[:, 0:32]; g_mx = gt[:, 32:33]; g_sum = gt[:, 33:34]
    g_gs = gt[:, 34:42]; g_g8 = gt[:, 42:50]; g_keepg = gt[:, 50:58]
    g_rs = gt[:, 58:59]
    g_keepe = gt[:, 64:96]; g_tt1 = gt[:, 96:128]; g_m8 = gt[:, 128:136]
    g_pm = gt[:, 136:168]
    gtb = B("gt")
    for tt in range(16):
        bk = nextbank()
        for dc in range(8):
            mm(bank(bk)[:, 0:32], x1_fmv[:, dc, tt * 128:(tt + 1) * 128],
               gwv[:, dc, :], dc == 0, dc == 7,
               [B("x1_fm"), B("gwsb")], [pbuf[bk]], tag="gate_mm")
        op("dve", lambda e, bk=bk: e.tensor_tensor(
            g_p, bank(bk)[:, 0:32], gbb, ALU.add),
            [pbuf[bk], B("gbb"), gtb], [gtb], tag="gate_b")
        op("dve", lambda e: e.tensor_reduce(g_mx, g_p, mybir.AxisListType.X,
                                            ALU.max),
           [gtb], [gtb], tag="gate_mx")
        op("dve", lambda e: e.tensor_scalar_mul(g_mx, g_mx, -1.0),
           [gtb], [gtb], tag="gate_nmx")
        op("act", lambda e: e.activation(g_pm, g_p, AF.Exp, bias=g_mx,
                                         accum_out=g_sum),
           [gtb], [gtb], tag="gate_exp")
        op("dve", lambda e: e.reciprocal(g_rs, g_sum), [gtb], [gtb],
           tag="gate_rs")
        op("dve", lambda e: e.tensor_scalar_mul(g_p, g_pm, g_rs), [gtb], [gtb],
           tag="gate_p")
        pv = g_p.rearrange("p (g j) -> p g j", g=8)
        first = True
        for a in range(4):
            for b in range(a + 1, 4):
                op("dve", lambda e, a=a, b=b: e.tensor_tensor(
                    g_tt1[:, 0:8], pv[:, :, a], pv[:, :, b], ALU.add),
                    [gtb], [gtb], tag="g2a")
                if first:
                    op("dve", lambda e: e.tensor_copy(g_gs, g_tt1[:, 0:8]),
                       [gtb], [gtb], tag="g2b")
                    first = False
                else:
                    op("dve", lambda e: e.tensor_tensor(
                        g_gs, g_gs, g_tt1[:, 0:8], ALU.max),
                        [gtb], [gtb], tag="g2c")
        op("dve", lambda e: e.max(g_g8, g_gs), [gtb], [gtb], tag="g8")
        op("dve", lambda e: e.tensor_scalar(g_keepg, g_gs, g_g8[:, 3:4], None,
                                            ALU.is_ge),
           [gtb], [gtb], tag="keepg")
        kev = g_keepe.rearrange("p (g j) -> p g j", g=8)
        for jj in range(4):
            op("dve", lambda e, jj=jj: e.tensor_copy(kev[:, :, jj], g_keepg),
               [gtb], [gtb], tag="keepe")
        op("dve", lambda e: e.tensor_scalar(g_tt1, g_keepe, 1e30, 1e30,
                                            ALU.mult, ALU.subtract),
           [gtb], [gtb], tag="gmask1")
        op("dve", lambda e: e.tensor_tensor(g_pm, g_p, g_keepe, ALU.mult),
           [gtb], [gtb], tag="gmask2")
        op("dve", lambda e: e.tensor_tensor(g_pm, g_pm, g_tt1, ALU.add),
           [gtb], [gtb], tag="gmask3")
        op("dve", lambda e: e.max(g_m8, g_pm), [gtb], [gtb], tag="gm8")
        op("dve", lambda e: e.tensor_scalar(g_tt1, g_pm, g_m8[:, 3:4],
                                            None, ALU.is_ge),
           [gtb], [gtb], tag="keep4")
        op("dve", lambda e, tt=tt: e.tensor_tensor(
            cwv[:, tt, :], g_p, g_tt1, ALU.mult),
            [gtb], [cwb[tt]], tag="cwt")
        bk2 = nextbank()
        P.stepw("pe", lambda e, tt=tt, bk2=bk2: e.transpose(
            bank(bk2)[0:32, 0:128], cwv[:, tt, :], c_ident),
            reads=[cwb[tt], B("c_ident")], writes=[pbuf[bk2]], tag="cwT")
        op("dve", lambda e, tt=tt, bk2=bk2: e.tensor_copy(
            cw_fm[0:32, tt * 128:(tt + 1) * 128], bank(bk2)[0:32, 0:128]),
            [pbuf[bk2]], [B("cw_fm")], tag="cwTe")

    dma("sync", d_cw, cw_fm, [B("cw_fm")], [B("d_cw")], tag="st_cw")

    if "cw" in debug_dumps:
        t = dumpt("cw", (128, 512))
        dma("sync", t, cw_tm, cwb, [B("dump_cw")])

    if stop_after == "gate":
        P.finalize()
        lf = {i: c for i, c in enumerate(P.lane_count) if c}
        return nc, P, dumps, lf, stack
    sfree("cw_tm"); sfree("gt"); sfree("gwsb")
    b2sb = newv("b2sb", 32, 1024, BF16)
    dma("pool", b2sb, d_b2, [], [B("b2sb")], tag="ld_b2")
    y_sb = newv("y_sb", 128, 16 * 1024, F32,
                bufs_=[f"y{t}" for t in range(16)])
    y_sbv = y_sb.rearrange("p (c j) -> p c j", c=16)
    ytb = [B(f"y{t}") for t in range(16)]

    w1s = [newv("w1s0", 128, 4096, BF16), newv("w1s1", 128, 4096, BF16)]
    w3s = [newv("w3s0", 128, 4096, BF16), newv("w3s1", 128, 4096, BF16)]
    w2s = [newv("w2s0", 128, 4096, BF16), newv("w2s1", 128, 4096, BF16)]
    w1b = [B("w1s0"), B("w1s1")]; w3b = [B("w3s0"), B("w3s1")]
    w2b = [B("w2s0"), B("w2s1")]
    hh_fm = [newv("hh0", 128, 4 * 2048, BF16), newv("hh1", 128, 4 * 2048, BF16)]
    hh_fmv = [t.rearrange("p (c j) -> p c j", c=4) for t in hh_fm]
    hhb = [B("hh0"), B("hh1")]
    s1_s = newv("s1_s", 128, 2 * 512, F32, bufs_=["s10", "s11"])
    s1_sv = s1_s.rearrange("p (r j) -> p r j", r=2)
    s1b = [B("s10"), B("s11")]
    s4_s = newv("s4_s", 128, 2 * 512, BF16, bufs_=["s40", "s41"])
    s4_sv = s4_s.rearrange("p (r j) -> p r j", r=2)
    s4b = [B("s40"), B("s41")]
    cwrow = [newv("cwrow0", 1, 2048, BF16)] * 2
    cwrb = [B("cwrow0")] * 2
    cwbt = [newv("cwb0", 128, 2048, BF16)] * 2
    cwbb = [B("cwb0")] * 2

    b1T = newv("b1T", 128, 128, F32)
    b3T = newv("b3T", 128, 128, F32)
    sb1T = newv("sb1T", 128, 8, F32)
    sb3T = newv("sb3T", 128, 8, F32)
    bsb = newv("bsb", 32, 512, F32, side="hi")
    for (dsrc, bT, bTn) in ((d_b1, b1T, "b1T"), (d_b3, b3T, "b3T")):
        dma("sync", bsb[0:32, 0:512], dsrc, [B("bsb")], [B("bsb")],
            tag="ld_" + bTn)
        for ic in range(4):
            bk = nextbank()
            P.stepw("pe", lambda e, ic=ic, bk=bk: e.transpose(
                bank(bk)[:, 0:32], bsb[0:32, ic * 128:(ic + 1) * 128],
                c_ident[0:32, 0:32]),
                reads=[B("bsb"), B("c_ident")], writes=[pbuf[bk]], tag="Tb")
            op("act", lambda e, ic=ic, bk=bk, bT=bT: e.activation(
                bT[:, ic * 32:(ic + 1) * 32], bank(bk)[:, 0:32], AF.Copy),
                [pbuf[bk]], [B(bTn)], tag="Tbe")
    for (dsrc, bT, bTn) in ((d_sb1, sb1T, "sb1T"), (d_sb3, sb3T, "sb3T")):
        dma("sync", bsb[0:2, 0:512].rearrange("h c -> h c"),
            dsrc.rearrange("o (h c) -> (o h) c", h=2), [B("bsb")], [B("bsb")],
            tag="ld_" + bTn)
        for ic in range(4):
            bk = nextbank()
            P.stepw("pe", lambda e, ic=ic, bk=bk: e.transpose(
                bank(bk)[:, 0:2], bsb[0:2, ic * 128:(ic + 1) * 128],
                c_ident[0:2, 0:2]),
                reads=[B("bsb"), B("c_ident")], writes=[pbuf[bk]], tag="Tsb")
            op("act", lambda e, ic=ic, bk=bk, bT=bT: e.activation(
                bT[:, ic * 2:ic * 2 + 2], bank(bk)[:, 0:2], AF.Copy),
                [pbuf[bk]], [B(bTn)], tag="Tsbe")
    sfree("bsb")

    def exp_srcs(e):
        if e < 32:
            return (d_w1[e], d_w3[e], d_w2[e])
        hh = e - 32
        return (d_sw1[:, hh * 512:(hh + 1) * 512],
                d_sw3[:, hh * 512:(hh + 1) * 512],
                d_sw2[hh * 512:(hh + 1) * 512, :])

    def load_expert(e):
        sl = e % 2
        s_w1, s_w3, s_w2 = exp_srcs(e)
        dma("pool", w1s[sl].rearrange("p (c i) -> p c i", c=8),
            s_w1.rearrange("(c p) i -> p c i", p=128), [], [w1b[sl]],
            tag=f"ldw1_{e}")
        dma("pool", w3s[sl].rearrange("p (c i) -> p c i", c=8),
            s_w3.rearrange("(c p) i -> p c i", p=128), [], [w3b[sl]],
            tag=f"ldw3_{e}")
        dma("pool", w2s[sl].rearrange("p (c i) -> p c i", c=4),
            s_w2.rearrange("(c p) i -> p c i", p=128), [], [w2b[sl]],
            tag=f"ldw2_{e}")

    def h_phase(e):
        sl = e % 2
        w1v = w1s[sl].rearrange("p (c i) -> p c i", c=8)
        w3v = w3s[sl].rearrange("p (c i) -> p c i", c=8)
        if e < 32:
            dma("sync", cwrow[sl], d_cw[None, e, :], [B("d_cw")], [cwrb[sl]],
                tag=f"ldcw_{e}")
            for tf in range(4):
                bk = nextbank()
                mm(bank(bk), ones_rb, cwrow[sl][0:1, tf * 512:(tf + 1) * 512],
                   True, True, [B("ones_rb"), cwrb[sl]], [pbuf[bk]], tag="cwbm")
                op("act", lambda e_, sl=sl, tf=tf, bk=bk: e_.activation(
                    cwbt[sl][:, tf * 512:(tf + 1) * 512], bank(bk), AF.Copy),
                    [pbuf[bk]], [cwbb[sl]], tag="cwbe")
        for ic in range(4):
            if e < 32:
                b1c = b1T[:, ic * 32 + e:ic * 32 + e + 1]
                b3c = b3T[:, ic * 32 + e:ic * 32 + e + 1]
                b1cb, b3cb = B("b1T"), B("b3T")
            else:
                hx = e - 32
                b1c = sb1T[:, ic * 2 + hx:ic * 2 + hx + 1]
                b3c = sb3T[:, ic * 2 + hx:ic * 2 + hx + 1]
                b1cb, b3cb = B("sb1T"), B("sb3T")
            for tf in range(4):
                r = (ic * 4 + tf) % 2
                b1k = nextbank()
                b3k = nextbank()
                for dc in range(8):
                    mm(bank(b1k), w1v[:, dc, ic * 128:(ic + 1) * 128],
                       x1_fmv[:, dc, tf * 512:(tf + 1) * 512], dc == 0, dc == 7,
                       [B("x1_fm"), w1b[sl]], [pbuf[b1k]], tag="h1mm")
                for dc in range(8):
                    mm(bank(b3k), w3v[:, dc, ic * 128:(ic + 1) * 128],
                       x1_fmv[:, dc, tf * 512:(tf + 1) * 512], dc == 0, dc == 7,
                       [B("x1_fm"), w3b[sl]], [pbuf[b3k]], tag="h3mm")
                op("act", lambda e_, b1k=b1k, r=r, b1c=b1c: e_.activation(
                    s1_sv[:, r, :], bank(b1k), AF.Silu, bias=b1c),
                    [pbuf[b1k], b1cb], [s1b[r]], tag="silu")
                if e < 32:
                    op("dve", lambda e_, b3k=b3k, r=r, b3c=b3c:
                       e_.scalar_tensor_tensor(
                           s4_sv[:, r, :], bank(b3k), b3c, s1_sv[:, r, :],
                           ALU.add, ALU.mult),
                       [pbuf[b3k], s1b[r], b3cb], [s4b[r]], tag="h3c")
                    op("pool", lambda e_, sl=sl, ic=ic, tf=tf, r=r:
                       e_.tensor_mul(
                           hh_fmv[sl][:, ic, tf * 512:(tf + 1) * 512],
                           s4_sv[:, r, :],
                           cwbt[sl][:, tf * 512:(tf + 1) * 512]),
                       [s4b[r], cwbb[sl]], [hhb[sl]], tag="hhat")
                else:
                    op("dve", lambda e_, b3k=b3k, r=r, b3c=b3c, sl=sl, ic=ic,
                       tf=tf: e_.scalar_tensor_tensor(
                           hh_fmv[sl][:, ic, tf * 512:(tf + 1) * 512],
                           bank(b3k), b3c, s1_sv[:, r, :],
                           ALU.add, ALU.mult),
                       [pbuf[b3k], s1b[r], b3cb], [hhb[sl]], tag="hhat_s")

    def w2_phase(es, first_group):
        for tt in range(16):
            for dh in range(2):
                bk = nextbank()
                nmm = len(es) * 4 + (1 if first_group else 0)
                i = 0
                if first_group:
                    mm(bank(bk), cw_fm[0:32, tt * 128:(tt + 1) * 128],
                       b2sb[:, dh * 512:(dh + 1) * 512], True, False,
                       [B("cw_fm"), B("b2sb")], [pbuf[bk]], tag="b2cw")
                    i = 1
                for e in es:
                    sl = e % 2
                    w2v = w2s[sl].rearrange("p (c j) -> p c j", c=4)
                    for ic in range(4):
                        mm(bank(bk), hh_fmv[sl][:, ic, tt * 128:(tt + 1) * 128],
                           w2v[:, ic, dh * 512:(dh + 1) * 512],
                           i == 0, i == nmm - 1,
                           [hhb[sl], w2b[sl]], [pbuf[bk]], tag="w2mm")
                        i += 1
                sl2 = slice(dh * 512, (dh + 1) * 512)
                if first_group:
                    op("act", lambda e_, tt=tt, sl2=sl2, bk=bk: e_.activation(
                        y_sbv[:, tt, sl2], bank(bk), AF.Copy),
                        [pbuf[bk]], [ytb[tt]], tag="yw")
                else:
                    op("dve", lambda e_, tt=tt, sl2=sl2, bk=bk: e_.tensor_tensor(
                        y_sbv[:, tt, sl2], bank(bk), y_sbv[:, tt, sl2], ALU.add),
                        [pbuf[bk], ytb[tt]], [ytb[tt]], tag="yadd")

    load_expert(0)
    load_expert(1)
    h_phase(0)
    h_phase(1)
    first = True
    for g0 in range(0, NEX, 2):
        w2_phase([g0, g0 + 1], first)
        first = False
        if g0 + 2 < NEX:
            load_expert(g0 + 2)
            load_expert(g0 + 3)
            h_phase(g0 + 2)
            h_phase(g0 + 3)

    for nm_ in ("w1s0", "w1s1", "w3s0", "w3s1", "w2s0", "w2s1", "hh0", "hh1",
                "s1_s", "s4_s", "cwrow0", "cwb0",
                "b1T", "b3T", "sb1T", "sb3T",
                "cw_fm", "gbb", "b2sb", "x1_fm"):
        sfree(nm_)
    sb2b = bcast_from("sb2b", d_sb2[None, :], 1024)
    x2_s = newv("x2_s", 128, 1024, F32)
    xf2_0 = newv("xf2_0", 128, 1024, F32)
    xf2_1 = newv("xf2_1", 128, 1024, F32)
    xf2s = [(xf2_0, B("xf2_0")), (xf2_1, B("xf2_1"))]
    out_sids = []
    for tt in range(16):
        xv, xb_ = xf2s[tt % 2]
        dma("sync", xv, d_x1[tt * 128:(tt + 1) * 128, :], [B("d_x1"), xb_],
            [xb_], tag="ld_x1f")
        op("pool", lambda e, tt=tt, xv=xv: e.tensor_add(
            y_sbv[:, tt, :], y_sbv[:, tt, :], xv),
            [xb_, ytb[tt]], [ytb[tt]], tag="fadd1")
        op("pool", lambda e, tt=tt: e.tensor_add(
            y_sbv[:, tt, :], y_sbv[:, tt, :], sb2b),
            [B("sb2b"), ytb[tt]], [ytb[tt]], tag="fadd2")
        ln_stats(y_sbv[:, tt, 0:512], y_sbv[:, tt, 512:1024], [ytb[tt]],
                 eps5_c, B("eps5_c"))
        op("dve", lambda e, tt=tt: e.tensor_scalar(
            x2_s, y_sbv[:, tt, :], st6[:, 17:18], st6[:, 23:24],
            ALU.subtract, ALU.mult),
            [ytb[tt], B("st6")], [B("x2_s")], tag="x2")
        sid = dma("sync", d_final[tt * 128:(tt + 1) * 128, :], x2_s,
                  [B("x2_s")], [B("d_final")], tag="st_x2")
        out_sids.append(sid)

    P.finalize()
    lane_final = {i: c for i, c in enumerate(P.lane_count) if c}
    return nc, P, dumps, lane_final, stack


_CONSTS = _host_consts()


def _weights_dict(kw):
    return dict(
        ln_g=kw["L0_attn_ln_g"], ln_b=kw["L0_attn_ln_b"],
        mh_g=kw["L0_attn_mh_g"], mh_b=kw["L0_attn_mh_b"],
        wupl=kw["L0_attn_W_up_l"], bupl=kw["L0_attn_b_up_l"],
        wupr=kw["L0_attn_W_up_r"], bupr=kw["L0_attn_b_up_r"],
        wdown=kw["L0_attn_W_down"], bdown=kw["L0_attn_b_down"],
        convw=kw["L0_attn_conv_w"],
        convb=np.asarray(kw["L0_attn_conv_b"], np.float32).reshape(1),
        wskip=kw["L0_attn_W_skip"], bskip=kw["L0_attn_b_skip"],
        wi=kw["L0_attn_W_i"],
        bi=np.asarray(kw["L0_attn_b_i"], np.float32).reshape(H, 1),
        wf=kw["L0_attn_W_f"],
        bf=np.asarray(kw["L0_attn_b_f"], np.float32).reshape(H, 1),
        wo=kw["L0_attn_W_o"], bo=kw["L0_attn_b_o"],
        wq=kw["L0_attn_W_q"], bq=kw["L0_attn_b_q"],
        wk=kw["L0_attn_W_k"], bk=kw["L0_attn_b_k"],
        wv=kw["L0_attn_W_v"], bv=kw["L0_attn_b_v"],
        gw=kw["L0_ffn_gw"], gb=kw["L0_ffn_gb"],
        w1=kw["L0_ffn_w1"], b1=kw["L0_ffn_b1"],
        w3=kw["L0_ffn_w3"], b3=kw["L0_ffn_b3"],
        w2=kw["L0_ffn_w2"], b2=kw["L0_ffn_b2"],
        sw1=kw["L0_ffn_sw1"],
        sb1=np.asarray(kw["L0_ffn_sb1"], np.float32).reshape(1, -1),
        sw3=kw["L0_ffn_sw3"],
        sb3=np.asarray(kw["L0_ffn_sb3"], np.float32).reshape(1, -1),
        sw2=kw["L0_ffn_sw2"], sb2=kw["L0_ffn_sb2"],
    )


def _run(kw, core_ids, debug_dumps=(), stop_after=None):
    nc, P, dumps, lane_final, stack = build_program(debug_dumps=debug_dumps,
                                                    stop_after=stop_after)
    with stack:
        with ExitStack() as es2:
            esems = {s: es2.enter_context(nc.semaphore(f"es_{s}"))
                     for s in Prog.STREAMS}
            lsems = [es2.enter_context(nc.semaphore(f"lane{i}"))
                     for i in range(P.n_lanes)]
            with nc.Block() as block:
                @block.tensor
                def _(eng):
                    _emit(P, "pe", eng, esems, lsems)

                @block.vector
                def _(eng):
                    _emit(P, "dve", eng, esems, lsems)

                @block.scalar
                def _(eng):
                    _emit(P, "act", eng, esems, lsems)

                @block.gpsimd
                def _(eng):
                    _emit(P, "pool", eng, esems, lsems)

                @block.sync
                def _(eng):
                    _emit(P, "sync", eng, esems, lsems)
                    for ln, pos in sorted(lane_final.items()):
                        eng.wait_ge(lsems[ln], pos * 16)

            x_full = np.ascontiguousarray(
                np.asarray(kw["L0_attn_x"], dtype=np.float32))
            W = {k: np.ascontiguousarray(np.asarray(v, np.float32))
                 for k, v in _weights_dict(kw).items()}
            in_maps = []
            for c in core_ids:
                m = dict(_CONSTS)
                m["x"] = x_full[c]
                m.update(W)
                in_maps.append(m)
            res = run_bass_kernel_spmd(nc, in_maps, core_ids=list(core_ids))
    return res


def kernel(**kw):
    res = _run(kw, list(range(8)))
    return np.stack([np.asarray(r["x2"], np.float32) for r in res.results])


# version 06 (final)
# speedup vs baseline: 1.0322x; 1.0322x over previous
import numpy as np
from contextlib import ExitStack
import concourse.bass as bass
import concourse.mybir as mybir
from concourse.bass_utils import run_bass_kernel_spmd

F32 = mybir.dt.float32
BF16 = mybir.dt.bfloat16
I8 = mybir.dt.int8
AF = mybir.ActivationFunctionType
ALU = mybir.AluOpType

S = 2048
D = 1024
P2 = 2048
HID = 512
H = 8
HD = 64
NE = 32
NEX = 34
INTER = 512
SB_BYTES = 206 * 1024


def _host_consts():
    c = {}
    c["cident"] = np.eye(128, dtype=np.float32)
    ind8 = np.zeros((8, 1024), np.float32)
    for h in range(8):
        ind8[h, h * 128:(h + 1) * 128] = 1.0
    c["cind8"] = ind8
    s = np.arange(128)
    m1 = np.where(s[:, None] <= s[None, :], 0.0, -1e30).astype(np.float32)
    c["cmask"] = np.tile(m1, (1, 8)).astype(np.float32)
    par2 = np.zeros((2, 128), np.float32)
    par2[0, :64] = 1.0
    par2[1, 64:] = 1.0
    c["cpar2"] = par2
    dd = np.arange(128)
    bm = np.zeros((4, 128, 128), np.float32)
    b2m = np.zeros((4, 128, 128), np.float32)
    for m in range(4):
        bm[m] = (dd[:, None] == (dd[None, :] - 3 + m)).astype(np.float32)
        b2m[m] = (dd[:, None] == (dd[None, :] - 3 + m + 128)).astype(np.float32)
    c["cband"] = bm
    c["cband2"] = b2m
    return c


class Buf:
    __slots__ = ("name", "writers", "readers")

    def __init__(self, name):
        self.name = name
        self.writers = []
        self.readers = {}


class Prog:
    STREAMS = ("pe", "dve", "act", "pool", "sync")

    def __init__(self):
        self.steps = []
        self.stream_steps = {s: [] for s in self.STREAMS}
        self.n_lanes = 17
        self.lane_count = [0] * self.n_lanes
        self.lane_rr = 0
        self.lane_rr_sw = 0

    def stepw(self, stream, fn, reads=(), writes=(), dma=False, tag="",
              tpose=False):
        sid = len(self.steps)
        deps = set()
        for b in reads:
            deps.update(b.writers)
        for b in writes:
            deps.update(b.readers.values())
            deps.update(b.writers)
        deps.discard(sid)
        if dma:
            if tpose:
                lane = 16
            elif stream == "pool":
                lane = 12 + self.lane_rr_sw % 4
                self.lane_rr_sw += 1
            else:
                lane = self.lane_rr % 12
                self.lane_rr += 1
            self.lane_count[lane] += 1
            pos = self.lane_count[lane]
        else:
            lane, pos = None, None
        st = dict(sid=sid, stream=stream, fn=fn, deps=sorted(deps), dma=dma,
                  lane=lane, lanepos=pos, tag=tag, marked=False, semval=None)
        self.steps.append(st)
        self.stream_steps[stream].append(sid)
        for b in reads:
            b.readers[stream] = sid
        for b in writes:
            b.writers = [sid]
            b.readers = {}
        return sid

    NEAR = 6

    def finalize(self):
        order = {}
        for s in self.STREAMS:
            for i, sid in enumerate(self.stream_steps[s]):
                order[sid] = i
        for st in self.steps:
            for d in st["deps"]:
                dst = self.steps[d]
                if dst["dma"] or dst["stream"] != st["stream"]:
                    dst["marked"] = True
                elif (st["stream"] in ("dve", "act", "pool")
                      and order[st["sid"]] - order[d] < self.NEAR):
                    dst["marked"] = True
                    st.setdefault("near", []).append(d)
        self.stream_totals = {}
        for s in self.STREAMS:
            v = 0
            for sid in self.stream_steps[s]:
                st = self.steps[sid]
                if st["dma"]:
                    continue
                if st["marked"]:
                    v += 1
                    st["semval"] = v
            self.stream_totals[s] = v
        know = {s: {} for s in self.STREAMS}
        snap = [None] * len(self.steps)
        for st in self.steps:
            s = st["stream"]
            k = know[s]
            waits = []
            near = st.get("near", ())
            for d in st["deps"]:
                dst = self.steps[d]
                if dst["dma"]:
                    key = ("l", dst["lane"])
                    val = dst["lanepos"] * 16
                else:
                    if dst["stream"] == s and d not in near:
                        dk = snap[d]
                        for kk, vv in dk.items():
                            if k.get(kk, 0) < vv:
                                k[kk] = vv
                        continue
                    key = ("e", dst["stream"])
                    val = dst["semval"]
                if k.get(key, 0) >= val:
                    continue
                waits.append((key, val))
                k[key] = val
                dk = snap[d]
                for kk, vv in dk.items():
                    if k.get(kk, 0) < vv:
                        k[kk] = vv
            st["waits"] = waits
            if st["dma"]:
                k2 = dict(k)
                k2[("l", st["lane"])] = st["lanepos"] * 16
                snap[st["sid"]] = k2
            elif st["marked"]:
                k2 = dict(k)
                k2[("e", s)] = st["semval"]
                snap[st["sid"]] = k2
            else:
                snap[st["sid"]] = k


def _emit(prog, stream, eng, esems, lsems):
    for sid in prog.stream_steps[stream]:
        st = prog.steps[sid]
        for (key, val) in st["waits"]:
            if key[0] == "e":
                eng.wait_ge(esems[key[1]], val)
            else:
                eng.wait_ge(lsems[key[1]], val)
        if st["dma"] and st["lanepos"] > 1:
            eng.wait_ge(lsems[st["lane"]], (st["lanepos"] - 1) * 16)
        inst = st["fn"](eng)
        if st["dma"]:
            inst.then_inc(lsems[st["lane"]], 16)
        elif st["marked"]:
            inst.then_inc(esems[stream], 1)


def build_program(debug_dumps=(), stop_after=None):
    nc = bass.Bass("TRN2", target_bir_lowering=False,
                   detect_race_conditions=False)

    def inp(name, shape, dtype=F32):
        return nc.dram_tensor(name, list(shape), dtype,
                              kind="ExternalInput").ap()

    d_x = inp("x", (S, D))
    d_lng = inp("ln_g", (D,)); d_lnb = inp("ln_b", (D,))
    d_mhg = inp("mh_g", (H, HD)); d_mhb = inp("mh_b", (H, HD))
    d_wupl = inp("wupl", (P2, D)); d_bupl = inp("bupl", (P2,))
    d_wupr = inp("wupr", (HID, D)); d_bupr = inp("bupr", (HID,))
    d_wdown = inp("wdown", (D, HID)); d_bdown = inp("bdown", (D,))
    d_convw = inp("convw", (4,)); d_convb = inp("convb", (1,))
    d_wskip = inp("wskip", (HID, P2)); d_bskip = inp("bskip", (HID,))
    d_wi = inp("wi", (H, P2)); d_bi = inp("bi", (H, 1))
    d_wf = inp("wf", (H, P2)); d_bf = inp("bf", (H, 1))
    d_wo = inp("wo", (HID, P2)); d_bo = inp("bo", (HID,))
    d_wq = inp("wq", (HID, P2)); d_bq = inp("bq", (HID,))
    d_wk = inp("wk", (HID, P2)); d_bk = inp("bk", (HID,))
    d_wv = inp("wv", (HID, P2)); d_bv = inp("bv", (HID,))
    d_gw = inp("gw", (D, NE)); d_gb = inp("gb", (NE,))
    d_w1 = inp("w1", (NE, D, INTER)); d_b1 = inp("b1", (NE, INTER))
    d_w3 = inp("w3", (NE, D, INTER)); d_b3 = inp("b3", (NE, INTER))
    d_w2 = inp("w2", (NE, INTER, D)); d_b2 = inp("b2", (NE, D))
    d_sw1 = inp("sw1", (D, 2 * INTER)); d_sb1 = inp("sb1", (1, 2 * INTER))
    d_sw3 = inp("sw3", (D, 2 * INTER)); d_sb3 = inp("sb3", (1, 2 * INTER))
    d_sw2 = inp("sw2", (2 * INTER, D)); d_sb2 = inp("sb2", (D,))
    d_cident = inp("cident", (128, 128))
    d_cind8 = inp("cind8", (8, 1024))
    d_cmask = inp("cmask", (128, 1024))
    d_cpar2 = inp("cpar2", (2, 128))
    d_cband = inp("cband", (4, 128, 128))
    d_cband2 = inp("cband2", (4, 128, 128))

    d_final = nc.dram_tensor("x2", [S, D], F32, kind="ExternalOutput").ap()

    def scratch(name, shape, dtype):
        return nc.dram_tensor(name, list(shape), dtype, kind="Internal").ap()

    d_xn = scratch("s_xn", (S, D), BF16)
    d_wuplb = scratch("s_wuplb", (P2, D), BF16)
    d_wuprb = scratch("s_wuprb", (HID, D), BF16)
    d_wvb = scratch("s_wvb", (HID, P2), BF16)
    d_wob = scratch("s_wob", (HID, P2), BF16)
    d_wqb = scratch("s_wqb", (HID, P2), BF16)
    d_wkb = scratch("s_wkb", (HID, P2), BF16)
    d_wskipb = scratch("s_wskipb", (HID, P2), BF16)
    d_wib = scratch("s_wib", (H, P2), BF16)
    d_wfb = scratch("s_wfb", (H, P2), BF16)
    d_wdownb = scratch("s_wdownb", (D, HID), BF16)
    d_k = scratch("s_k", (HID, S), BF16)
    d_o = scratch("s_o", (S, HID), BF16)
    d_skip = scratch("s_skip", (S, HID), BF16)
    d_v = scratch("s_v", (S, HID), BF16)
    d_rt = scratch("s_rt", (S, HID), BF16)
    d_outp = scratch("s_outp", (S, HID), BF16)
    d_x1 = scratch("s_x1", (S, D), F32)
    d_x1b = scratch("s_x1b", (S, D), BF16)
    d_negw3 = scratch("s_negw3", (16, 8, 128), F32)
    d_u3 = scratch("s_u3", (16, 8, 128), F32)
    d_mprev2 = scratch("s_mprev2", (16, 8), F32)
    d_negwl2 = scratch("s_negwl2", (16, 8), F32)
    d_phi2 = scratch("s_phi2", (16, 8), F32)
    d_cw = scratch("s_cw", (NE, S), BF16)

    dumps = {}
    def dumpt(name, shape, dtype=F32):
        dumps[name] = nc.dram_tensor("dump_" + name, list(shape), dtype,
                                     kind="ExternalOutput").ap()
        return dumps[name]

    P = Prog()
    bufs = {}
    def B(name):
        if name not in bufs:
            bufs[name] = Buf(name)
        return bufs[name]

    stack = ExitStack()
    arena = stack.enter_context(nc.sbuf_tensor("arena", [128, SB_BYTES], I8))
    arena_ap = arena[:, :]
    free_ranges = [(0, SB_BYTES)]
    allocs = {}
    graveyard = []

    def newv(name, parts, elems, dtype, bufs_=(), side="lo"):
        esz = 2 if dtype == BF16 else (1 if dtype == I8 else 4)
        sz = elems * esz
        align = 64
        sz = (sz + align - 1) // align * align
        off = None
        idxs = (range(len(free_ranges)) if side == "lo"
                else range(len(free_ranges) - 1, -1, -1))
        for i in idxs:
            a, b = free_ranges[i]
            if side == "lo":
                a2 = (a + align - 1) // align * align
            else:
                a2 = (b - sz) // align * align
                if a2 < a:
                    continue
            if b - a2 >= sz and a2 >= a:
                off = a2
                rest = []
                if a2 > a:
                    rest.append((a, a2))
                if a2 + sz < b:
                    rest.append((a2 + sz, b))
                free_ranges[i:i + 1] = rest
                break
        if off is None:
            used = sum(v[1] for v in allocs.values())
            lay = sorted((v[0], v[1], k) for k, v in allocs.items())
            raise MemoryError(f"SBUF full for {name} ({sz}B); used={used}; "
                              f"free={free_ranges}; layout={lay}")
        bufnames = [name] + list(bufs_)
        allocs[name] = (off, sz, bufnames)
        old = set()
        for (ga, gb_, sids) in graveyard:
            if not (gb_ <= off or ga >= off + sz):
                old.update(sids)
        if old:
            for bn in bufnames:
                B(bn).writers = sorted(set(B(bn).writers) | old)
        return arena_ap[0:parts, off:off + elems * esz].bitcast(dtype)

    def sfree(name):
        off, sz, bufnames = allocs.pop(name)
        sids = set()
        for bn in bufnames:
            sids.update(B(bn).writers)
            sids.update(B(bn).readers.values())
        graveyard.append((off, off + sz, sorted(sids)))
        free_ranges.append((off, off + sz))
        free_ranges.sort()
        merged = []
        for r in free_ranges:
            if merged and merged[-1][1] >= r[0]:
                merged[-1] = (merged[-1][0], max(merged[-1][1], r[1]))
            else:
                merged.append(list(r))
        free_ranges[:] = [tuple(r) for r in merged]

    pa = stack.enter_context(nc.psum_tensor("pa", [128, 4096], F32))
    def bank(k, cols=512):
        return pa[:, k * 512:k * 512 + cols]
    pbuf = [B(f"psum{k}") for k in range(8)]


    def dma(stream, dst, src, reads, writes, tag="", **kw):
        return P.stepw(stream,
                       (lambda eng, dst=dst, src=src, kw=kw:
                        eng.dma_start(out=dst, in_=src, **kw)),
                       reads=reads, writes=writes, dma=True, tag=tag,
                       tpose=bool(kw.get("transpose")))

    def mm(out, lhsT, rhs, start, stop, reads, writes, tag=""):
        return P.stepw("pe",
                       (lambda eng, out=out, lhsT=lhsT, rhs=rhs, s1=start,
                        s2=stop: eng.matmul(out, lhsT, rhs, start=s1, stop=s2)),
                       reads=reads, writes=writes, tag=tag)

    def op(stream, fn, reads, writes, tag=""):
        return P.stepw(stream, fn, reads=reads, writes=writes, tag=tag)

    rrb = [0]
    def nextbank(lo=0, hi=6):
        rrb[0] += 1
        return lo + (rrb[0] % (hi - lo))

    op("dve", lambda e: e.memset(pa[:, :], 0.0), [], pbuf, tag="psum0")

    for src, dst, nm in ((d_wupl, d_wuplb, "wuplb"), (d_wupr, d_wuprb, "wuprb"),
                         (d_wv, d_wvb, "wvb"), (d_wo, d_wob, "wob"),
                         (d_wq, d_wqb, "wqb"), (d_wk, d_wkb, "wkb"),
                         (d_wskip, d_wskipb, "wskipb"),
                         (d_wdown, d_wdownb, "wdownb")):
        dma("pool", dst, src, [], [B(nm)], tag="cast_" + nm)

    c_ident = newv("c_ident", 128, 128, F32)
    c_ind8 = newv("c_ind8", 8, 1024, F32)
    c_mask = newv("c_mask", 128, 1024, F32)
    c_par2 = newv("c_par2", 2, 128, F32)
    c_band = newv("c_band", 128, 4 * 128, F32)
    c_band2 = newv("c_band2", 128, 4 * 128, F32)
    ones_r = newv("ones_r", 1, 128, F32)
    ones_rb = newv("ones_rb", 1, 128, BF16)
    ones_cb = newv("ones_cb", 128, 1, BF16)
    dma("sync", c_ident, d_cident, [], [B("c_ident")])
    dma("sync", c_ind8, d_cind8, [], [B("c_ind8")])
    dma("sync", c_mask, d_cmask, [], [B("c_mask")])
    dma("sync", c_par2, d_cpar2, [], [B("c_par2")])
    cbv = c_band.rearrange("p (m j) -> p m j", m=4)
    cb2v = c_band2.rearrange("p (m j) -> p m j", m=4)
    dma("sync", cbv, d_cband.rearrange("m p j -> p m j"), [], [B("c_band")])
    dma("sync", cb2v, d_cband2.rearrange("m p j -> p m j"), [], [B("c_band2")])
    op("dve", lambda e: e.memset(ones_r, 1.0), [], [B("ones_r")])
    op("dve", lambda e: e.memset(ones_rb, 1.0), [], [B("ones_rb")])
    op("dve", lambda e: e.memset(ones_cb, 1.0), [], [B("ones_cb")])
    eps6_c = newv("eps6_c", 128, 1, F32)
    eps5_c = newv("eps5_c", 128, 1, F32)
    op("dve", lambda e: e.memset(eps6_c, 1e-6), [], [B("eps6_c")])
    op("dve", lambda e: e.memset(eps5_c, 1e-5), [], [B("eps5_c")])

    bul_c = newv("bul_c", 128, 16, F32)
    dma("sync", bul_c, d_bupl.rearrange("(c p) -> p c", p=128), [], [B("bul_c")], allow_slow_non_contiguous=True)
    bq_c = newv("bq_c", 64, 8, F32)
    dma("sync", bq_c, d_bq.rearrange("(c p) -> p c", p=64), [], [B("bq_c")], allow_slow_non_contiguous=True)
    bk_c = newv("bk_c", 64, 8, F32)
    dma("sync", bk_c, d_bk.rearrange("(c p) -> p c", p=64), [], [B("bk_c")], allow_slow_non_contiguous=True)
    bk8_c = newv("bk8_c", 64, 8, F32)
    op("dve", lambda e: e.tensor_scalar_mul(bk8_c, bk_c, 0.125),
       [B("bk_c")], [B("bk8_c")])
    bi_c = newv("bi_c", 8, 1, F32)
    dma("sync", bi_c, d_bi, [], [B("bi_c")])
    bf_c = newv("bf_c", 8, 1, F32)
    dma("sync", bf_c, d_bf, [], [B("bf_c")])
    bi15 = newv("bi15", 8, 1, F32)
    op("dve", lambda e: e.tensor_scalar_mul(bi15, bi_c, 1.0 / 15.0),
       [B("bi_c")], [B("bi15")])
    bf15 = newv("bf15", 8, 1, F32)
    op("dve", lambda e: e.tensor_scalar_mul(bf15, bf_c, 1.0 / 15.0),
       [B("bf_c")], [B("bf15")])

    bupr_rb = newv("bupr_rb", 1, 512, BF16)
    dma("pool", bupr_rb, d_bupr[None, :], [], [B("bupr_rb")])
    bv_rb = newv("bv_rb", 1, 512, BF16)
    dma("pool", bv_rb, d_bv[None, :], [], [B("bv_rb")])
    bo_rb = newv("bo_rb", 1, 512, BF16)
    dma("pool", bo_rb, d_bo[None, :], [], [B("bo_rb")])
    bskip_rb = newv("bskip_rb", 1, 512, BF16)
    dma("pool", bskip_rb, d_bskip[None, :], [], [B("bskip_rb")])

    def bcast_from(name, dram_row, cols):
        rown = "row_" + name
        row_ap = newv(rown, 1, cols, F32, side="hi")
        dma("sync", row_ap, dram_row, [], [B(rown)], tag="ldr_" + name)
        t = newv(name, 128, cols, F32)
        nhalf = (cols + 511) // 512
        for hh in range(nhalf):
            w = min(512, cols - hh * 512)
            bk = nextbank()
            mm(bank(bk)[:, 0:w], ones_r, row_ap[0:1, hh * 512:hh * 512 + w],
               True, True, [B("ones_r"), B(rown)], [pbuf[bk]], tag="bc_" + name)
            op("act",
               (lambda e, dst=t[:, hh * 512:hh * 512 + w], src=bank(bk)[:, 0:w]:
                e.activation(dst, src, AF.Copy)),
               [pbuf[bk]], [B(name)], tag="bce_" + name)
        sfree(rown)
        return t

    lgb = bcast_from("lgb", d_lng[None, :], 1024)
    lbb = bcast_from("lbb", d_lnb[None, :], 1024)
    wcol4 = bcast_from("wcol4", d_convw[None, :], 4)
    convb_c = bcast_from("convb_c", d_convb[None, :], 1)

    bandA = newv("bandA", 128, 128, F32, side="hi")
    bandB = newv("bandB", 128, 128, F32, side="hi")
    band_bf = newv("band_bf", 128, 128, BF16)
    band2_bf = newv("band2_bf", 128, 128, BF16)
    for which, cb, outv, nm in ((0, cbv, band_bf, "band_bf"),
                                (1, cb2v, band2_bf, "band2_bf")):
        src_buf = B("c_band") if which == 0 else B("c_band2")
        op("dve", lambda e, cb=cb: e.tensor_scalar(
            bandA, cb[:, 0, :], wcol4[:, 0:1], None, ALU.mult),
            [src_buf, B("wcol4")], [B("bandA")])
        cur, oth = bandA, bandB
        curb, othb = B("bandA"), B("bandB")
        for m in range(1, 4):
            op("dve", lambda e, cb=cb, m=m, cur=cur, oth=oth:
               e.scalar_tensor_tensor(oth, cb[:, m, :], wcol4[:, m:m + 1], cur,
                                      ALU.mult, ALU.add),
               [src_buf, B("wcol4"), curb], [othb])
            cur, oth = oth, cur
            curb, othb = othb, curb
        op("dve", lambda e, cur=cur, outv=outv: e.tensor_copy(outv, cur),
           [curb], [B(nm)])
    for nm_ in ("bandA", "bandB", "c_band", "c_band2"):
        sfree(nm_)

    if stop_after == "consts":
        P.finalize()
        lf = {i: c for i, c in enumerate(P.lane_count) if c}
        return nc, P, dumps, lf, stack
    xt0 = newv("xt0", 128, 1024, F32)
    xt1 = newv("xt1", 128, 1024, F32)
    xts = [(xt0, B("xt0")), (xt1, B("xt1"))]
    st6 = newv("st6", 128, 24, F32)
    xn_s = newv("xn_s", 128, 1024, F32)

    def ln_stats(xap0, xap1, rd, eps_col, eps_buf):
        sb = B("st6")
        op("dve", lambda e: e.bn_stats(st6[:, 0:6], xap0), rd + [sb], [sb])
        op("dve", lambda e: e.bn_stats(st6[:, 6:12], xap1), rd + [sb], [sb])
        op("dve", lambda e: e.bn_aggr(st6[:, 12:14], st6[:, 0:6]), [sb], [sb])
        op("dve", lambda e: e.bn_aggr(st6[:, 14:16], st6[:, 6:12]), [sb], [sb])
        op("dve", lambda e: e.tensor_add(st6[:, 16:17], st6[:, 12:13],
                                         st6[:, 14:15]), [sb], [sb])
        op("dve", lambda e: e.tensor_scalar_mul(st6[:, 17:18], st6[:, 16:17],
                                                0.5), [sb], [sb])
        op("dve", lambda e: e.tensor_sub(st6[:, 18:19], st6[:, 12:13],
                                         st6[:, 14:15]), [sb], [sb])
        op("dve", lambda e: e.tensor_mul(st6[:, 19:20], st6[:, 18:19],
                                         st6[:, 18:19]), [sb], [sb])
        op("dve", lambda e: e.tensor_add(st6[:, 20:21], st6[:, 13:14],
                                         st6[:, 15:16]), [sb], [sb])
        op("dve", lambda e: e.tensor_scalar_mul(st6[:, 20:21], st6[:, 20:21],
                                                0.5), [sb], [sb])
        op("dve", lambda e: e.scalar_tensor_tensor(
            st6[:, 21:22], st6[:, 19:20], 0.25, st6[:, 20:21],
            ALU.mult, ALU.add), [sb], [sb])
        op("act", lambda e: e.activation(st6[:, 22:23], st6[:, 21:22], AF.Ln,
                                         bias=eps_col[:, 0:1]),
           [sb, eps_buf], [sb])
        op("act", lambda e: e.activation(st6[:, 23:24], st6[:, 22:23], AF.Exp,
                                         scale=-0.5), [sb], [sb])
    for tt in range(16):
        xv, xb_ = xts[tt % 2]
        dma("sync", xv, d_x[tt * 128:(tt + 1) * 128, :], [], [xb_], tag="ldx")
        ln_stats(xv[:, 0:512], xv[:, 512:1024], [xb_], eps6_c, B("eps6_c"))
        op("dve", lambda e, xv=xv: e.tensor_scalar(
            xn_s, xv, st6[:, 17:18], st6[:, 23:24], ALU.subtract, ALU.mult),
            [xb_, B("st6")], [B("xn_s")])
        op("pool", lambda e: e.tensor_mul(xn_s, xn_s, lgb),
           [B("xn_s"), B("lgb")], [B("xn_s")])
        op("pool", lambda e: e.tensor_add(xn_s, xn_s, lbb),
           [B("xn_s"), B("lbb")], [B("xn_s")])
        dma("pool", d_xn[tt * 128:(tt + 1) * 128, :], xn_s,
            [B("xn_s")], [B("d_xn")], tag="stxn")

    if "ln1dbg" in debug_dumps:
        t = dumpt("st6", (128, 16))
        dma("sync", t, st6, [B("st6")], [B("dump_st6")])
        t2 = dumpt("xns", (128, 1024))
        dma("sync", t2, xn_s, [B("xn_s")], [B("dump_xns")])
        t3 = dumpt("xtile", (128, 1024))
        dma("sync", t3, xts[15 % 2][0], [xts[15 % 2][1]], [B("dump_xtile")])
    if "xn" in debug_dumps:
        t = dumpt("xn", (S, D), BF16)
        dma("sync", t, d_xn, [B("d_xn")], [B("dump_xn")])
    sfree("lgb"); sfree("lbb"); sfree("xn_s")

    xn_fm = newv("xn_fm", 128, 8 * 2048, BF16, side="hi")
    xn_fmv = xn_fm.rearrange("p (c t) -> p c t", c=8)
    for dc in range(8):
        dma("sync", xn_fmv[:, dc, :], d_xn[:, dc * 128:(dc + 1) * 128],
            [B("d_xn")], [B("xn_fm")], tag="Txn", transpose=True)

    if stop_after == "xn":
        P.finalize()
        lf = {i: c for i, c in enumerate(P.lane_count) if c}
        return nc, P, dumps, lf, stack
    wuplT = newv("wuplT", 128, 8 * 2048, BF16, side="hi")
    wuplTv = wuplT.rearrange("p (c j) -> p c j", c=8)
    for dc in range(8):
        dma("act", wuplTv[:, dc, :], d_wuplb[:, dc * 128:(dc + 1) * 128],
            [B("wuplb")], [B("wuplT")], tag="Twupl", transpose=True)
    if "fmdbg" in debug_dumps:
        t = dumpt("xnfm0", (128, 2048), BF16)
        dma("sync", t, xn_fmv[:, 0, :], [B("xn_fm")], [B("dump_xnfm0")])
        t2 = dumpt("xnfm5", (128, 2048), BF16)
        dma("sync", t2, xn_fmv[:, 5, :], [B("xn_fm")], [B("dump_xnfm5")])
        t3 = dumpt("wuplT0", (128, 2048), BF16)
        dma("sync", t3, wuplTv[:, 0, :], [B("wuplT")], [B("dump_wuplT0")])
        t4 = dumpt("wuplT5", (128, 2048), BF16)
        dma("sync", t4, wuplTv[:, 5, :], [B("wuplT")], [B("dump_wuplT5")])
    xt_fm = newv("xt_fm", 128, 16 * 2048, BF16,
                 bufs_=[f"xt_fm{c}" for c in range(16)])
    xt_fmv = xt_fm.rearrange("p (c t) -> p c t", c=16)
    xtb = [B(f"xt_fm{c}") for c in range(16)]
    for jc in range(16):
        for tf in range(4):
            bk = nextbank()
            for dc in range(8):
                mm(bank(bk), wuplTv[:, dc, jc * 128:(jc + 1) * 128],
                   xn_fmv[:, dc, tf * 512:(tf + 1) * 512], dc == 0, dc == 7,
                   [B("wuplT"), B("xn_fm")], [pbuf[bk]], tag="xt_mm")
            op("act", lambda e, jc=jc, tf=tf, bk=bk: e.activation(
                xt_fmv[:, jc, tf * 512:(tf + 1) * 512], bank(bk), AF.Identity,
                bias=bul_c[:, jc:jc + 1]),
                [pbuf[bk], B("bul_c")], [xtb[jc]], tag="xt_ev")

    if stop_after == "xt":
        P.finalize()
        lf = {i: c for i, c in enumerate(P.lane_count) if c}
        return nc, P, dumps, lf, stack
    if "xtc" in debug_dumps:
        t = dumpt("xtc", (128, 2048), BF16)
        dma("sync", t, xt_fmv[:, 0, :], [xtb[0]], [B("dump_xtc")])
        t2 = dumpt("xtc5", (128, 2048), BF16)
        dma("sync", t2, xt_fmv[:, 5, :], [xtb[5]], [B("dump_xtc5")])
    wuprT = newv("wuprT", 128, 8 * 512, BF16, side="hi")
    wuprTv = wuprT.rearrange("p (c j) -> p c j", c=8)
    for dc in range(8):
        dma("act", wuprTv[:, dc, :], d_wuprb[:, dc * 128:(dc + 1) * 128],
            [B("wuprb")], [B("wuprT")], tag="Twupr", transpose=True)
    rt_t = newv("rt_t", 128, 2 * 512, BF16, bufs_=["rt_t0", "rt_t1"])
    rt_tv = rt_t.rearrange("p (c j) -> p c j", c=2)
    for tt in range(16):
        bk = nextbank()
        mm(bank(bk), ones_rb, bupr_rb, True, False,
           [B("ones_rb"), B("bupr_rb")], [pbuf[bk]], tag="rt_b")
        for dc in range(8):
            mm(bank(bk), xn_fmv[:, dc, tt * 128:(tt + 1) * 128], wuprTv[:, dc, :],
               False, dc == 7, [B("xn_fm"), B("wuprT")], [pbuf[bk]], tag="rt_mm")
        sl = rt_tv[:, tt % 2, :]
        op("act", lambda e, sl=sl, bk=bk: e.activation(sl, bank(bk), AF.Silu),
           [pbuf[bk]], [B(f"rt_t{tt % 2}")], tag="rt_ev")
        dma("sync", d_rt[tt * 128:(tt + 1) * 128, :], sl,
            [B(f"rt_t{tt % 2}")], [B("d_rt")], tag="st_rt")

    sfree("xn_fm"); sfree("wuplT"); sfree("wuprT"); sfree("rt_t")

    xc_fm = newv("xc_fm", 128, 16 * 2048, BF16,
                 bufs_=[f"xc_fm{c}" for c in range(16)])
    xc_fmv = xc_fm.rearrange("p (c t) -> p c t", c=16)
    xcb = [B(f"xc_fm{c}") for c in range(16)]
    for cc in range(16):
        for tf in range(4):
            bk = nextbank()
            mm(bank(bk), band_bf, xt_fmv[:, cc, tf * 512:(tf + 1) * 512],
               True, cc == 0, [B("band_bf"), xtb[cc]], [pbuf[bk]], tag="cv_mm")
            if cc > 0:
                mm(bank(bk), band2_bf, xt_fmv[:, cc - 1, tf * 512:(tf + 1) * 512],
                   False, True, [B("band2_bf"), xtb[cc - 1]], [pbuf[bk]],
                   tag="cv_mm2")
            op("act", lambda e, cc=cc, tf=tf, bk=bk: e.activation(
                xc_fmv[:, cc, tf * 512:(tf + 1) * 512], bank(bk), AF.Silu,
                bias=convb_c[:, 0:1]),
                [pbuf[bk], B("convb_c")], [xcb[cc]], tag="cv_ev")

    if "xcc" in debug_dumps:
        t = dumpt("xcc", (128, 2048), BF16)
        dma("sync", t, xc_fmv[:, 0, :], [xcb[0]], [B("dump_xcc")])
        t2 = dumpt("xcc5", (128, 2048), BF16)
        dma("sync", t2, xc_fmv[:, 5, :], [xcb[5]], [B("dump_xcc5")])
    wvT = newv("wvT", 128, 16 * 512, BF16, side="hi")
    wvTv = wvT.rearrange("p (c j) -> p c j", c=16)
    for pc in range(16):
        dma("act", wvTv[:, pc, :], d_wvb[:, pc * 128:(pc + 1) * 128],
            [B("wvb")], [B("wvT")], tag="Twv", transpose=True)
    v_s = newv("v_s", 128, 2 * 512, BF16, bufs_=["v_s0", "v_s1"])
    v_sv = v_s.rearrange("p (c j) -> p c j", c=2)
    for tt in range(16):
        bk = nextbank()
        mm(bank(bk), ones_rb, bv_rb, True, False,
           [B("ones_rb"), B("bv_rb")], [pbuf[bk]], tag="v_b")
        for pc in range(16):
            mm(bank(bk), xt_fmv[:, pc, tt * 128:(tt + 1) * 128], wvTv[:, pc, :],
               False, pc == 15, [xtb[pc], B("wvT")], [pbuf[bk]], tag="v_mm")
        sl = v_sv[:, tt % 2, :]
        op("act", lambda e, sl=sl, bk=bk: e.activation(sl, bank(bk), AF.Copy),
           [pbuf[bk]], [B(f"v_s{tt % 2}")], tag="v_ev")
        dma("sync", d_v[tt * 128:(tt + 1) * 128, :], sl,
            [B(f"v_s{tt % 2}")], [B("d_v")], tag="st_v")
    sfree("wvT"); sfree("v_s")

    woT = newv("woT", 128, 16 * 512, BF16, side="hi")
    woTv = woT.rearrange("p (c j) -> p c j", c=16)
    for pc in range(16):
        dma("act", woTv[:, pc, :], d_wob[:, pc * 128:(pc + 1) * 128],
            [B("wob")], [B("woT")], tag="Two", transpose=True)
    o_s = newv("o_s", 128, 2 * 512, BF16, bufs_=["o_s0", "o_s1"])
    o_sv = o_s.rearrange("p (c j) -> p c j", c=2)
    for tt in range(16):
        bk = nextbank()
        mm(bank(bk), ones_rb, bo_rb, True, False,
           [B("ones_rb"), B("bo_rb")], [pbuf[bk]], tag="o_b")
        for pc in range(16):
            mm(bank(bk), xt_fmv[:, pc, tt * 128:(tt + 1) * 128], woTv[:, pc, :],
               False, pc == 15, [xtb[pc], B("woT")], [pbuf[bk]], tag="o_mm")
        sl = o_sv[:, tt % 2, :]
        op("act", lambda e, sl=sl, bk=bk: e.activation(sl, bank(bk), AF.Sigmoid),
           [pbuf[bk]], [B(f"o_s{tt % 2}")], tag="o_ev")
        dma("sync", d_o[tt * 128:(tt + 1) * 128, :], sl,
            [B(f"o_s{tt % 2}")], [B("d_o")], tag="st_o")
    sfree("woT"); sfree("xt_fm"); sfree("o_s")

    if stop_after == "conv":
        P.finalize()
        lf = {i: c for i, c in enumerate(P.lane_count) if c}
        return nc, P, dumps, lf, stack
    q8 = newv("q8", 64, 8 * 2048, BF16)
    q8v = q8.rearrange("p (h t) -> p h t", h=8)
    k8 = newv("k8", 64, 8 * 2048, BF16)
    k8v = k8.rearrange("p (h t) -> p h t", h=8)
    wiT = newv("wiT", 128, 16 * 8, BF16, side="hi")
    wiTv = wiT.rearrange("p (c j) -> p c j", c=16)
    wfT = newv("wfT", 128, 16 * 8, BF16, side="hi")
    wfTv = wfT.rearrange("p (c j) -> p c j", c=16)
    wi_sb = newv("wi_sb", 8, 2048, F32, side="hi")
    wf_sb = newv("wf_sb", 8, 2048, F32, side="hi")
    dma("sync", wi_sb, d_wi, [], [B("wi_sb")], tag="ld_wi")
    dma("sync", wf_sb, d_wf, [], [B("wf_sb")], tag="ld_wf")
    for (wsb, wsbb, wTv_, wtb_) in ((wi_sb, B("wi_sb"), wiTv, B("wiT")),
                                    (wf_sb, B("wf_sb"), wfTv, B("wfT"))):
        for pc in range(16):
            bk = nextbank()
            P.stepw("pe", lambda e, wsb=wsb, pc=pc, bk=bk: e.transpose(
                bank(bk)[:, 0:8], wsb[0:8, pc * 128:(pc + 1) * 128],
                c_ident[0:8, 0:8]),
                reads=[wsbb, B("c_ident")], writes=[pbuf[bk]], tag="Twi_pe")
            op("act", lambda e, wTv_=wTv_, pc=pc, bk=bk: e.activation(
                wTv_[:, pc, :], bank(bk)[:, 0:8], AF.Copy),
                [pbuf[bk]], [wtb_], tag="Twi_ev")
    sfree("wi_sb"); sfree("wf_sb")
    a_fm = newv("a_fm", 8, 2048, F32, side="hi")
    f_fm = newv("f_fm", 8, 2048, F32, side="hi")
    m_fm = newv("m_fm", 8, 2048, F32, side="hi")
    F_fm = newv("F_fm", 8, 2048, F32, side="hi")
    for (wTv_, wb_, gfm, gb_, bc, bcb) in (
            (wiTv, B("wiT"), a_fm, B("a_fm"), bi15, B("bi15")),
            (wfTv, B("wfT"), f_fm, B("f_fm"), bf15, B("bf15"))):
        for tf in range(4):
            bk = nextbank()
            for pc in range(16):
                mm(bank(bk)[0:8, :], wTv_[:, pc, :],
                   xc_fmv[:, pc, tf * 512:(tf + 1) * 512], pc == 0, pc == 15,
                   [wb_, xcb[pc]], [pbuf[bk]], tag="g_mm")
            op("act", lambda e, gfm=gfm, tf=tf, bk=bk, bc=bc: e.activation(
                gfm[0:8, tf * 512:(tf + 1) * 512], bank(bk)[0:8, :], AF.Tanh,
                bias=bc[0:8, 0:1], scale=1.0 / 15.0),
                [pbuf[bk], bcb], [gb_], tag="g_ev")
        op("dve", lambda e, gfm=gfm: e.tensor_scalar_mul(gfm, gfm, 15.0),
           [gb_], [gb_], tag="g_x15")
    sfree("wiT"); sfree("wfT")

    if "afm" in debug_dumps:
        t = dumpt("afm", (8, 2048))
        dma("sync", t, a_fm, [B("a_fm")], [B("dump_afm")])
        t2 = dumpt("ffm", (8, 2048))
        dma("sync", t2, f_fm, [B("f_fm")], [B("dump_ffm")])

    if stop_after == "gates":
        P.finalize()
        lf = {i: c for i, c in enumerate(P.lane_count) if c}
        return nc, P, dumps, lf, stack

    if "afm" in debug_dumps:
        t = dumpt("afm", (8, 2048))
        dma("sync", t, a_fm, [B("a_fm")], [B("dump_afm")])
        t2 = dumpt("ffm", (8, 2048))
        dma("sync", t2, f_fm, [B("f_fm")], [B("dump_ffm")])

    if stop_after == "gates":
        P.finalize()
        lf = {i: c for i, c in enumerate(P.lane_count) if c}
        return nc, P, dumps, lf, stack
    for (wsrc, wbuf, dstv, nm, bias_c, bias_buf, scale) in (
            (d_wqb, B("wqb"), q8v, "q8", bq_c, B("bq_c"), 1.0),
            (d_wkb, B("wkb"), k8v, "k8", bk8_c, B("bk8_c"), 0.125)):
        wTa = newv("wTa_" + nm, 128, 8 * 512, BF16)
        wTb = newv("wTb_" + nm, 128, 8 * 512, BF16)
        wTav = wTa.rearrange("p (c j) -> p c j", c=8)
        wTbv = wTb.rearrange("p (c j) -> p c j", c=8)
        def wTv(pc, wTav=wTav, wTbv=wTbv):
            return wTav[:, pc, :] if pc < 8 else wTbv[:, pc - 8, :]
        for pc in range(16):
            dma("act", wTv(pc), wsrc[:, pc * 128:(pc + 1) * 128],
                [wbuf], [B("wT_" + nm)], tag="T" + nm, transpose=True)
        for h in range(8):
            for tf in range(4):
                bk = nextbank()
                for pc in range(16):
                    mm(bank(bk)[0:64, :], wTv(pc)[:, h * 64:(h + 1) * 64],
                       xc_fmv[:, pc, tf * 512:(tf + 1) * 512], pc == 0, pc == 15,
                       [B("wT_" + nm), xcb[pc]], [pbuf[bk]], tag=nm + "_mm")
                op("act", lambda e, h=h, tf=tf, bk=bk, dstv=dstv,
                   bias_c=bias_c, scale=scale: e.activation(
                       dstv[:, h, tf * 512:(tf + 1) * 512], bank(bk)[0:64, :],
                       AF.Identity, bias=bias_c[0:64, h:h + 1], scale=scale),
                   [pbuf[bk], bias_buf], [B(nm)], tag=nm + "_ev")
        sfree("wTa_" + nm); sfree("wTb_" + nm)

    for h in range(8):
        dma("sync", d_k[h * 64:(h + 1) * 64, :], k8v[:, h, :],
            [B("k8")], [B("d_k")], tag="st_k")

    wskTa = newv("wskTa", 128, 8 * 512, BF16)
    wskTb = newv("wskTb", 128, 8 * 512, BF16)
    wskTav = wskTa.rearrange("p (c j) -> p c j", c=8)
    wskTbv = wskTb.rearrange("p (c j) -> p c j", c=8)
    def wskTv(pc):
        return wskTav[:, pc, :] if pc < 8 else wskTbv[:, pc - 8, :]
    for pc in range(16):
        dma("act", wskTv(pc), d_wskipb[:, pc * 128:(pc + 1) * 128],
            [B("wskipb")], [B("wskT")], tag="Twsk", transpose=True)
    sk_s = newv("sk_s", 128, 2 * 512, BF16, bufs_=["sk_s0", "sk_s1"])
    sk_sv = sk_s.rearrange("p (c j) -> p c j", c=2)
    for tt in range(16):
        bk = nextbank()
        mm(bank(bk), ones_rb, bskip_rb, True, False,
           [B("ones_rb"), B("bskip_rb")], [pbuf[bk]], tag="sk_b")
        for pc in range(16):
            mm(bank(bk), xc_fmv[:, pc, tt * 128:(tt + 1) * 128], wskTv(pc),
               False, pc == 15, [xcb[pc], B("wskT")], [pbuf[bk]], tag="sk_mm")
        sl = sk_sv[:, tt % 2, :]
        op("act", lambda e, sl=sl, bk=bk: e.activation(sl, bank(bk), AF.Copy),
           [pbuf[bk]], [B(f"sk_s{tt % 2}")], tag="sk_ev")
        dma("sync", d_skip[tt * 128:(tt + 1) * 128, :], sl,
            [B(f"sk_s{tt % 2}")], [B("d_skip")], tag="st_sk")
    sfree("wskTa"); sfree("wskTb"); sfree("sk_s")

    sfree("xc_fm")
    op("dve", lambda e: e.tensor_tensor_scan(m_fm, f_fm, a_fm, 0.0,
                                             ALU.add, ALU.max),
       [B("a_fm"), B("f_fm")], [B("m_fm")], tag="m_scan")
    for cc in range(16):
        sl = slice(cc * 128, (cc + 1) * 128)
        op("dve", lambda e, sl=sl: e.tensor_tensor_scan(
            F_fm[:, sl], f_fm[:, sl], f_fm[:, sl], 0.0, ALU.add, ALU.bypass),
            [B("f_fm"), B("F_fm")], [B("F_fm")], tag="F_scan")
    u_fm = a_fm
    op("dve", lambda e: e.tensor_sub(u_fm, a_fm, F_fm),
       [B("a_fm"), B("F_fm")], [B("a_fm")], tag="u_ip")
    negw_fm = f_fm
    op("dve", lambda e: e.tensor_sub(negw_fm, F_fm, m_fm),
       [B("F_fm"), B("m_fm"), B("f_fm")], [B("f_fm")], tag="nw_ip")

    m_3 = m_fm.rearrange("p (c t) -> p c t", t=128)
    F_3 = F_fm.rearrange("p (c t) -> p c t", t=128)
    nw_3 = negw_fm.rearrange("p (c t) -> p c t", t=128)
    gsc = newv("gsc", 8, 80, F32, side="hi")
    mp = gsc[0:8, 0:16]; g_t1 = gsc[0:8, 16:32]; g_t2 = gsc[0:8, 32:48]
    phi_fm = gsc[0:8, 48:64]; negwl_fm = gsc[0:8, 64:80]
    op("dve", lambda e: e.memset(mp, 0.0), [B("gsc")], [B("gsc")])
    op("dve", lambda e: e.tensor_copy(mp[:, 1:16], m_3[0:8, 0:15, 127]),
       [B("m_fm"), B("gsc")], [B("gsc")])
    op("dve", lambda e: e.tensor_add(g_t1, mp, F_3[0:8, :, 127]),
       [B("gsc"), B("F_fm")], [B("gsc")])
    op("dve", lambda e: e.tensor_sub(g_t2, g_t1, m_3[0:8, :, 127]),
       [B("gsc"), B("m_fm")], [B("gsc")])
    op("act", lambda e: e.activation(phi_fm, g_t2, AF.Exp),
       [B("gsc")], [B("gsc")], tag="phi")
    op("dve", lambda e: e.tensor_copy(negwl_fm, nw_3[0:8, :, 127]),
       [B("f_fm"), B("gsc")], [B("gsc")])

    dma("sync", d_negw3.rearrange("c h t -> h c t"),
        negw_fm.rearrange("h (c t) -> h c t", c=16),
        [B("f_fm")], [B("d_negw3")], tag="bn_negw")
    dma("sync", d_u3.rearrange("c h t -> h c t"),
        u_fm.rearrange("h (c t) -> h c t", c=16),
        [B("a_fm")], [B("d_u3")], tag="bn_u")
    dma("sync", d_mprev2.rearrange("c h -> h c"), mp,
        [B("gsc")], [B("d_mprev2")], tag="bn_mp", allow_slow_non_contiguous=True)
    dma("sync", d_negwl2.rearrange("c h -> h c"), negwl_fm,
        [B("gsc")], [B("d_negwl2")], tag="bn_nwl", allow_slow_non_contiguous=True)
    dma("sync", d_phi2.rearrange("c h -> h c"), phi_fm,
        [B("gsc")], [B("d_phi2")], tag="bn_phi", allow_slow_non_contiguous=True)

    wcols = newv("wcols", 128, 128, F32, side="hi")
    ucols = newv("ucols", 128, 128, F32, side="hi")
    l2scr = newv("l2scr", 128, 128, F32, side="hi")
    for (dsrc, dsb, outap, outb, nm_) in (
            (d_negw3, B("d_negw3"), wcols, B("wcols"), "wc"),
            (d_u3, B("d_u3"), ucols, B("ucols"), "uc")):
        dma("sync", l2scr, dsrc.rearrange("c h t -> (c h) t"),
            [dsb, B("l2scr")], [B("l2scr")], tag="ld2_" + nm_)
        bk = nextbank()
        P.stepw("pe", lambda e, bk=bk: e.transpose(
            bank(bk)[:, 0:128], l2scr, c_ident),
            reads=[B("l2scr"), B("c_ident")], writes=[pbuf[bk]],
            tag="T2_" + nm_)
        op("act", lambda e, outap=outap, bk=bk: e.activation(
            outap, bank(bk)[:, 0:128], AF.Copy),
            [pbuf[bk]], [outb], tag="T2e_" + nm_)
    sfree("l2scr")
    mprev_r = newv("mprev_r", 1, 128, F32, side="hi")
    dma("sync", mprev_r.rearrange("o (c h) -> o c h", c=16),
        d_mprev2[None, :, :], [B("d_mprev2")], [B("mprev_r")], tag="ld_mpr")
    negwl_r = newv("negwl_r", 1, 128, F32, side="hi")
    dma("sync", negwl_r.rearrange("o (c h) -> o c h", c=16),
        d_negwl2[None, :, :], [B("d_negwl2")], [B("negwl_r")], tag="ld_nwlr")
    phi_r = newv("phi_r", 1, 128, F32, side="hi")
    dma("sync", phi_r.rearrange("o (c h) -> o c h", c=16),
        d_phi2[None, :, :], [B("d_phi2")], [B("phi_r")], tag="ld_phir")

    egcols = newv("egcols", 128, 128, F32)
    eucols = newv("eucols", 128, 128, F32)
    phicols = newv("phicols", 64, 128, F32)
    gscr = newv("gscr", 128, 128, F32, side="hi")
    for (rowap, rowbuf, colsap, colsbuf, outap, outbuf, nm) in (
            (mprev_r, B("mprev_r"), wcols, B("wcols"), egcols, B("egcols"), "eg"),
            (negwl_r, B("negwl_r"), ucols, B("ucols"), eucols, B("eucols"), "eu")):
        bk = nextbank()
        mm(bank(bk)[:, 0:128], ones_r, rowap, True, True,
           [B("ones_r"), rowbuf], [pbuf[bk]], tag="bc_" + nm)
        op("dve", lambda e, colsap=colsap, bk=bk: e.tensor_tensor(
            gscr, bank(bk)[:, 0:128], colsap, ALU.add),
            [pbuf[bk], colsbuf, B("gscr")], [B("gscr")], tag="add_" + nm)
        op("act", lambda e, outap=outap: e.activation(outap, gscr, AF.Exp),
           [B("gscr")], [outbuf], tag="exp_" + nm)
    bk = nextbank()
    mm(bank(bk)[0:64, 0:128], ones_r[0:1, 0:64], phi_r, True, True,
       [B("ones_r"), B("phi_r")], [pbuf[bk]], tag="phicols_mm")
    op("dve", lambda e, bk=bk: e.tensor_copy(phicols, bank(bk)[0:64, 0:128]),
       [pbuf[bk]], [B("phicols")], tag="phicols_ev")

    if "gates" in debug_dumps:
        for nm2, ap2, bf2 in (("m_fm", m_fm, B("m_fm")), ("u_fm", u_fm, B("a_fm")),
                              ("negw_fm", negw_fm, B("f_fm"))):
            t = dumpt(nm2, (8, 2048))
            dma("sync", t, ap2, [bf2], [B("dump_" + nm2)])
        t = dumpt("egcols", (128, 128))
        dma("sync", t, egcols, [B("egcols")], [B("dump_egcols")])
        t = dumpt("eucols", (128, 128))
        dma("sync", t, eucols, [B("eucols")], [B("dump_eucols")])
        t = dumpt("phicols", (128, 64))
        dma("sync", t, phicols, [B("phicols")], [B("dump_phicols")])

    if stop_after == "gatesprep":
        P.finalize()
        lf = {i: c for i, c in enumerate(P.lane_count) if c}
        return nc, P, dumps, lf, stack
    Wt = newv("Wt", 128, 16 * 1024, BF16, bufs_=[f"Wt{c}" for c in range(16)])
    Wtv = Wt.rearrange("p (c t) -> p c t", c=16)
    Wtb = [B(f"Wt{c}") for c in range(16)]
    nrow0 = newv("nrow0", 1, 1024, F32, side="hi")
    nrow1 = newv("nrow1", 1, 1024, F32, side="hi")
    nrows = [(nrow0, B("nrow0")), (nrow1, B("nrow1"))]
    for cc in range(16):
        par = cc % 2
        pD = pa[:, par * 1024:(par + 1) * 1024]
        pb0, pb1 = pbuf[2 * par], pbuf[2 * par + 1]
        nr, nrb = nrows[cc % 2]
        dma("sync", nr.rearrange("o (h t) -> o h t", h=8),
            d_negw3[None, cc, :, :], [B("d_negw3")], [nrb], tag="ld_nrow")
        for half in range(2):
            hs = slice(half * 512, (half + 1) * 512)
            pb = pb0 if half == 0 else pb1
            mm(pD[:, hs], u_fm[:, cc * 128:(cc + 1) * 128], c_ind8[:, hs],
               True, False, [B("a_fm"), B("c_ind8")], [pb], tag="D_u")
            mm(pD[:, hs], ones_r, nr[0:1, hs], False, False,
               [B("ones_r"), nrb], [pb], tag="D_w")
            mm(pD[:, hs], c_ident, c_mask[:, hs], False, True,
               [B("c_ident"), B("c_mask")], [pb], tag="D_m")
        op("act", lambda e, cc=cc, pD=pD: e.activation(Wtv[:, cc, :], pD,
                                                       AF.Exp),
           [pb0, pb1], [Wtb[cc]], tag="W_exp")
    sfree("a_fm"); sfree("f_fm"); sfree("m_fm"); sfree("F_fm")
    sfree("c_ind8"); sfree("c_mask"); sfree("c_par2")
    sfree("nrow0"); sfree("nrow1"); sfree("wcols"); sfree("ucols")
    sfree("mprev_r"); sfree("negwl_r"); sfree("phi_r"); sfree("gscr")
    sfree("gsc")

    if "Wt" in debug_dumps:
        t = dumpt("Wt", (128, 16 * 1024), BF16)
        dma("sync", t, Wt, Wtb, [B("dump_Wt")])

    k_tm = newv("k_tm", 128, 16 * 512, BF16)
    k_tmv = k_tm.rearrange("p (c j) -> p c j", c=16)
    for cc in range(16):
        dma("act", k_tmv[:, cc, :], d_k[:, cc * 128:(cc + 1) * 128],
            [B("d_k")], [B("k_tm")], tag="Tktm", transpose=True)

    if stop_after == "Wt":
        P.finalize()
        lf = {i: c for i, c in enumerate(P.lane_count) if c}
        return nc, P, dumps, lf, stack
    C8 = newv("C8", 64, 8 * 65, F32, bufs_=[f"C{h}" for h in range(8)])
    C8v = C8.rearrange("p (h e) -> p h e", h=8)
    Cb8 = newv("Cb8", 64, 8 * 65, BF16)
    Cb8v = Cb8.rearrange("p (h e) -> p h e", h=8)
    Cbuf = [B(f"C{h}") for h in range(8)]
    Cbbuf = B("Cb8")
    euk = newv("euk", 128, 8 * 64, BF16, bufs_=[f"euk{h}" for h in range(8)])
    eukv = euk.rearrange("p (h e) -> p h e", h=8)
    eukbuf = [B(f"euk{h}") for h in range(8)]
    h_tmA = newv("h_tmA", 128, 8 * 512, F32, bufs_=[f"h{c}" for c in range(8)])
    h_tmB = newv("h_tmB", 128, 8 * 512, F32,
                 bufs_=[f"h{c}" for c in range(8, 16)])
    h_tA = h_tmA.rearrange("p (c j) -> p c j", c=8)
    h_tB = h_tmB.rearrange("p (c j) -> p c j", c=8)
    def h_tmv_f(cc):
        return h_tA[:, cc, :] if cc < 8 else h_tB[:, cc - 8, :]
    htb = [B(f"h{c}") for c in range(16)]
    o_t0 = newv("o_t0", 128, 512, BF16)
    o_t1 = newv("o_t1", 128, 512, BF16)
    otiles = [(o_t0, B("o_t0")), (o_t1, B("o_t1"))]
    sw_s = newv("sw_s", 128, 2 * 512, BF16, bufs_=["sw0", "sw1"])
    sw_sv = sw_s.rearrange("p (g j) -> p g j", g=2)
    swbuf = [B("sw0"), B("sw1")]
    scrI = newv("scrI", 128, 2 * 67, F32, bufs_=["scrI0", "scrI1"])
    scrIv = scrI.rearrange("p (r j) -> p r j", r=2)
    scrIbuf = [B("scrI0"), B("scrI1")]
    scrN = newv("scrN", 128, 8 * 64, F32,
                bufs_=[f"scrN{h}" for h in range(8)])
    scrNv = scrN.rearrange("p (r j) -> p r j", r=8)
    scrNbuf = [B(f"scrN{h}") for h in range(8)]
    scrD = newv("scrD", 128, 24, F32)
    scrDb = B("scrD")
    statc = newv("statc", 128, 36, F32)
    rstdc = statc[:, 16:24]
    bnsc = statc[:, 24:36]
    statc_var = statc[:, 0:16].rearrange("p (h two) -> p h two", two=2)

    for h in range(8):
        op("dve", lambda e, h=h: e.memset(C8v[:, h, 0:64], 0.0),
           [Cbuf[h]], [Cbuf[h]])
        op("dve", lambda e, h=h: e.memset(C8v[:, h, 64:65], 1.0),
           [Cbuf[h]], [Cbuf[h]])

    v_t0 = newv("v_t0", 128, 512, BF16)
    v_t1 = newv("v_t1", 128, 512, BF16)
    vtiles = [(v_t0, B("v_t0")), (v_t1, B("v_t1"))]
    for cc in range(16):
        ov, ovb = otiles[cc % 2]
        dma("sync", ov, d_o[cc * 128:(cc + 1) * 128, :], [B("d_o")], [ovb],
            tag="ld_o")
        vv, vvb = vtiles[cc % 2]
        dma("sync", vv, d_v[cc * 128:(cc + 1) * 128, :], [B("d_v")], [vvb],
            tag="ld_v")
        op("dve", lambda e: e.tensor_copy(Cb8, C8),
           Cbuf, [Cbbuf], tag="cast_C")
        for h in range(8):
            op("pool", lambda e, h=h, cc=cc: e.tensor_scalar(
                eukv[:, h, :], k_tmv[:, cc, h * 64:(h + 1) * 64],
                eucols[:, cc * 8 + h:cc * 8 + h + 1], None, ALU.mult),
                [B("k_tm"), B("eucols")], [eukbuf[h]], tag="euk")
        for g in range(2):
            pSb = 2 + (g + cc * 2) % 2
            for hh in range(4):
                h = g * 4 + hh
                mm(bank(pSb)[:, hh * 128:(hh + 1) * 128],
                   k8v[:, h, cc * 128:(cc + 1) * 128],
                   q8v[:, h, cc * 128:(cc + 1) * 128],
                   True, True, [B("k8"), B("q8")], [pbuf[pSb]], tag="score")
            op("dve", lambda e, g=g, cc=cc, pSb=pSb: e.tensor_tensor(
                sw_sv[:, g, :], bank(pSb), Wtv[:, cc, g * 512:(g + 1) * 512],
                ALU.mult),
                [pbuf[pSb], Wtb[cc]], [swbuf[g]], tag="swmul")
            for hh in range(4):
                h = g * 4 + hh
                r = h % 2
                pW = 4 + (h + cc * 8) % 3
                colq = q8v[:, h, cc * 128:(cc + 1) * 128]
                mm(bank(pW)[:, 0:64], sw_sv[:, g, hh * 128:(hh + 1) * 128],
                   vv[:, h * 64:(h + 1) * 64], True, True,
                   [swbuf[g], vvb], [pbuf[pW]], tag="num")
                mm(bank(pW)[:, 64:128], colq, Cb8v[:, h, 0:64], True, True,
                   [B("q8"), Cbbuf], [pbuf[pW]], tag="inter")
                mm(bank(pW)[:, 128:129], sw_sv[:, g, hh * 128:(hh + 1) * 128],
                   ones_cb, True, True, [swbuf[g], B("ones_cb")], [pbuf[pW]],
                   tag="rowsum")
                mm(bank(pW)[:, 130:131], colq, Cb8v[:, h, 64:65], True, True,
                   [B("q8"), Cbbuf], [pbuf[pW]], tag="nq")
                egc = egcols[:, cc * 8 + h:cc * 8 + h + 1]
                op("act", lambda e, r=r, pW=pW: e.activation(
                    scrIv[:, r, :], bank(pW)[:, 64:131], AF.Copy),
                    [pbuf[pW]], [scrIbuf[r]], tag="evI")
                op("dve", lambda e, h=h, r=r, pW=pW, egc=egc:
                   e.scalar_tensor_tensor(
                       scrNv[:, h, :], scrIv[:, r, 0:64], egc,
                       bank(pW)[:, 0:64], ALU.mult, ALU.add),
                    [scrIbuf[r], B("egcols"), pbuf[pW]], [scrNbuf[h]],
                    tag="numc")
                op("dve", lambda e, h=h, r=r, egc=egc: e.scalar_tensor_tensor(
                    scrD[:, h:h + 1], scrIv[:, r, 66:67], egc,
                    scrIv[:, r, 64:65], ALU.mult, ALU.add),
                    [scrIbuf[r], B("egcols")], [scrDb], tag="denc")
        op("dve", lambda e: e.tensor_scalar_max(scrD[:, 8:16], scrD[:, 0:8],
                                                1.0),
           [scrDb], [scrDb], tag="denm")
        op("dve", lambda e: e.reciprocal(scrD[:, 16:24], scrD[:, 8:16]),
           [scrDb], [scrDb], tag="denr")
        for h in range(8):
            op("dve", lambda e, cc=cc, h=h, ov=ov: e.scalar_tensor_tensor(
                h_tmv_f(cc)[:, h * 64:(h + 1) * 64], scrNv[:, h, :],
                scrD[:, 16 + h:17 + h], ov[:, h * 64:(h + 1) * 64],
                ALU.mult, ALU.mult),
                [scrNbuf[h], scrDb, ovb], [htb[cc]], tag="hwr")
        for h in range(8):
            pC = 7
            op("pool", lambda e, h=h, cc=cc: e.tensor_scalar(
                C8v[:, h, :], C8v[:, h, :],
                phicols[:, cc * 8 + h:cc * 8 + h + 1], None, ALU.mult),
                [B("phicols"), Cbuf[h]], [Cbuf[h]], tag="Cphi")
            mm(bank(pC)[0:64, 0:64], eukv[:, h, :],
               vv[:, h * 64:(h + 1) * 64], True, True,
               [eukbuf[h], vvb], [pbuf[pC]], tag="Cmm")
            mm(bank(pC)[0:64, 64:65], eukv[:, h, :], ones_cb, True, True,
               [eukbuf[h], B("ones_cb")], [pbuf[pC]], tag="Cn")
            op("dve", lambda e, h=h, pC=pC: e.tensor_add(
                C8v[:, h, :], bank(pC)[0:64, 0:65], C8v[:, h, :]),
                [pbuf[pC], Cbuf[h]], [Cbuf[h]], tag="Cadd")

    if "hs" in debug_dumps:
        t = dumpt("hs", (128, 16 * 512))
        dma("sync", t[:, 0:8 * 512], h_tmA, htb[:8], [B("dump_hs")])
        dma("sync", t[:, 8 * 512:], h_tmB, htb[8:], [B("dump_hs")])

    if stop_after == "scan":
        P.finalize()
        lf = {i: c for i, c in enumerate(P.lane_count) if c}
        return nc, P, dumps, lf, stack
    mhgb = bcast_from("mhgb", d_mhg.rearrange("h d -> (h d)")[None, :], 512)
    mhbb = bcast_from("mhbb", d_mhb.rearrange("h d -> (h d)")[None, :], 512)
    rt_l0 = newv("rt_l0", 128, 512, BF16)
    rt_l1 = newv("rt_l1", 128, 512, BF16)
    rtl = [(rt_l0, B("rt_l0")), (rt_l1, B("rt_l1"))]
    sk_l0 = newv("sk_l0", 128, 512, BF16)
    sk_l1 = newv("sk_l1", 128, 512, BF16)
    skl = [(sk_l0, B("sk_l0")), (sk_l1, B("sk_l1"))]
    out_s0 = newv("out_s0", 128, 512, BF16)
    out_s1 = newv("out_s1", 128, 512, BF16)
    outs_ = [(out_s0, B("out_s0")), (out_s1, B("out_s1"))]
    for cc in range(16):
        for h in range(8):
            op("dve", lambda e, cc=cc, h=h: e.bn_stats(
                bnsc[:, 0:6], h_tmv_f(cc)[:, h * 64:(h + 1) * 64]),
                [htb[cc], B("statc")], [B("statc")], tag="bn1")
            op("dve", lambda e, h=h: e.bn_aggr(
                statc[:, 2 * h:2 * h + 2], bnsc[:, 0:6]),
                [B("statc")], [B("statc")], tag="bn2")
        op("act", lambda e: e.activation(rstdc, statc_var[:, :, 1], AF.Ln,
                                         bias=eps6_c[:, 0:1]),
           [B("statc")], [B("statc")], tag="lnv")
        op("act", lambda e: e.activation(rstdc, rstdc, AF.Exp, scale=-0.5),
           [B("statc")], [B("statc")], tag="lnr")
        for h in range(8):
            op("dve", lambda e, cc=cc, h=h: e.tensor_scalar(
                h_tmv_f(cc)[:, h * 64:(h + 1) * 64],
                h_tmv_f(cc)[:, h * 64:(h + 1) * 64],
                statc[:, 2 * h:2 * h + 1], rstdc[:, h:h + 1],
                ALU.subtract, ALU.mult),
                [B("statc"), htb[cc]], [htb[cc]], tag="hnorm")
        op("pool", lambda e, cc=cc: e.tensor_mul(h_tmv_f(cc),
                                                 h_tmv_f(cc), mhgb),
           [B("mhgb"), htb[cc]], [htb[cc]], tag="hg")
        op("pool", lambda e, cc=cc: e.tensor_add(h_tmv_f(cc),
                                                 h_tmv_f(cc), mhbb),
           [B("mhbb"), htb[cc]], [htb[cc]], tag="hb")
        rv, rvb = rtl[cc % 2]
        dma("sync", rv, d_rt[cc * 128:(cc + 1) * 128, :], [B("d_rt")], [rvb],
            tag="ld_rt")
        sv_, svb = skl[cc % 2]
        dma("sync", sv_, d_skip[cc * 128:(cc + 1) * 128, :], [B("d_skip")],
            [svb], tag="ld_sk")
        op("pool", lambda e, cc=cc, sv_=sv_: e.tensor_add(
            h_tmv_f(cc), h_tmv_f(cc), sv_),
            [svb, htb[cc]], [htb[cc]], tag="hskip")
        osl, oslb = outs_[cc % 2]
        op("dve", lambda e, cc=cc, rv=rv, osl=osl: e.tensor_tensor(
            osl, h_tmv_f(cc), rv, ALU.mult),
            [htb[cc], rvb], [oslb], tag="hout")
        dma("sync", d_outp[cc * 128:(cc + 1) * 128, :], osl, [oslb],
            [B("d_outp")], tag="st_out")

    sfree("Wt"); sfree("k_tm"); sfree("k8"); sfree("q8")
    sfree("h_tmA"); sfree("h_tmB"); sfree("euk")
    sfree("mhgb"); sfree("mhbb")
    for nm_ in ("o_t0", "o_t1", "sw_s", "scrI", "scrN", "scrD", "statc",
                "egcols", "eucols", "phicols", "C8", "Cb8",
                "rt_l0", "rt_l1", "sk_l0", "sk_l1", "out_s0", "out_s1", "v_t0", "v_t1",
                "band_bf", "band2_bf", "bupr_rb", "bv_rb", "bo_rb",
                "bskip_rb", "bul_c", "bq_c", "bk_c", "bk8_c", "bi_c",
                "bf_c", "bi15", "bf15"):
        sfree(nm_)

    bdb = bcast_from("bdb", d_bdown[None, :], 1024)
    out_fm = newv("out_fm", 128, 4 * 2048, BF16, side="hi")
    out_fmv = out_fm.rearrange("p (c t) -> p c t", c=4)
    for ic in range(4):
        dma("act", out_fmv[:, ic, :], d_outp[:, ic * 128:(ic + 1) * 128],
            [B("d_outp")], [B("out_fm")], tag="Tout", transpose=True)
    wdT = newv("wdT", 128, 4 * 1024, BF16, side="hi")
    wdTv = wdT.rearrange("p (c j) -> p c j", c=4)
    for ic in range(4):
        dma("act", wdTv[:, ic, :], d_wdownb[:, ic * 128:(ic + 1) * 128],
            [B("wdownb")], [B("wdT")], tag="Twd", transpose=True)

    y_t = newv("y_t", 128, 1024, F32)
    x1_s = newv("x1_s", 128, 1024, F32)
    for tt in range(16):
        xv, xb_ = xts[tt % 2]
        dma("sync", xv, d_x[tt * 128:(tt + 1) * 128, :], [xb_], [xb_],
            tag="ldx2")
        for dh in range(2):
            bk = nextbank()
            for ic in range(4):
                mm(bank(bk), out_fmv[:, ic, tt * 128:(tt + 1) * 128],
                   wdTv[:, ic, dh * 512:(dh + 1) * 512], ic == 0, ic == 3,
                   [B("out_fm"), B("wdT")], [pbuf[bk]], tag="y_mm")
            sl = slice(dh * 512, (dh + 1) * 512)
            op("dve", lambda e, sl=sl, bk=bk, xv=xv: e.scalar_tensor_tensor(
                y_t[:, sl], xv[:, sl], 2.0, bank(bk), ALU.mult, ALU.add),
                [xb_, pbuf[bk], B("y_t")], [B("y_t")], tag="y_ev")
        op("pool", lambda e: e.tensor_add(y_t, y_t, bdb),
           [B("y_t"), B("bdb")], [B("y_t")], tag="y_bd")
        ln_stats(y_t[:, 0:512], y_t[:, 512:1024], [B("y_t")], eps5_c,
                 B("eps5_c"))
        op("dve", lambda e: e.tensor_scalar(
            x1_s, y_t, st6[:, 17:18], st6[:, 23:24], ALU.subtract, ALU.mult),
            [B("y_t"), B("st6")], [B("x1_s")], tag="x1")
        dma("sync", d_x1[tt * 128:(tt + 1) * 128, :], x1_s,
            [B("x1_s")], [B("d_x1")], tag="st_x1")
        dma("pool", d_x1b[tt * 128:(tt + 1) * 128, :], x1_s,
            [B("x1_s")], [B("d_x1b")], tag="st_x1b")

    sfree("out_fm"); sfree("wdT"); sfree("bdb"); sfree("y_t")
    sfree("xt0"); sfree("xt1"); sfree("x1_s")

    if "x1" in debug_dumps:
        t = dumpt("x1", (S, D))
        dma("sync", t, d_x1, [B("d_x1")], [B("dump_x1")])

    if stop_after == "x1":
        P.finalize()
        lf = {i: c for i, c in enumerate(P.lane_count) if c}
        return nc, P, dumps, lf, stack
    gbb = bcast_from("gbb", d_gb[None, :], 32)
    x1_fm = newv("x1_fm", 128, 8 * 2048, BF16)
    x1_fmv = x1_fm.rearrange("p (c t) -> p c t", c=8)
    for dc in range(8):
        dma("act", x1_fmv[:, dc, :], d_x1b[:, dc * 128:(dc + 1) * 128],
            [B("d_x1b")], [B("x1_fm")], tag="Tx1", transpose=True)

    gwsb = newv("gwsb", 128, 8 * 32, BF16)
    gwv = gwsb.rearrange("p (c e) -> p c e", c=8)
    dma("pool", gwv, d_gw.rearrange("(c p) e -> p c e", p=128),
        [], [B("gwsb")], tag="ld_gw")
    cw_tm = newv("cw_tm", 128, 16 * 32, F32,
                 bufs_=[f"cw{t}" for t in range(16)])
    cwv = cw_tm.rearrange("p (c e) -> p c e", c=16)
    cwb = [B(f"cw{t}") for t in range(16)]
    cw_fm = newv("cw_fm", 32, 2048, BF16)
    gt = newv("gt", 128, 192, F32)
    g_p = gt[:, 0:32]; g_mx = gt[:, 32:33]; g_sum = gt[:, 33:34]
    g_gs = gt[:, 34:42]; g_g8 = gt[:, 42:50]; g_keepg = gt[:, 50:58]
    g_rs = gt[:, 58:59]
    g_keepe = gt[:, 64:96]; g_tt1 = gt[:, 96:128]; g_m8 = gt[:, 128:136]
    g_pm = gt[:, 136:168]
    gtb = B("gt")
    for tt in range(16):
        bk = nextbank()
        for dc in range(8):
            mm(bank(bk)[:, 0:32], x1_fmv[:, dc, tt * 128:(tt + 1) * 128],
               gwv[:, dc, :], dc == 0, dc == 7,
               [B("x1_fm"), B("gwsb")], [pbuf[bk]], tag="gate_mm")
        op("dve", lambda e, bk=bk: e.tensor_tensor(
            g_p, bank(bk)[:, 0:32], gbb, ALU.add),
            [pbuf[bk], B("gbb"), gtb], [gtb], tag="gate_b")
        op("dve", lambda e: e.tensor_reduce(g_mx, g_p, mybir.AxisListType.X,
                                            ALU.max),
           [gtb], [gtb], tag="gate_mx")
        op("dve", lambda e: e.tensor_scalar_mul(g_mx, g_mx, -1.0),
           [gtb], [gtb], tag="gate_nmx")
        op("act", lambda e: e.activation(g_pm, g_p, AF.Exp, bias=g_mx,
                                         accum_out=g_sum),
           [gtb], [gtb], tag="gate_exp")
        op("dve", lambda e: e.reciprocal(g_rs, g_sum), [gtb], [gtb],
           tag="gate_rs")
        op("dve", lambda e: e.tensor_scalar_mul(g_p, g_pm, g_rs), [gtb], [gtb],
           tag="gate_p")
        pv = g_p.rearrange("p (g j) -> p g j", g=8)
        first = True
        for a in range(4):
            for b in range(a + 1, 4):
                op("dve", lambda e, a=a, b=b: e.tensor_tensor(
                    g_tt1[:, 0:8], pv[:, :, a], pv[:, :, b], ALU.add),
                    [gtb], [gtb], tag="g2a")
                if first:
                    op("dve", lambda e: e.tensor_copy(g_gs, g_tt1[:, 0:8]),
                       [gtb], [gtb], tag="g2b")
                    first = False
                else:
                    op("dve", lambda e: e.tensor_tensor(
                        g_gs, g_gs, g_tt1[:, 0:8], ALU.max),
                        [gtb], [gtb], tag="g2c")
        op("dve", lambda e: e.max(g_g8, g_gs), [gtb], [gtb], tag="g8")
        op("dve", lambda e: e.tensor_scalar(g_keepg, g_gs, g_g8[:, 3:4], None,
                                            ALU.is_ge),
           [gtb], [gtb], tag="keepg")
        kev = g_keepe.rearrange("p (g j) -> p g j", g=8)
        for jj in range(4):
            op("dve", lambda e, jj=jj: e.tensor_copy(kev[:, :, jj], g_keepg),
               [gtb], [gtb], tag="keepe")
        op("dve", lambda e: e.tensor_scalar(g_tt1, g_keepe, 1e30, 1e30,
                                            ALU.mult, ALU.subtract),
           [gtb], [gtb], tag="gmask1")
        op("dve", lambda e: e.tensor_tensor(g_pm, g_p, g_keepe, ALU.mult),
           [gtb], [gtb], tag="gmask2")
        op("dve", lambda e: e.tensor_tensor(g_pm, g_pm, g_tt1, ALU.add),
           [gtb], [gtb], tag="gmask3")
        op("dve", lambda e: e.max(g_m8, g_pm), [gtb], [gtb], tag="gm8")
        op("dve", lambda e: e.tensor_scalar(g_tt1, g_pm, g_m8[:, 3:4],
                                            None, ALU.is_ge),
           [gtb], [gtb], tag="keep4")
        op("dve", lambda e, tt=tt: e.tensor_tensor(
            cwv[:, tt, :], g_p, g_tt1, ALU.mult),
            [gtb], [cwb[tt]], tag="cwt")
        bk2 = nextbank()
        P.stepw("pe", lambda e, tt=tt, bk2=bk2: e.transpose(
            bank(bk2)[0:32, 0:128], cwv[:, tt, :], c_ident),
            reads=[cwb[tt], B("c_ident")], writes=[pbuf[bk2]], tag="cwT")
        op("dve", lambda e, tt=tt, bk2=bk2: e.tensor_copy(
            cw_fm[0:32, tt * 128:(tt + 1) * 128], bank(bk2)[0:32, 0:128]),
            [pbuf[bk2]], [B("cw_fm")], tag="cwTe")

    dma("sync", d_cw, cw_fm, [B("cw_fm")], [B("d_cw")], tag="st_cw")

    if "cw" in debug_dumps:
        t = dumpt("cw", (128, 512))
        dma("sync", t, cw_tm, cwb, [B("dump_cw")])

    if stop_after == "gate":
        P.finalize()
        lf = {i: c for i, c in enumerate(P.lane_count) if c}
        return nc, P, dumps, lf, stack
    sfree("cw_tm"); sfree("gt"); sfree("gwsb")
    b2sb = newv("b2sb", 32, 1024, BF16)
    dma("pool", b2sb, d_b2, [], [B("b2sb")], tag="ld_b2")
    y_sb = newv("y_sb", 128, 16 * 1024, F32,
                bufs_=[f"y{t}" for t in range(16)])
    y_sbv = y_sb.rearrange("p (c j) -> p c j", c=16)
    ytb = [B(f"y{t}") for t in range(16)]

    w1s = [newv("w1s0", 128, 4096, BF16), newv("w1s1", 128, 4096, BF16)]
    w3s = [newv("w3s0", 128, 4096, BF16), newv("w3s1", 128, 4096, BF16)]
    w2s = [newv("w2s0", 128, 4096, BF16), newv("w2s1", 128, 4096, BF16)]
    w1b = [B("w1s0"), B("w1s1")]; w3b = [B("w3s0"), B("w3s1")]
    w2b = [B("w2s0"), B("w2s1")]
    hh_fm = [newv("hh0", 128, 4 * 2048, BF16), newv("hh1", 128, 4 * 2048, BF16)]
    hh_fmv = [t.rearrange("p (c j) -> p c j", c=4) for t in hh_fm]
    hhb = [B("hh0"), B("hh1")]
    s1_s = newv("s1_s", 128, 2 * 512, F32, bufs_=["s10", "s11"])
    s1_sv = s1_s.rearrange("p (r j) -> p r j", r=2)
    s1b = [B("s10"), B("s11")]
    s4_s = newv("s4_s", 128, 2 * 512, BF16, bufs_=["s40", "s41"])
    s4_sv = s4_s.rearrange("p (r j) -> p r j", r=2)
    s4b = [B("s40"), B("s41")]
    cwrow = [newv("cwrow0", 1, 2048, BF16)] * 2
    cwrb = [B("cwrow0")] * 2
    cwbt = [newv("cwb0", 128, 2048, BF16)] * 2
    cwbb = [B("cwb0")] * 2

    b1T = newv("b1T", 128, 128, F32)
    b3T = newv("b3T", 128, 128, F32)
    sb1T = newv("sb1T", 128, 8, F32)
    sb3T = newv("sb3T", 128, 8, F32)
    bsb = newv("bsb", 32, 512, F32, side="hi")
    for (dsrc, bT, bTn) in ((d_b1, b1T, "b1T"), (d_b3, b3T, "b3T")):
        dma("sync", bsb[0:32, 0:512], dsrc, [B("bsb")], [B("bsb")],
            tag="ld_" + bTn)
        for ic in range(4):
            bk = nextbank()
            P.stepw("pe", lambda e, ic=ic, bk=bk: e.transpose(
                bank(bk)[:, 0:32], bsb[0:32, ic * 128:(ic + 1) * 128],
                c_ident[0:32, 0:32]),
                reads=[B("bsb"), B("c_ident")], writes=[pbuf[bk]], tag="Tb")
            op("act", lambda e, ic=ic, bk=bk, bT=bT: e.activation(
                bT[:, ic * 32:(ic + 1) * 32], bank(bk)[:, 0:32], AF.Copy),
                [pbuf[bk]], [B(bTn)], tag="Tbe")
    for (dsrc, bT, bTn) in ((d_sb1, sb1T, "sb1T"), (d_sb3, sb3T, "sb3T")):
        dma("sync", bsb[0:2, 0:512].rearrange("h c -> h c"),
            dsrc.rearrange("o (h c) -> (o h) c", h=2), [B("bsb")], [B("bsb")],
            tag="ld_" + bTn)
        for ic in range(4):
            bk = nextbank()
            P.stepw("pe", lambda e, ic=ic, bk=bk: e.transpose(
                bank(bk)[:, 0:2], bsb[0:2, ic * 128:(ic + 1) * 128],
                c_ident[0:2, 0:2]),
                reads=[B("bsb"), B("c_ident")], writes=[pbuf[bk]], tag="Tsb")
            op("act", lambda e, ic=ic, bk=bk, bT=bT: e.activation(
                bT[:, ic * 2:ic * 2 + 2], bank(bk)[:, 0:2], AF.Copy),
                [pbuf[bk]], [B(bTn)], tag="Tsbe")
    sfree("bsb")

    def exp_srcs(e):
        if e < 32:
            return (d_w1[e], d_w3[e], d_w2[e])
        hh = e - 32
        return (d_sw1[:, hh * 512:(hh + 1) * 512],
                d_sw3[:, hh * 512:(hh + 1) * 512],
                d_sw2[hh * 512:(hh + 1) * 512, :])

    def load_expert(e):
        sl = e % 2
        s_w1, s_w3, s_w2 = exp_srcs(e)
        dma("pool", w1s[sl].rearrange("p (c i) -> p c i", c=8),
            s_w1.rearrange("(c p) i -> p c i", p=128), [], [w1b[sl]],
            tag=f"ldw1_{e}")
        dma("pool", w3s[sl].rearrange("p (c i) -> p c i", c=8),
            s_w3.rearrange("(c p) i -> p c i", p=128), [], [w3b[sl]],
            tag=f"ldw3_{e}")
        dma("pool", w2s[sl].rearrange("p (c i) -> p c i", c=4),
            s_w2.rearrange("(c p) i -> p c i", p=128), [], [w2b[sl]],
            tag=f"ldw2_{e}")

    def h_phase(e):
        sl = e % 2
        w1v = w1s[sl].rearrange("p (c i) -> p c i", c=8)
        w3v = w3s[sl].rearrange("p (c i) -> p c i", c=8)
        if e < 32:
            dma("sync", cwrow[sl], d_cw[None, e, :], [B("d_cw")], [cwrb[sl]],
                tag=f"ldcw_{e}")
            for tf in range(4):
                bk = nextbank()
                mm(bank(bk), ones_rb, cwrow[sl][0:1, tf * 512:(tf + 1) * 512],
                   True, True, [B("ones_rb"), cwrb[sl]], [pbuf[bk]], tag="cwbm")
                op("act", lambda e_, sl=sl, tf=tf, bk=bk: e_.activation(
                    cwbt[sl][:, tf * 512:(tf + 1) * 512], bank(bk), AF.Copy),
                    [pbuf[bk]], [cwbb[sl]], tag="cwbe")
        for ic in range(4):
            if e < 32:
                b1c = b1T[:, ic * 32 + e:ic * 32 + e + 1]
                b3c = b3T[:, ic * 32 + e:ic * 32 + e + 1]
                b1cb, b3cb = B("b1T"), B("b3T")
            else:
                hx = e - 32
                b1c = sb1T[:, ic * 2 + hx:ic * 2 + hx + 1]
                b3c = sb3T[:, ic * 2 + hx:ic * 2 + hx + 1]
                b1cb, b3cb = B("sb1T"), B("sb3T")
            for tf in range(4):
                r = (ic * 4 + tf) % 2
                b1k = nextbank()
                b3k = nextbank()
                for dc in range(8):
                    mm(bank(b1k), w1v[:, dc, ic * 128:(ic + 1) * 128],
                       x1_fmv[:, dc, tf * 512:(tf + 1) * 512], dc == 0, dc == 7,
                       [B("x1_fm"), w1b[sl]], [pbuf[b1k]], tag="h1mm")
                for dc in range(8):
                    mm(bank(b3k), w3v[:, dc, ic * 128:(ic + 1) * 128],
                       x1_fmv[:, dc, tf * 512:(tf + 1) * 512], dc == 0, dc == 7,
                       [B("x1_fm"), w3b[sl]], [pbuf[b3k]], tag="h3mm")
                op("act", lambda e_, b1k=b1k, r=r, b1c=b1c: e_.activation(
                    s1_sv[:, r, :], bank(b1k), AF.Silu, bias=b1c),
                    [pbuf[b1k], b1cb], [s1b[r]], tag="silu")
                if e < 32:
                    op("dve", lambda e_, b3k=b3k, r=r, b3c=b3c:
                       e_.scalar_tensor_tensor(
                           s4_sv[:, r, :], bank(b3k), b3c, s1_sv[:, r, :],
                           ALU.add, ALU.mult),
                       [pbuf[b3k], s1b[r], b3cb], [s4b[r]], tag="h3c")
                    op("pool", lambda e_, sl=sl, ic=ic, tf=tf, r=r:
                       e_.tensor_mul(
                           hh_fmv[sl][:, ic, tf * 512:(tf + 1) * 512],
                           s4_sv[:, r, :],
                           cwbt[sl][:, tf * 512:(tf + 1) * 512]),
                       [s4b[r], cwbb[sl]], [hhb[sl]], tag="hhat")
                else:
                    op("dve", lambda e_, b3k=b3k, r=r, b3c=b3c, sl=sl, ic=ic,
                       tf=tf: e_.scalar_tensor_tensor(
                           hh_fmv[sl][:, ic, tf * 512:(tf + 1) * 512],
                           bank(b3k), b3c, s1_sv[:, r, :],
                           ALU.add, ALU.mult),
                       [pbuf[b3k], s1b[r], b3cb], [hhb[sl]], tag="hhat_s")

    def w2_phase(es, first_group):
        for tt in range(16):
            for dh in range(2):
                bk = nextbank()
                nmm = len(es) * 4 + (1 if first_group else 0)
                i = 0
                if first_group:
                    mm(bank(bk), cw_fm[0:32, tt * 128:(tt + 1) * 128],
                       b2sb[:, dh * 512:(dh + 1) * 512], True, False,
                       [B("cw_fm"), B("b2sb")], [pbuf[bk]], tag="b2cw")
                    i = 1
                for e in es:
                    sl = e % 2
                    w2v = w2s[sl].rearrange("p (c j) -> p c j", c=4)
                    for ic in range(4):
                        mm(bank(bk), hh_fmv[sl][:, ic, tt * 128:(tt + 1) * 128],
                           w2v[:, ic, dh * 512:(dh + 1) * 512],
                           i == 0, i == nmm - 1,
                           [hhb[sl], w2b[sl]], [pbuf[bk]], tag="w2mm")
                        i += 1
                sl2 = slice(dh * 512, (dh + 1) * 512)
                if first_group:
                    op("act", lambda e_, tt=tt, sl2=sl2, bk=bk: e_.activation(
                        y_sbv[:, tt, sl2], bank(bk), AF.Copy),
                        [pbuf[bk]], [ytb[tt]], tag="yw")
                else:
                    op("dve", lambda e_, tt=tt, sl2=sl2, bk=bk: e_.tensor_tensor(
                        y_sbv[:, tt, sl2], bank(bk), y_sbv[:, tt, sl2], ALU.add),
                        [pbuf[bk], ytb[tt]], [ytb[tt]], tag="yadd")

    load_expert(0)
    load_expert(1)
    h_phase(0)
    h_phase(1)
    first = True
    for g0 in range(0, NEX, 2):
        w2_phase([g0, g0 + 1], first)
        first = False
        if g0 + 2 < NEX:
            load_expert(g0 + 2)
            load_expert(g0 + 3)
            h_phase(g0 + 2)
            h_phase(g0 + 3)

    for nm_ in ("w1s0", "w1s1", "w3s0", "w3s1", "w2s0", "w2s1", "hh0", "hh1",
                "s1_s", "s4_s", "cwrow0", "cwb0",
                "b1T", "b3T", "sb1T", "sb3T",
                "cw_fm", "gbb", "b2sb", "x1_fm"):
        sfree(nm_)
    sb2b = bcast_from("sb2b", d_sb2[None, :], 1024)
    x2_s = newv("x2_s", 128, 1024, F32)
    xf2_0 = newv("xf2_0", 128, 1024, F32)
    xf2_1 = newv("xf2_1", 128, 1024, F32)
    xf2s = [(xf2_0, B("xf2_0")), (xf2_1, B("xf2_1"))]
    out_sids = []
    for tt in range(16):
        xv, xb_ = xf2s[tt % 2]
        dma("sync", xv, d_x1[tt * 128:(tt + 1) * 128, :], [B("d_x1"), xb_],
            [xb_], tag="ld_x1f")
        op("pool", lambda e, tt=tt, xv=xv: e.tensor_add(
            y_sbv[:, tt, :], y_sbv[:, tt, :], xv),
            [xb_, ytb[tt]], [ytb[tt]], tag="fadd1")
        op("pool", lambda e, tt=tt: e.tensor_add(
            y_sbv[:, tt, :], y_sbv[:, tt, :], sb2b),
            [B("sb2b"), ytb[tt]], [ytb[tt]], tag="fadd2")
        ln_stats(y_sbv[:, tt, 0:512], y_sbv[:, tt, 512:1024], [ytb[tt]],
                 eps5_c, B("eps5_c"))
        op("dve", lambda e, tt=tt: e.tensor_scalar(
            x2_s, y_sbv[:, tt, :], st6[:, 17:18], st6[:, 23:24],
            ALU.subtract, ALU.mult),
            [ytb[tt], B("st6")], [B("x2_s")], tag="x2")
        sid = dma("sync", d_final[tt * 128:(tt + 1) * 128, :], x2_s,
                  [B("x2_s")], [B("d_final")], tag="st_x2")
        out_sids.append(sid)

    P.finalize()
    lane_final = {i: c for i, c in enumerate(P.lane_count) if c}
    return nc, P, dumps, lane_final, stack


_CONSTS = _host_consts()


def _weights_dict(kw):
    return dict(
        ln_g=kw["L0_attn_ln_g"], ln_b=kw["L0_attn_ln_b"],
        mh_g=kw["L0_attn_mh_g"], mh_b=kw["L0_attn_mh_b"],
        wupl=kw["L0_attn_W_up_l"], bupl=kw["L0_attn_b_up_l"],
        wupr=kw["L0_attn_W_up_r"], bupr=kw["L0_attn_b_up_r"],
        wdown=kw["L0_attn_W_down"], bdown=kw["L0_attn_b_down"],
        convw=kw["L0_attn_conv_w"],
        convb=np.asarray(kw["L0_attn_conv_b"], np.float32).reshape(1),
        wskip=kw["L0_attn_W_skip"], bskip=kw["L0_attn_b_skip"],
        wi=kw["L0_attn_W_i"],
        bi=np.asarray(kw["L0_attn_b_i"], np.float32).reshape(H, 1),
        wf=kw["L0_attn_W_f"],
        bf=np.asarray(kw["L0_attn_b_f"], np.float32).reshape(H, 1),
        wo=kw["L0_attn_W_o"], bo=kw["L0_attn_b_o"],
        wq=kw["L0_attn_W_q"], bq=kw["L0_attn_b_q"],
        wk=kw["L0_attn_W_k"], bk=kw["L0_attn_b_k"],
        wv=kw["L0_attn_W_v"], bv=kw["L0_attn_b_v"],
        gw=kw["L0_ffn_gw"], gb=kw["L0_ffn_gb"],
        w1=kw["L0_ffn_w1"], b1=kw["L0_ffn_b1"],
        w3=kw["L0_ffn_w3"], b3=kw["L0_ffn_b3"],
        w2=kw["L0_ffn_w2"], b2=kw["L0_ffn_b2"],
        sw1=kw["L0_ffn_sw1"],
        sb1=np.asarray(kw["L0_ffn_sb1"], np.float32).reshape(1, -1),
        sw3=kw["L0_ffn_sw3"],
        sb3=np.asarray(kw["L0_ffn_sb3"], np.float32).reshape(1, -1),
        sw2=kw["L0_ffn_sw2"], sb2=kw["L0_ffn_sb2"],
    )


def _run(kw, core_ids, debug_dumps=(), stop_after=None):
    nc, P, dumps, lane_final, stack = build_program(debug_dumps=debug_dumps,
                                                    stop_after=stop_after)
    with stack:
        with ExitStack() as es2:
            esems = {s: es2.enter_context(nc.semaphore(f"es_{s}"))
                     for s in Prog.STREAMS}
            lsems = [es2.enter_context(nc.semaphore(f"lane{i}"))
                     for i in range(P.n_lanes)]
            with nc.Block() as block:
                @block.tensor
                def _(eng):
                    _emit(P, "pe", eng, esems, lsems)

                @block.vector
                def _(eng):
                    _emit(P, "dve", eng, esems, lsems)

                @block.scalar
                def _(eng):
                    _emit(P, "act", eng, esems, lsems)

                @block.gpsimd
                def _(eng):
                    _emit(P, "pool", eng, esems, lsems)

                @block.sync
                def _(eng):
                    _emit(P, "sync", eng, esems, lsems)
                    for ln, pos in sorted(lane_final.items()):
                        eng.wait_ge(lsems[ln], pos * 16)

            x_full = np.ascontiguousarray(
                np.asarray(kw["L0_attn_x"], dtype=np.float32))
            W = {k: np.ascontiguousarray(np.asarray(v, np.float32))
                 for k, v in _weights_dict(kw).items()}
            in_maps = []
            for c in core_ids:
                m = dict(_CONSTS)
                m["x"] = x_full[c]
                m.update(W)
                in_maps.append(m)
            res = run_bass_kernel_spmd(nc, in_maps, core_ids=list(core_ids))
    return res


def kernel(**kw):
    res = _run(kw, list(range(8)))
    return np.stack([np.asarray(r["x2"], np.float32) for r in res.results])


# version 08
# speedup vs baseline: 1.0458x; 1.0132x over previous
import numpy as np
from contextlib import ExitStack
import concourse.bass as bass
import concourse.mybir as mybir
from concourse.bass_utils import run_bass_kernel_spmd

F32 = mybir.dt.float32
BF16 = mybir.dt.bfloat16
I8 = mybir.dt.int8
AF = mybir.ActivationFunctionType
ALU = mybir.AluOpType

S = 2048
D = 1024
P2 = 2048
HID = 512
H = 8
HD = 64
NE = 32
NEX = 34
INTER = 512
SB_BYTES = 206 * 1024


def _host_consts():
    c = {}
    c["cident"] = np.eye(128, dtype=np.float32)
    ind8 = np.zeros((8, 1024), np.float32)
    for h in range(8):
        ind8[h, h * 128:(h + 1) * 128] = 1.0
    c["cind8"] = ind8
    s = np.arange(128)
    m1 = np.where(s[:, None] <= s[None, :], 0.0, -1e30).astype(np.float32)
    c["cmask"] = np.tile(m1, (1, 8)).astype(np.float32)
    par2 = np.zeros((2, 128), np.float32)
    par2[0, :64] = 1.0
    par2[1, 64:] = 1.0
    c["cpar2"] = par2
    dd = np.arange(128)
    bm = np.zeros((4, 128, 128), np.float32)
    b2m = np.zeros((4, 128, 128), np.float32)
    for m in range(4):
        bm[m] = (dd[:, None] == (dd[None, :] - 3 + m)).astype(np.float32)
        b2m[m] = (dd[:, None] == (dd[None, :] - 3 + m + 128)).astype(np.float32)
    c["cband"] = bm
    c["cband2"] = b2m
    return c


class Buf:
    __slots__ = ("name", "writers", "readers")

    def __init__(self, name):
        self.name = name
        self.writers = []
        self.readers = {}


class Prog:
    STREAMS = ("pe", "dve", "act", "pool", "sync")

    def __init__(self):
        self.steps = []
        self.stream_steps = {s: [] for s in self.STREAMS}
        self.n_lanes = 17
        self.lane_count = [0] * self.n_lanes
        self.lane_rr = 0
        self.lane_rr_sw = 0

    def stepw(self, stream, fn, reads=(), writes=(), dma=False, tag="",
              tpose=False):
        sid = len(self.steps)
        deps = set()
        for b in reads:
            deps.update(b.writers)
        for b in writes:
            deps.update(b.readers.values())
            deps.update(b.writers)
        deps.discard(sid)
        if dma:
            if tpose:
                lane = 16
            elif stream == "pool":
                lane = 12 + self.lane_rr_sw % 4
                self.lane_rr_sw += 1
            else:
                lane = self.lane_rr % 12
                self.lane_rr += 1
            self.lane_count[lane] += 1
            pos = self.lane_count[lane]
        else:
            lane, pos = None, None
        st = dict(sid=sid, stream=stream, fn=fn, deps=sorted(deps), dma=dma,
                  lane=lane, lanepos=pos, tag=tag, marked=False, semval=None)
        self.steps.append(st)
        self.stream_steps[stream].append(sid)
        for b in reads:
            b.readers[stream] = sid
        for b in writes:
            b.writers = [sid]
            b.readers = {}
        return sid

    NEAR = 6

    def finalize(self):
        order = {}
        for s in self.STREAMS:
            for i, sid in enumerate(self.stream_steps[s]):
                order[sid] = i
        for st in self.steps:
            for d in st["deps"]:
                dst = self.steps[d]
                if dst["dma"] or dst["stream"] != st["stream"]:
                    dst["marked"] = True
                elif (st["stream"] in ("dve", "act", "pool")
                      and order[st["sid"]] - order[d] < self.NEAR):
                    dst["marked"] = True
                    st.setdefault("near", []).append(d)
        self.stream_totals = {}
        for s in self.STREAMS:
            v = 0
            for sid in self.stream_steps[s]:
                st = self.steps[sid]
                if st["dma"]:
                    continue
                if st["marked"]:
                    v += 1
                    st["semval"] = v
            self.stream_totals[s] = v
        know = {s: {} for s in self.STREAMS}
        snap = [None] * len(self.steps)
        for st in self.steps:
            s = st["stream"]
            k = know[s]
            waits = []
            near = st.get("near", ())
            for d in st["deps"]:
                dst = self.steps[d]
                if dst["dma"]:
                    key = ("l", dst["lane"])
                    val = dst["lanepos"] * 16
                else:
                    if dst["stream"] == s and d not in near:
                        dk = snap[d]
                        for kk, vv in dk.items():
                            if k.get(kk, 0) < vv:
                                k[kk] = vv
                        continue
                    key = ("e", dst["stream"])
                    val = dst["semval"]
                if k.get(key, 0) >= val:
                    continue
                waits.append((key, val))
                k[key] = val
                dk = snap[d]
                for kk, vv in dk.items():
                    if k.get(kk, 0) < vv:
                        k[kk] = vv
            st["waits"] = waits
            if st["dma"]:
                k2 = dict(k)
                k2[("l", st["lane"])] = st["lanepos"] * 16
                snap[st["sid"]] = k2
            elif st["marked"]:
                k2 = dict(k)
                k2[("e", s)] = st["semval"]
                snap[st["sid"]] = k2
            else:
                snap[st["sid"]] = k


def _emit(prog, stream, eng, esems, lsems):
    for sid in prog.stream_steps[stream]:
        st = prog.steps[sid]
        for (key, val) in st["waits"]:
            if key[0] == "e":
                eng.wait_ge(esems[key[1]], val)
            else:
                eng.wait_ge(lsems[key[1]], val)
        if st["dma"] and st["lanepos"] > 1:
            eng.wait_ge(lsems[st["lane"]], (st["lanepos"] - 1) * 16)
        inst = st["fn"](eng)
        if st["dma"]:
            inst.then_inc(lsems[st["lane"]], 16)
        elif st["marked"]:
            inst.then_inc(esems[stream], 1)


def build_program(debug_dumps=(), stop_after=None):
    nc = bass.Bass("TRN2", target_bir_lowering=False,
                   detect_race_conditions=False)

    def inp(name, shape, dtype=F32):
        return nc.dram_tensor(name, list(shape), dtype,
                              kind="ExternalInput").ap()

    d_x = inp("x", (S, D))
    d_lng = inp("ln_g", (D,)); d_lnb = inp("ln_b", (D,))
    d_mhg = inp("mh_g", (H, HD)); d_mhb = inp("mh_b", (H, HD))
    d_wupl = inp("wupl", (P2, D)); d_bupl = inp("bupl", (P2,))
    d_wupr = inp("wupr", (HID, D)); d_bupr = inp("bupr", (HID,))
    d_wdown = inp("wdown", (D, HID)); d_bdown = inp("bdown", (D,))
    d_convw = inp("convw", (4,)); d_convb = inp("convb", (1,))
    d_wskip = inp("wskip", (HID, P2)); d_bskip = inp("bskip", (HID,))
    d_wi = inp("wi", (H, P2)); d_bi = inp("bi", (H, 1))
    d_wf = inp("wf", (H, P2)); d_bf = inp("bf", (H, 1))
    d_wo = inp("wo", (HID, P2)); d_bo = inp("bo", (HID,))
    d_wq = inp("wq", (HID, P2)); d_bq = inp("bq", (HID,))
    d_wk = inp("wk", (HID, P2)); d_bk = inp("bk", (HID,))
    d_wv = inp("wv", (HID, P2)); d_bv = inp("bv", (HID,))
    d_gw = inp("gw", (D, NE)); d_gb = inp("gb", (NE,))
    d_w1 = inp("w1", (NE, D, INTER)); d_b1 = inp("b1", (NE, INTER))
    d_w3 = inp("w3", (NE, D, INTER)); d_b3 = inp("b3", (NE, INTER))
    d_w2 = inp("w2", (NE, INTER, D)); d_b2 = inp("b2", (NE, D))
    d_sw1 = inp("sw1", (D, 2 * INTER)); d_sb1 = inp("sb1", (1, 2 * INTER))
    d_sw3 = inp("sw3", (D, 2 * INTER)); d_sb3 = inp("sb3", (1, 2 * INTER))
    d_sw2 = inp("sw2", (2 * INTER, D)); d_sb2 = inp("sb2", (D,))
    d_cident = inp("cident", (128, 128))
    d_cind8 = inp("cind8", (8, 1024))
    d_cmask = inp("cmask", (128, 1024))
    d_cpar2 = inp("cpar2", (2, 128))
    d_cband = inp("cband", (4, 128, 128))
    d_cband2 = inp("cband2", (4, 128, 128))

    d_final = nc.dram_tensor("x2", [S, D], F32, kind="ExternalOutput").ap()

    def scratch(name, shape, dtype):
        return nc.dram_tensor(name, list(shape), dtype, kind="Internal").ap()

    d_xn = scratch("s_xn", (S, D), BF16)
    d_wuplb = scratch("s_wuplb", (P2, D), BF16)
    d_wuprb = scratch("s_wuprb", (HID, D), BF16)
    d_wvb = scratch("s_wvb", (HID, P2), BF16)
    d_wob = scratch("s_wob", (HID, P2), BF16)
    d_wqb = scratch("s_wqb", (HID, P2), BF16)
    d_wkb = scratch("s_wkb", (HID, P2), BF16)
    d_wskipb = scratch("s_wskipb", (HID, P2), BF16)
    d_wib = scratch("s_wib", (H, P2), BF16)
    d_wfb = scratch("s_wfb", (H, P2), BF16)
    d_wdownb = scratch("s_wdownb", (D, HID), BF16)
    d_k = scratch("s_k", (HID, S), BF16)
    d_o = scratch("s_o", (S, HID), BF16)
    d_skip = scratch("s_skip", (S, HID), BF16)
    d_v = scratch("s_v", (S, HID), BF16)
    d_rt = scratch("s_rt", (S, HID), BF16)
    d_outp = scratch("s_outp", (S, HID), BF16)
    d_x1 = scratch("s_x1", (S, D), F32)
    d_x1b = scratch("s_x1b", (S, D), BF16)
    d_negw3 = scratch("s_negw3", (16, 8, 128), F32)
    d_u3 = scratch("s_u3", (16, 8, 128), F32)
    d_mprev2 = scratch("s_mprev2", (16, 8), F32)
    d_negwl2 = scratch("s_negwl2", (16, 8), F32)
    d_phi2 = scratch("s_phi2", (16, 8), F32)
    d_cw = scratch("s_cw", (NE, S), BF16)

    dumps = {}
    def dumpt(name, shape, dtype=F32):
        dumps[name] = nc.dram_tensor("dump_" + name, list(shape), dtype,
                                     kind="ExternalOutput").ap()
        return dumps[name]

    P = Prog()
    bufs = {}
    def B(name):
        if name not in bufs:
            bufs[name] = Buf(name)
        return bufs[name]

    stack = ExitStack()
    arena = stack.enter_context(nc.sbuf_tensor("arena", [128, SB_BYTES], I8))
    arena_ap = arena[:, :]
    free_ranges = [(0, SB_BYTES)]
    allocs = {}
    graveyard = []

    def newv(name, parts, elems, dtype, bufs_=(), side="lo"):
        esz = 2 if dtype == BF16 else (1 if dtype == I8 else 4)
        sz = elems * esz
        align = 64
        sz = (sz + align - 1) // align * align
        off = None
        idxs = (range(len(free_ranges)) if side == "lo"
                else range(len(free_ranges) - 1, -1, -1))
        for i in idxs:
            a, b = free_ranges[i]
            if side == "lo":
                a2 = (a + align - 1) // align * align
            else:
                a2 = (b - sz) // align * align
                if a2 < a:
                    continue
            if b - a2 >= sz and a2 >= a:
                off = a2
                rest = []
                if a2 > a:
                    rest.append((a, a2))
                if a2 + sz < b:
                    rest.append((a2 + sz, b))
                free_ranges[i:i + 1] = rest
                break
        if off is None:
            used = sum(v[1] for v in allocs.values())
            lay = sorted((v[0], v[1], k) for k, v in allocs.items())
            raise MemoryError(f"SBUF full for {name} ({sz}B); used={used}; "
                              f"free={free_ranges}; layout={lay}")
        bufnames = [name] + list(bufs_)
        allocs[name] = (off, sz, bufnames)
        old = set()
        for (ga, gb_, sids) in graveyard:
            if not (gb_ <= off or ga >= off + sz):
                old.update(sids)
        if old:
            for bn in bufnames:
                B(bn).writers = sorted(set(B(bn).writers) | old)
        return arena_ap[0:parts, off:off + elems * esz].bitcast(dtype)

    def sfree(name):
        off, sz, bufnames = allocs.pop(name)
        sids = set()
        for bn in bufnames:
            sids.update(B(bn).writers)
            sids.update(B(bn).readers.values())
        graveyard.append((off, off + sz, sorted(sids)))
        free_ranges.append((off, off + sz))
        free_ranges.sort()
        merged = []
        for r in free_ranges:
            if merged and merged[-1][1] >= r[0]:
                merged[-1] = (merged[-1][0], max(merged[-1][1], r[1]))
            else:
                merged.append(list(r))
        free_ranges[:] = [tuple(r) for r in merged]

    pa = stack.enter_context(nc.psum_tensor("pa", [128, 4096], F32))
    def bank(k, cols=512):
        return pa[:, k * 512:k * 512 + cols]
    pbuf = [B(f"psum{k}") for k in range(8)]


    def dma(stream, dst, src, reads, writes, tag="", **kw):
        return P.stepw(stream,
                       (lambda eng, dst=dst, src=src, kw=kw:
                        eng.dma_start(out=dst, in_=src, **kw)),
                       reads=reads, writes=writes, dma=True, tag=tag,
                       tpose=bool(kw.get("transpose")))

    def mm(out, lhsT, rhs, start, stop, reads, writes, tag=""):
        return P.stepw("pe",
                       (lambda eng, out=out, lhsT=lhsT, rhs=rhs, s1=start,
                        s2=stop: eng.matmul(out, lhsT, rhs, start=s1, stop=s2)),
                       reads=reads, writes=writes, tag=tag)

    def op(stream, fn, reads, writes, tag=""):
        return P.stepw(stream, fn, reads=reads, writes=writes, tag=tag)

    rrb = [0]
    def nextbank(lo=0, hi=6):
        rrb[0] += 1
        return lo + (rrb[0] % (hi - lo))

    op("dve", lambda e: e.memset(pa[:, :], 0.0), [], pbuf, tag="psum0")

    for src, dst, nm in ((d_wupl, d_wuplb, "wuplb"), (d_wupr, d_wuprb, "wuprb"),
                         (d_wv, d_wvb, "wvb"), (d_wo, d_wob, "wob"),
                         (d_wq, d_wqb, "wqb"), (d_wk, d_wkb, "wkb"),
                         (d_wskip, d_wskipb, "wskipb"),
                         (d_wdown, d_wdownb, "wdownb")):
        dma("pool", dst, src, [], [B(nm)], tag="cast_" + nm)

    c_ident = newv("c_ident", 128, 128, F32)
    c_par2 = newv("c_par2", 2, 128, F32)
    c_band = newv("c_band", 128, 4 * 128, F32)
    c_band2 = newv("c_band2", 128, 4 * 128, F32)
    ones_r = newv("ones_r", 1, 128, F32)
    ones_rb = newv("ones_rb", 1, 128, BF16)
    ones_cb = newv("ones_cb", 128, 1, BF16)
    dma("sync", c_ident, d_cident, [], [B("c_ident")])
    dma("sync", c_par2, d_cpar2, [], [B("c_par2")])
    cbv = c_band.rearrange("p (m j) -> p m j", m=4)
    cb2v = c_band2.rearrange("p (m j) -> p m j", m=4)
    dma("sync", cbv, d_cband.rearrange("m p j -> p m j"), [], [B("c_band")])
    dma("sync", cb2v, d_cband2.rearrange("m p j -> p m j"), [], [B("c_band2")])
    op("dve", lambda e: e.memset(ones_r, 1.0), [], [B("ones_r")])
    op("dve", lambda e: e.memset(ones_rb, 1.0), [], [B("ones_rb")])
    op("dve", lambda e: e.memset(ones_cb, 1.0), [], [B("ones_cb")])
    eps6_c = newv("eps6_c", 128, 1, F32)
    eps5_c = newv("eps5_c", 128, 1, F32)
    op("dve", lambda e: e.memset(eps6_c, 1e-6), [], [B("eps6_c")])
    op("dve", lambda e: e.memset(eps5_c, 1e-5), [], [B("eps5_c")])

    bul_c = newv("bul_c", 128, 16, F32)
    dma("sync", bul_c, d_bupl.rearrange("(c p) -> p c", p=128), [], [B("bul_c")], allow_slow_non_contiguous=True)
    bq_c = newv("bq_c", 64, 8, F32)
    dma("sync", bq_c, d_bq.rearrange("(c p) -> p c", p=64), [], [B("bq_c")], allow_slow_non_contiguous=True)
    bk_c = newv("bk_c", 64, 8, F32)
    dma("sync", bk_c, d_bk.rearrange("(c p) -> p c", p=64), [], [B("bk_c")], allow_slow_non_contiguous=True)
    bk8_c = newv("bk8_c", 64, 8, F32)
    op("dve", lambda e: e.tensor_scalar_mul(bk8_c, bk_c, 0.125),
       [B("bk_c")], [B("bk8_c")])
    bi_c = newv("bi_c", 8, 1, F32)
    dma("sync", bi_c, d_bi, [], [B("bi_c")])
    bf_c = newv("bf_c", 8, 1, F32)
    dma("sync", bf_c, d_bf, [], [B("bf_c")])
    bi15 = newv("bi15", 8, 1, F32)
    op("dve", lambda e: e.tensor_scalar_mul(bi15, bi_c, 1.0 / 15.0),
       [B("bi_c")], [B("bi15")])
    bf15 = newv("bf15", 8, 1, F32)
    op("dve", lambda e: e.tensor_scalar_mul(bf15, bf_c, 1.0 / 15.0),
       [B("bf_c")], [B("bf15")])

    bupr_rb = newv("bupr_rb", 1, 512, BF16)
    dma("pool", bupr_rb, d_bupr[None, :], [], [B("bupr_rb")])
    bv_rb = newv("bv_rb", 1, 512, BF16)
    dma("pool", bv_rb, d_bv[None, :], [], [B("bv_rb")])
    bo_rb = newv("bo_rb", 1, 512, BF16)
    dma("pool", bo_rb, d_bo[None, :], [], [B("bo_rb")])
    bskip_rb = newv("bskip_rb", 1, 512, BF16)
    dma("pool", bskip_rb, d_bskip[None, :], [], [B("bskip_rb")])

    def bcast_from(name, dram_row, cols):
        rown = "row_" + name
        row_ap = newv(rown, 1, cols, F32, side="hi")
        dma("sync", row_ap, dram_row, [], [B(rown)], tag="ldr_" + name)
        t = newv(name, 128, cols, F32)
        nhalf = (cols + 511) // 512
        for hh in range(nhalf):
            w = min(512, cols - hh * 512)
            bk = nextbank()
            mm(bank(bk)[:, 0:w], ones_r, row_ap[0:1, hh * 512:hh * 512 + w],
               True, True, [B("ones_r"), B(rown)], [pbuf[bk]], tag="bc_" + name)
            op("act",
               (lambda e, dst=t[:, hh * 512:hh * 512 + w], src=bank(bk)[:, 0:w]:
                e.activation(dst, src, AF.Copy)),
               [pbuf[bk]], [B(name)], tag="bce_" + name)
        sfree(rown)
        return t

    lgb = bcast_from("lgb", d_lng[None, :], 1024)
    lbb = bcast_from("lbb", d_lnb[None, :], 1024)
    wcol4 = bcast_from("wcol4", d_convw[None, :], 4)
    convb_c = bcast_from("convb_c", d_convb[None, :], 1)

    bandA = newv("bandA", 128, 128, F32, side="hi")
    bandB = newv("bandB", 128, 128, F32, side="hi")
    band_bf = newv("band_bf", 128, 128, BF16)
    band2_bf = newv("band2_bf", 128, 128, BF16)
    for which, cb, outv, nm in ((0, cbv, band_bf, "band_bf"),
                                (1, cb2v, band2_bf, "band2_bf")):
        src_buf = B("c_band") if which == 0 else B("c_band2")
        op("dve", lambda e, cb=cb: e.tensor_scalar(
            bandA, cb[:, 0, :], wcol4[:, 0:1], None, ALU.mult),
            [src_buf, B("wcol4")], [B("bandA")])
        cur, oth = bandA, bandB
        curb, othb = B("bandA"), B("bandB")
        for m in range(1, 4):
            op("dve", lambda e, cb=cb, m=m, cur=cur, oth=oth:
               e.scalar_tensor_tensor(oth, cb[:, m, :], wcol4[:, m:m + 1], cur,
                                      ALU.mult, ALU.add),
               [src_buf, B("wcol4"), curb], [othb])
            cur, oth = oth, cur
            curb, othb = othb, curb
        op("dve", lambda e, cur=cur, outv=outv: e.tensor_copy(outv, cur),
           [curb], [B(nm)])
    for nm_ in ("bandA", "bandB", "c_band", "c_band2"):
        sfree(nm_)

    if stop_after == "consts":
        P.finalize()
        lf = {i: c for i, c in enumerate(P.lane_count) if c}
        return nc, P, dumps, lf, stack
    xt0 = newv("xt0", 128, 1024, F32)
    xt1 = newv("xt1", 128, 1024, F32)
    xts = [(xt0, B("xt0")), (xt1, B("xt1"))]
    st6 = newv("st6", 128, 48, F32, bufs_=["st6_0", "st6_1"])
    xn_s = newv("xn_s", 128, 1024, F32)

    def ln_stats(xap0, xap1, rd, eps_col, eps_buf, slot=0):
        sb = B(f"st6_{slot}")
        s6 = st6[:, slot * 24:(slot + 1) * 24]
        op("dve", lambda e: e.bn_stats(s6[:, 0:6], xap0), rd + [sb], [sb])
        op("dve", lambda e: e.bn_stats(s6[:, 6:12], xap1), rd + [sb], [sb])
        op("dve", lambda e: e.bn_aggr(s6[:, 12:14], s6[:, 0:6]), [sb], [sb])
        op("dve", lambda e: e.bn_aggr(s6[:, 14:16], s6[:, 6:12]), [sb], [sb])
        op("dve", lambda e: e.tensor_add(s6[:, 16:17], s6[:, 12:13],
                                         s6[:, 14:15]), [sb], [sb])
        op("dve", lambda e: e.tensor_scalar_mul(s6[:, 17:18], s6[:, 16:17],
                                                0.5), [sb], [sb])
        op("dve", lambda e: e.tensor_sub(s6[:, 18:19], s6[:, 12:13],
                                         s6[:, 14:15]), [sb], [sb])
        op("dve", lambda e: e.tensor_mul(s6[:, 19:20], s6[:, 18:19],
                                         s6[:, 18:19]), [sb], [sb])
        op("dve", lambda e: e.tensor_add(s6[:, 20:21], s6[:, 13:14],
                                         s6[:, 15:16]), [sb], [sb])
        op("dve", lambda e: e.tensor_scalar_mul(s6[:, 20:21], s6[:, 20:21],
                                                0.5), [sb], [sb])
        op("dve", lambda e: e.scalar_tensor_tensor(
            s6[:, 21:22], s6[:, 19:20], 0.25, s6[:, 20:21],
            ALU.mult, ALU.add), [sb], [sb])
        op("act", lambda e: e.activation(s6[:, 22:23], s6[:, 21:22], AF.Ln,
                                         bias=eps_col[:, 0:1]),
           [sb, eps_buf], [sb])
        op("act", lambda e: e.activation(s6[:, 23:24], s6[:, 22:23], AF.Exp,
                                         scale=-0.5), [sb], [sb])
        return s6, sb
    for tt in range(16):
        xv, xb_ = xts[tt % 2]
        dma("sync", xv, d_x[tt * 128:(tt + 1) * 128, :], [], [xb_], tag="ldx")
        s6, s6b = ln_stats(xv[:, 0:512], xv[:, 512:1024], [xb_], eps6_c,
                           B("eps6_c"), slot=tt % 2)
        op("dve", lambda e, xv=xv, s6=s6: e.tensor_scalar(
            xn_s, xv, s6[:, 17:18], s6[:, 23:24], ALU.subtract, ALU.mult),
            [xb_, s6b], [B("xn_s")])
        op("pool", lambda e: e.tensor_mul(xn_s, xn_s, lgb),
           [B("xn_s"), B("lgb")], [B("xn_s")])
        op("pool", lambda e: e.tensor_add(xn_s, xn_s, lbb),
           [B("xn_s"), B("lbb")], [B("xn_s")])
        dma("pool", d_xn[tt * 128:(tt + 1) * 128, :], xn_s,
            [B("xn_s")], [B("d_xn")], tag="stxn")

    if "ln1dbg" in debug_dumps:
        t = dumpt("st6", (128, 16))
        dma("sync", t, st6, [B("st6")], [B("dump_st6")])
        t2 = dumpt("xns", (128, 1024))
        dma("sync", t2, xn_s, [B("xn_s")], [B("dump_xns")])
        t3 = dumpt("xtile", (128, 1024))
        dma("sync", t3, xts[15 % 2][0], [xts[15 % 2][1]], [B("dump_xtile")])
    if "xn" in debug_dumps:
        t = dumpt("xn", (S, D), BF16)
        dma("sync", t, d_xn, [B("d_xn")], [B("dump_xn")])
    sfree("lgb"); sfree("lbb"); sfree("xn_s")

    xn_fm = newv("xn_fm", 128, 8 * 2048, BF16, side="hi")
    xn_fmv = xn_fm.rearrange("p (c t) -> p c t", c=8)
    for dc in range(8):
        dma("act", xn_fmv[:, dc, :], d_xn[:, dc * 128:(dc + 1) * 128],
            [B("d_xn")], [B("xn_fm")], tag="Txn", transpose=True)

    if stop_after == "xn":
        P.finalize()
        lf = {i: c for i, c in enumerate(P.lane_count) if c}
        return nc, P, dumps, lf, stack
    wuplT = newv("wuplT", 128, 8 * 2048, BF16, side="hi")
    wuplTv = wuplT.rearrange("p (c j) -> p c j", c=8)
    for dc in range(8):
        dma("act", wuplTv[:, dc, :], d_wuplb[:, dc * 128:(dc + 1) * 128],
            [B("wuplb")], [B("wuplT")], tag="Twupl", transpose=True)
    if "fmdbg" in debug_dumps:
        t = dumpt("xnfm0", (128, 2048), BF16)
        dma("act", t, xn_fmv[:, 0, :], [B("xn_fm")], [B("dump_xnfm0")])
        t2 = dumpt("xnfm5", (128, 2048), BF16)
        dma("sync", t2, xn_fmv[:, 5, :], [B("xn_fm")], [B("dump_xnfm5")])
        t3 = dumpt("wuplT0", (128, 2048), BF16)
        dma("sync", t3, wuplTv[:, 0, :], [B("wuplT")], [B("dump_wuplT0")])
        t4 = dumpt("wuplT5", (128, 2048), BF16)
        dma("sync", t4, wuplTv[:, 5, :], [B("wuplT")], [B("dump_wuplT5")])
    xt_fm = newv("xt_fm", 128, 16 * 2048, BF16,
                 bufs_=[f"xt_fm{c}" for c in range(16)])
    xt_fmv = xt_fm.rearrange("p (c t) -> p c t", c=16)
    xtb = [B(f"xt_fm{c}") for c in range(16)]
    for jc in range(16):
        for tf in range(4):
            bk = nextbank()
            for dc in range(8):
                mm(bank(bk), wuplTv[:, dc, jc * 128:(jc + 1) * 128],
                   xn_fmv[:, dc, tf * 512:(tf + 1) * 512], dc == 0, dc == 7,
                   [B("wuplT"), B("xn_fm")], [pbuf[bk]], tag="xt_mm")
            op("act", lambda e, jc=jc, tf=tf, bk=bk: e.activation(
                xt_fmv[:, jc, tf * 512:(tf + 1) * 512], bank(bk), AF.Identity,
                bias=bul_c[:, jc:jc + 1]),
                [pbuf[bk], B("bul_c")], [xtb[jc]], tag="xt_ev")

    if stop_after == "xt":
        P.finalize()
        lf = {i: c for i, c in enumerate(P.lane_count) if c}
        return nc, P, dumps, lf, stack
    if "xtc" in debug_dumps:
        t = dumpt("xtc", (128, 2048), BF16)
        dma("sync", t, xt_fmv[:, 0, :], [xtb[0]], [B("dump_xtc")])
        t2 = dumpt("xtc5", (128, 2048), BF16)
        dma("sync", t2, xt_fmv[:, 5, :], [xtb[5]], [B("dump_xtc5")])
    wuprT = newv("wuprT", 128, 8 * 512, BF16, side="hi")
    wuprTv = wuprT.rearrange("p (c j) -> p c j", c=8)
    for dc in range(8):
        dma("sync", wuprTv[:, dc, :], d_wuprb[:, dc * 128:(dc + 1) * 128],
            [B("wuprb")], [B("wuprT")], tag="Twupr", transpose=True)
    rt_t = newv("rt_t", 128, 2 * 512, BF16, bufs_=["rt_t0", "rt_t1"])
    rt_tv = rt_t.rearrange("p (c j) -> p c j", c=2)
    for tt in range(16):
        bk = nextbank()
        mm(bank(bk), ones_rb, bupr_rb, True, False,
           [B("ones_rb"), B("bupr_rb")], [pbuf[bk]], tag="rt_b")
        for dc in range(8):
            mm(bank(bk), xn_fmv[:, dc, tt * 128:(tt + 1) * 128], wuprTv[:, dc, :],
               False, dc == 7, [B("xn_fm"), B("wuprT")], [pbuf[bk]], tag="rt_mm")
        sl = rt_tv[:, tt % 2, :]
        op("act", lambda e, sl=sl, bk=bk: e.activation(sl, bank(bk), AF.Silu),
           [pbuf[bk]], [B(f"rt_t{tt % 2}")], tag="rt_ev")
        dma("sync", d_rt[tt * 128:(tt + 1) * 128, :], sl,
            [B(f"rt_t{tt % 2}")], [B("d_rt")], tag="st_rt")

    sfree("xn_fm"); sfree("wuplT"); sfree("wuprT"); sfree("rt_t")

    xc_fm = newv("xc_fm", 128, 16 * 2048, BF16,
                 bufs_=[f"xc_fm{c}" for c in range(16)])
    xc_fmv = xc_fm.rearrange("p (c t) -> p c t", c=16)
    xcb = [B(f"xc_fm{c}") for c in range(16)]
    for cc in range(16):
        for tf in range(4):
            bk = nextbank()
            mm(bank(bk), band_bf, xt_fmv[:, cc, tf * 512:(tf + 1) * 512],
               True, cc == 0, [B("band_bf"), xtb[cc]], [pbuf[bk]], tag="cv_mm")
            if cc > 0:
                mm(bank(bk), band2_bf, xt_fmv[:, cc - 1, tf * 512:(tf + 1) * 512],
                   False, True, [B("band2_bf"), xtb[cc - 1]], [pbuf[bk]],
                   tag="cv_mm2")
            op("act", lambda e, cc=cc, tf=tf, bk=bk: e.activation(
                xc_fmv[:, cc, tf * 512:(tf + 1) * 512], bank(bk), AF.Silu,
                bias=convb_c[:, 0:1]),
                [pbuf[bk], B("convb_c")], [xcb[cc]], tag="cv_ev")

    if "xcc" in debug_dumps:
        t = dumpt("xcc", (128, 2048), BF16)
        dma("act", t, xc_fmv[:, 0, :], [xcb[0]], [B("dump_xcc")])
        t2 = dumpt("xcc5", (128, 2048), BF16)
        dma("sync", t2, xc_fmv[:, 5, :], [xcb[5]], [B("dump_xcc5")])
    wvT = newv("wvT", 128, 16 * 512, BF16, side="hi")
    wvTv = wvT.rearrange("p (c j) -> p c j", c=16)
    for pc in range(16):
        dma("sync", wvTv[:, pc, :], d_wvb[:, pc * 128:(pc + 1) * 128],
            [B("wvb")], [B("wvT")], tag="Twv", transpose=True)
    v_s = newv("v_s", 128, 2 * 512, BF16, bufs_=["v_s0", "v_s1"])
    v_sv = v_s.rearrange("p (c j) -> p c j", c=2)
    for tt in range(16):
        bk = nextbank()
        mm(bank(bk), ones_rb, bv_rb, True, False,
           [B("ones_rb"), B("bv_rb")], [pbuf[bk]], tag="v_b")
        for pc in range(16):
            mm(bank(bk), xt_fmv[:, pc, tt * 128:(tt + 1) * 128], wvTv[:, pc, :],
               False, pc == 15, [xtb[pc], B("wvT")], [pbuf[bk]], tag="v_mm")
        sl = v_sv[:, tt % 2, :]
        op("act", lambda e, sl=sl, bk=bk: e.activation(sl, bank(bk), AF.Copy),
           [pbuf[bk]], [B(f"v_s{tt % 2}")], tag="v_ev")
        dma("sync", d_v[tt * 128:(tt + 1) * 128, :], sl,
            [B(f"v_s{tt % 2}")], [B("d_v")], tag="st_v")
    sfree("wvT"); sfree("v_s")

    woT = newv("woT", 128, 16 * 512, BF16, side="hi")
    woTv = woT.rearrange("p (c j) -> p c j", c=16)
    for pc in range(16):
        dma("act", woTv[:, pc, :], d_wob[:, pc * 128:(pc + 1) * 128],
            [B("wob")], [B("woT")], tag="Two", transpose=True)
    o_s = newv("o_s", 128, 2 * 512, BF16, bufs_=["o_s0", "o_s1"])
    o_sv = o_s.rearrange("p (c j) -> p c j", c=2)
    for tt in range(16):
        bk = nextbank()
        mm(bank(bk), ones_rb, bo_rb, True, False,
           [B("ones_rb"), B("bo_rb")], [pbuf[bk]], tag="o_b")
        for pc in range(16):
            mm(bank(bk), xt_fmv[:, pc, tt * 128:(tt + 1) * 128], woTv[:, pc, :],
               False, pc == 15, [xtb[pc], B("woT")], [pbuf[bk]], tag="o_mm")
        sl = o_sv[:, tt % 2, :]
        op("act", lambda e, sl=sl, bk=bk: e.activation(sl, bank(bk), AF.Sigmoid),
           [pbuf[bk]], [B(f"o_s{tt % 2}")], tag="o_ev")
        dma("sync", d_o[tt * 128:(tt + 1) * 128, :], sl,
            [B(f"o_s{tt % 2}")], [B("d_o")], tag="st_o")
    sfree("woT"); sfree("xt_fm"); sfree("o_s")

    if stop_after == "conv":
        P.finalize()
        lf = {i: c for i, c in enumerate(P.lane_count) if c}
        return nc, P, dumps, lf, stack
    q8 = newv("q8", 64, 8 * 2048, BF16)
    q8v = q8.rearrange("p (h t) -> p h t", h=8)
    k8 = newv("k8", 64, 8 * 2048, BF16)
    k8v = k8.rearrange("p (h t) -> p h t", h=8)
    wiT = newv("wiT", 128, 16 * 8, BF16, side="hi")
    wiTv = wiT.rearrange("p (c j) -> p c j", c=16)
    wfT = newv("wfT", 128, 16 * 8, BF16, side="hi")
    wfTv = wfT.rearrange("p (c j) -> p c j", c=16)
    wi_sb = newv("wi_sb", 8, 2048, F32, side="hi")
    wf_sb = newv("wf_sb", 8, 2048, F32, side="hi")
    dma("sync", wi_sb, d_wi, [], [B("wi_sb")], tag="ld_wi")
    dma("sync", wf_sb, d_wf, [], [B("wf_sb")], tag="ld_wf")
    for (wsb, wsbb, wTv_, wtb_) in ((wi_sb, B("wi_sb"), wiTv, B("wiT")),
                                    (wf_sb, B("wf_sb"), wfTv, B("wfT"))):
        for pc in range(16):
            bk = nextbank()
            P.stepw("pe", lambda e, wsb=wsb, pc=pc, bk=bk: e.transpose(
                bank(bk)[:, 0:8], wsb[0:8, pc * 128:(pc + 1) * 128],
                c_ident[0:8, 0:8]),
                reads=[wsbb, B("c_ident")], writes=[pbuf[bk]], tag="Twi_pe")
            op("act", lambda e, wTv_=wTv_, pc=pc, bk=bk: e.activation(
                wTv_[:, pc, :], bank(bk)[:, 0:8], AF.Copy),
                [pbuf[bk]], [wtb_], tag="Twi_ev")
    sfree("wi_sb"); sfree("wf_sb")
    a_fm = newv("a_fm", 8, 2048, F32, side="hi")
    f_fm = newv("f_fm", 8, 2048, F32, side="hi")
    m_fm = newv("m_fm", 8, 2048, F32, side="hi")
    F_fm = newv("F_fm", 8, 2048, F32, side="hi")
    for (wTv_, wb_, gfm, gb_, bc, bcb) in (
            (wiTv, B("wiT"), a_fm, B("a_fm"), bi15, B("bi15")),
            (wfTv, B("wfT"), f_fm, B("f_fm"), bf15, B("bf15"))):
        for tf in range(4):
            bk = nextbank()
            for pc in range(16):
                mm(bank(bk)[0:8, :], wTv_[:, pc, :],
                   xc_fmv[:, pc, tf * 512:(tf + 1) * 512], pc == 0, pc == 15,
                   [wb_, xcb[pc]], [pbuf[bk]], tag="g_mm")
            op("act", lambda e, gfm=gfm, tf=tf, bk=bk, bc=bc: e.activation(
                gfm[0:8, tf * 512:(tf + 1) * 512], bank(bk)[0:8, :], AF.Tanh,
                bias=bc[0:8, 0:1], scale=1.0 / 15.0),
                [pbuf[bk], bcb], [gb_], tag="g_ev")
        op("dve", lambda e, gfm=gfm: e.tensor_scalar_mul(gfm, gfm, 15.0),
           [gb_], [gb_], tag="g_x15")
    sfree("wiT"); sfree("wfT")

    if "afm" in debug_dumps:
        t = dumpt("afm", (8, 2048))
        dma("sync", t, a_fm, [B("a_fm")], [B("dump_afm")])
        t2 = dumpt("ffm", (8, 2048))
        dma("sync", t2, f_fm, [B("f_fm")], [B("dump_ffm")])

    if stop_after == "gates":
        P.finalize()
        lf = {i: c for i, c in enumerate(P.lane_count) if c}
        return nc, P, dumps, lf, stack

    if "afm" in debug_dumps:
        t = dumpt("afm", (8, 2048))
        dma("sync", t, a_fm, [B("a_fm")], [B("dump_afm")])
        t2 = dumpt("ffm", (8, 2048))
        dma("sync", t2, f_fm, [B("f_fm")], [B("dump_ffm")])

    if stop_after == "gates":
        P.finalize()
        lf = {i: c for i, c in enumerate(P.lane_count) if c}
        return nc, P, dumps, lf, stack
    op("dve", lambda e: e.tensor_tensor_scan(m_fm, f_fm, a_fm, 0.0,
                                             ALU.add, ALU.max),
       [B("a_fm"), B("f_fm")], [B("m_fm")], tag="m_scan")
    for cc in range(16):
        sl = slice(cc * 128, (cc + 1) * 128)
        op("dve", lambda e, sl=sl: e.tensor_tensor_scan(
            F_fm[:, sl], f_fm[:, sl], f_fm[:, sl], 0.0, ALU.add, ALU.bypass),
            [B("f_fm"), B("F_fm")], [B("F_fm")], tag="F_scan")
    u_fm = a_fm
    op("dve", lambda e: e.tensor_sub(u_fm, a_fm, F_fm),
       [B("a_fm"), B("F_fm")], [B("a_fm")], tag="u_ip")
    negw_fm = f_fm
    op("dve", lambda e: e.tensor_sub(negw_fm, F_fm, m_fm),
       [B("F_fm"), B("m_fm"), B("f_fm")], [B("f_fm")], tag="nw_ip")

    m_3 = m_fm.rearrange("p (c t) -> p c t", t=128)
    F_3 = F_fm.rearrange("p (c t) -> p c t", t=128)
    nw_3 = negw_fm.rearrange("p (c t) -> p c t", t=128)
    gsc = newv("gsc", 8, 80, F32, side="hi")
    mp = gsc[0:8, 0:16]; g_t1 = gsc[0:8, 16:32]; g_t2 = gsc[0:8, 32:48]
    phi_fm = gsc[0:8, 48:64]; negwl_fm = gsc[0:8, 64:80]
    op("dve", lambda e: e.memset(mp, 0.0), [B("gsc")], [B("gsc")])
    op("dve", lambda e: e.tensor_copy(mp[:, 1:16], m_3[0:8, 0:15, 127]),
       [B("m_fm"), B("gsc")], [B("gsc")])
    op("dve", lambda e: e.tensor_add(g_t1, mp, F_3[0:8, :, 127]),
       [B("gsc"), B("F_fm")], [B("gsc")])
    op("dve", lambda e: e.tensor_sub(g_t2, g_t1, m_3[0:8, :, 127]),
       [B("gsc"), B("m_fm")], [B("gsc")])
    op("act", lambda e: e.activation(phi_fm, g_t2, AF.Exp),
       [B("gsc")], [B("gsc")], tag="phi")
    op("dve", lambda e: e.tensor_copy(negwl_fm, nw_3[0:8, :, 127]),
       [B("f_fm"), B("gsc")], [B("gsc")])

    dma("act", d_negw3.rearrange("c h t -> h c t"),
        negw_fm.rearrange("h (c t) -> h c t", c=16),
        [B("f_fm")], [B("d_negw3")], tag="bn_negw")
    dma("sync", d_u3.rearrange("c h t -> h c t"),
        u_fm.rearrange("h (c t) -> h c t", c=16),
        [B("a_fm")], [B("d_u3")], tag="bn_u")
    dma("sync", d_mprev2.rearrange("c h -> h c"), mp,
        [B("gsc")], [B("d_mprev2")], tag="bn_mp", allow_slow_non_contiguous=True)
    dma("sync", d_negwl2.rearrange("c h -> h c"), negwl_fm,
        [B("gsc")], [B("d_negwl2")], tag="bn_nwl", allow_slow_non_contiguous=True)
    dma("sync", d_phi2.rearrange("c h -> h c"), phi_fm,
        [B("gsc")], [B("d_phi2")], tag="bn_phi", allow_slow_non_contiguous=True)

    wcols = newv("wcols", 128, 128, F32, side="hi")
    ucols = newv("ucols", 128, 128, F32, side="hi")
    l2scr = newv("l2scr", 128, 128, F32, side="hi")
    for (dsrc, dsb, outap, outb, nm_) in (
            (d_negw3, B("d_negw3"), wcols, B("wcols"), "wc"),
            (d_u3, B("d_u3"), ucols, B("ucols"), "uc")):
        dma("sync", l2scr, dsrc.rearrange("c h t -> (c h) t"),
            [dsb, B("l2scr")], [B("l2scr")], tag="ld2_" + nm_)
        bk = nextbank()
        P.stepw("pe", lambda e, bk=bk: e.transpose(
            bank(bk)[:, 0:128], l2scr, c_ident),
            reads=[B("l2scr"), B("c_ident")], writes=[pbuf[bk]],
            tag="T2_" + nm_)
        op("act", lambda e, outap=outap, bk=bk: e.activation(
            outap, bank(bk)[:, 0:128], AF.Copy),
            [pbuf[bk]], [outb], tag="T2e_" + nm_)
    sfree("l2scr")
    mprev_r = newv("mprev_r", 1, 128, F32, side="hi")
    dma("sync", mprev_r.rearrange("o (c h) -> o c h", c=16),
        d_mprev2[None, :, :], [B("d_mprev2")], [B("mprev_r")], tag="ld_mpr")
    negwl_r = newv("negwl_r", 1, 128, F32, side="hi")
    dma("sync", negwl_r.rearrange("o (c h) -> o c h", c=16),
        d_negwl2[None, :, :], [B("d_negwl2")], [B("negwl_r")], tag="ld_nwlr")
    phi_r = newv("phi_r", 1, 128, F32, side="hi")
    dma("sync", phi_r.rearrange("o (c h) -> o c h", c=16),
        d_phi2[None, :, :], [B("d_phi2")], [B("phi_r")], tag="ld_phir")

    egcols = newv("egcols", 128, 128, F32)
    eucols = newv("eucols", 128, 128, F32)
    phicols = newv("phicols", 64, 128, F32)
    gscr = newv("gscr", 128, 128, F32, side="hi")
    for (rowap, rowbuf, colsap, colsbuf, outap, outbuf, nm) in (
            (mprev_r, B("mprev_r"), wcols, B("wcols"), egcols, B("egcols"), "eg"),
            (negwl_r, B("negwl_r"), ucols, B("ucols"), eucols, B("eucols"), "eu")):
        bk = nextbank()
        mm(bank(bk)[:, 0:128], ones_r, rowap, True, True,
           [B("ones_r"), rowbuf], [pbuf[bk]], tag="bc_" + nm)
        op("dve", lambda e, colsap=colsap, bk=bk: e.tensor_tensor(
            gscr, bank(bk)[:, 0:128], colsap, ALU.add),
            [pbuf[bk], colsbuf, B("gscr")], [B("gscr")], tag="add_" + nm)
        op("act", lambda e, outap=outap: e.activation(outap, gscr, AF.Exp),
           [B("gscr")], [outbuf], tag="exp_" + nm)
    bk = nextbank()
    mm(bank(bk)[0:64, 0:128], ones_r[0:1, 0:64], phi_r, True, True,
       [B("ones_r"), B("phi_r")], [pbuf[bk]], tag="phicols_mm")
    op("dve", lambda e, bk=bk: e.tensor_copy(phicols, bank(bk)[0:64, 0:128]),
       [pbuf[bk]], [B("phicols")], tag="phicols_ev")

    if "gates" in debug_dumps:
        for nm2, ap2, bf2 in (("m_fm", m_fm, B("m_fm")), ("u_fm", u_fm, B("a_fm")),
                              ("negw_fm", negw_fm, B("f_fm"))):
            t = dumpt(nm2, (8, 2048))
            dma("sync", t, ap2, [bf2], [B("dump_" + nm2)])
        t = dumpt("egcols", (128, 128))
        dma("sync", t, egcols, [B("egcols")], [B("dump_egcols")])
        t = dumpt("eucols", (128, 128))
        dma("sync", t, eucols, [B("eucols")], [B("dump_eucols")])
        t = dumpt("phicols", (128, 64))
        dma("sync", t, phicols, [B("phicols")], [B("dump_phicols")])

    if stop_after == "gatesprep":
        P.finalize()
        lf = {i: c for i, c in enumerate(P.lane_count) if c}
        return nc, P, dumps, lf, stack
    for (wsrc, wbuf, dstv, nm, bias_c, bias_buf, scale) in (
            (d_wqb, B("wqb"), q8v, "q8", bq_c, B("bq_c"), 1.0),
            (d_wkb, B("wkb"), k8v, "k8", bk8_c, B("bk8_c"), 0.125)):
        wTa = newv("wTa_" + nm, 128, 8 * 512, BF16, bufs_=["wT_" + nm])
        wTb = newv("wTb_" + nm, 128, 8 * 512, BF16, bufs_=["wT_" + nm])
        wTav = wTa.rearrange("p (c j) -> p c j", c=8)
        wTbv = wTb.rearrange("p (c j) -> p c j", c=8)
        def wTv(pc, wTav=wTav, wTbv=wTbv):
            return wTav[:, pc, :] if pc < 8 else wTbv[:, pc - 8, :]
        for pc in range(16):
            dma("sync", wTv(pc), wsrc[:, pc * 128:(pc + 1) * 128],
                [wbuf], [B("wT_" + nm)], tag="T" + nm, transpose=True)
        if "wtq" in debug_dumps and nm == "q8":
            t_ = dumpt("wtq", (128, 8 * 512), BF16)
            dma("sync", t_, wTa, [B("wT_" + nm)], [B("dump_wtq")])
            t2_ = dumpt("wtq2", (128, 8 * 512), BF16)
            dma("sync", t2_, wTb, [B("wT_" + nm)], [B("dump_wtq2")])
        for h in range(8):
            for tf in range(4):
                bk = nextbank()
                for pc in range(16):
                    mm(bank(bk)[0:64, :], wTv(pc)[:, h * 64:(h + 1) * 64],
                       xc_fmv[:, pc, tf * 512:(tf + 1) * 512], pc == 0, pc == 15,
                       [B("wT_" + nm), xcb[pc]], [pbuf[bk]], tag=nm + "_mm")
                op("act", lambda e, h=h, tf=tf, bk=bk, dstv=dstv,
                   bias_c=bias_c, scale=scale: e.activation(
                       dstv[:, h, tf * 512:(tf + 1) * 512], bank(bk)[0:64, :],
                       AF.Identity, bias=bias_c[0:64, h:h + 1], scale=scale),
                   [pbuf[bk], bias_buf], [B(nm)], tag=nm + "_ev")
        if "qearly" in debug_dumps and nm == "q8":
            t_ = dumpt("qearly", (64, 8 * 2048), BF16)
            dma("sync", t_, q8, [B("q8")], [B("dump_qearly")])
        sfree("wTa_" + nm); sfree("wTb_" + nm)

    if "qk" in debug_dumps:
        t = dumpt("q8d", (64, 8 * 2048), BF16)
        dma("act", t, q8, [B("q8")], [B("dump_q8d")])
        t2 = dumpt("k8d", (64, 8 * 2048), BF16)
        dma("sync", t2, k8, [B("k8")], [B("dump_k8d")])
        t3 = dumpt("vd", (2048, 512), BF16)
        dma("sync", t3, d_v, [B("d_v")], [B("dump_vd")])
        t4 = dumpt("od", (2048, 512), BF16)
        dma("sync", t4, d_o, [B("d_o")], [B("dump_od")])
    for h in range(8):
        dma("sync", d_k[h * 64:(h + 1) * 64, :], k8v[:, h, :],
            [B("k8")], [B("d_k")], tag="st_k")

    if "qk" in debug_dumps:
        tq2 = dumpt("q_postk", (64, 8 * 2048), BF16)
        dma("sync", tq2, q8, [B("q8")], [B("dump_q_postk")])
    wskTa = newv("wskTa", 128, 8 * 512, BF16, bufs_=["wskT"])
    wskTb = newv("wskTb", 128, 8 * 512, BF16, bufs_=["wskT"])
    wskTav = wskTa.rearrange("p (c j) -> p c j", c=8)
    wskTbv = wskTb.rearrange("p (c j) -> p c j", c=8)
    def wskTv(pc):
        return wskTav[:, pc, :] if pc < 8 else wskTbv[:, pc - 8, :]
    for pc in range(16):
        dma("sync", wskTv(pc), d_wskipb[:, pc * 128:(pc + 1) * 128],
            [B("wskipb")], [B("wskT")], tag="Twsk", transpose=True)
    sk_s = newv("sk_s", 128, 2 * 512, BF16, bufs_=["sk_s0", "sk_s1"])
    sk_sv = sk_s.rearrange("p (c j) -> p c j", c=2)
    for tt in range(16):
        bk = nextbank()
        mm(bank(bk), ones_rb, bskip_rb, True, False,
           [B("ones_rb"), B("bskip_rb")], [pbuf[bk]], tag="sk_b")
        for pc in range(16):
            mm(bank(bk), xc_fmv[:, pc, tt * 128:(tt + 1) * 128], wskTv(pc),
               False, pc == 15, [xcb[pc], B("wskT")], [pbuf[bk]], tag="sk_mm")
        sl = sk_sv[:, tt % 2, :]
        op("act", lambda e, sl=sl, bk=bk: e.activation(sl, bank(bk), AF.Copy),
           [pbuf[bk]], [B(f"sk_s{tt % 2}")], tag="sk_ev")
        dma("sync", d_skip[tt * 128:(tt + 1) * 128, :], sl,
            [B(f"sk_s{tt % 2}")], [B("d_skip")], tag="st_sk")
    sfree("wskTa"); sfree("wskTb"); sfree("sk_s")

    sfree("xc_fm")
    if "qk" in debug_dumps:
        tq3 = dumpt("q_postsk", (64, 8 * 2048), BF16)
        dma("sync", tq3, q8, [B("q8")], [B("dump_q_postsk")])
    c_ind8 = newv("c_ind8", 8, 1024, F32, side="hi")
    c_mask = newv("c_mask", 128, 1024, F32, side="hi")
    dma("sync", c_ind8, d_cind8, [], [B("c_ind8")])
    dma("sync", c_mask, d_cmask, [], [B("c_mask")])
    Wt = newv("Wt", 128, 16 * 1024, BF16, bufs_=[f"Wt{c}" for c in range(16)])
    Wtv = Wt.rearrange("p (c t) -> p c t", c=16)
    Wtb = [B(f"Wt{c}") for c in range(16)]
    nrow0 = newv("nrow0", 1, 1024, F32, side="hi")
    nrow1 = newv("nrow1", 1, 1024, F32, side="hi")
    nrows = [(nrow0, B("nrow0")), (nrow1, B("nrow1"))]
    for cc in range(16):
        par = cc % 2
        pD = pa[:, par * 1024:(par + 1) * 1024]
        pb0, pb1 = pbuf[2 * par], pbuf[2 * par + 1]
        nr, nrb = nrows[cc % 2]
        dma("sync", nr.rearrange("o (h t) -> o h t", h=8),
            d_negw3[None, cc, :, :], [B("d_negw3")], [nrb], tag="ld_nrow")
        for half in range(2):
            hs = slice(half * 512, (half + 1) * 512)
            pb = pb0 if half == 0 else pb1
            mm(pD[:, hs], u_fm[:, cc * 128:(cc + 1) * 128], c_ind8[:, hs],
               True, False, [B("a_fm"), B("c_ind8")], [pb], tag="D_u")
            mm(pD[:, hs], ones_r, nr[0:1, hs], False, False,
               [B("ones_r"), nrb], [pb], tag="D_w")
            mm(pD[:, hs], c_ident, c_mask[:, hs], False, True,
               [B("c_ident"), B("c_mask")], [pb], tag="D_m")
        op("act", lambda e, cc=cc, pD=pD: e.activation(Wtv[:, cc, :], pD,
                                                       AF.Exp),
           [pb0, pb1], [Wtb[cc]], tag="W_exp")
    sfree("a_fm"); sfree("f_fm"); sfree("m_fm"); sfree("F_fm")
    sfree("c_ind8"); sfree("c_mask"); sfree("c_par2")
    sfree("nrow0"); sfree("nrow1"); sfree("wcols"); sfree("ucols")
    sfree("mprev_r"); sfree("negwl_r"); sfree("phi_r"); sfree("gscr")
    sfree("gsc")

    if "Wt" in debug_dumps:
        t = dumpt("Wt", (128, 16 * 1024), BF16)
        dma("act", t, Wt, Wtb, [B("dump_Wt")])

    k_tm = newv("k_tm", 128, 16 * 512, BF16)
    k_tmv = k_tm.rearrange("p (c j) -> p c j", c=16)
    for cc in range(16):
        dma("sync", k_tmv[:, cc, :], d_k[:, cc * 128:(cc + 1) * 128],
            [B("d_k")], [B("k_tm")], tag="Tktm", transpose=True)

    if stop_after == "Wt":
        P.finalize()
        lf = {i: c for i, c in enumerate(P.lane_count) if c}
        return nc, P, dumps, lf, stack
    C8 = newv("C8", 64, 8 * 65, F32, bufs_=[f"C{h}" for h in range(8)])
    C8v = C8.rearrange("p (h e) -> p h e", h=8)
    Cb8 = newv("Cb8", 64, 8 * 65, BF16)
    Cb8v = Cb8.rearrange("p (h e) -> p h e", h=8)
    Cbuf = [B(f"C{h}") for h in range(8)]
    Cbbuf = B("Cb8")
    euk = newv("euk", 128, 8 * 64, BF16, bufs_=[f"euk{h}" for h in range(8)])
    eukv = euk.rearrange("p (h e) -> p h e", h=8)
    eukbuf = [B(f"euk{h}") for h in range(8)]
    h_tmA = newv("h_tmA", 128, 8 * 512, F32, bufs_=[f"h{c}" for c in range(8)])
    h_tmB = newv("h_tmB", 128, 8 * 512, F32,
                 bufs_=[f"h{c}" for c in range(8, 16)])
    h_tA = h_tmA.rearrange("p (c j) -> p c j", c=8)
    h_tB = h_tmB.rearrange("p (c j) -> p c j", c=8)
    def h_tmv_f(cc):
        return h_tA[:, cc, :] if cc < 8 else h_tB[:, cc - 8, :]
    htb = [B(f"h{c}") for c in range(16)]
    o_t0 = newv("o_t0", 128, 512, BF16)
    o_t1 = newv("o_t1", 128, 512, BF16)
    otiles = [(o_t0, B("o_t0")), (o_t1, B("o_t1"))]
    sw_s = newv("sw_s", 128, 2 * 512, BF16, bufs_=["sw0", "sw1"])
    sw_sv = sw_s.rearrange("p (g j) -> p g j", g=2)
    swbuf = [B("sw0"), B("sw1")]
    scrI = newv("scrI", 128, 2 * 67, F32, bufs_=["scrI0", "scrI1"])
    scrIv = scrI.rearrange("p (r j) -> p r j", r=2)
    scrIbuf = [B("scrI0"), B("scrI1")]
    scrN = newv("scrN", 128, 8 * 64, F32,
                bufs_=[f"scrN{h}" for h in range(8)])
    scrNv = scrN.rearrange("p (r j) -> p r j", r=8)
    scrNbuf = [B(f"scrN{h}") for h in range(8)]
    scrD = newv("scrD", 128, 24, F32)
    scrDb = B("scrD")
    statc = newv("statc", 128, 36, F32)
    rstdc = statc[:, 16:24]
    bnsc = statc[:, 24:36]
    statc_var = statc[:, 0:16].rearrange("p (h two) -> p h two", two=2)

    for h in range(8):
        op("dve", lambda e, h=h: e.memset(C8v[:, h, 0:64], 0.0),
           [Cbuf[h]], [Cbuf[h]])
        op("dve", lambda e, h=h: e.memset(C8v[:, h, 64:65], 1.0),
           [Cbuf[h]], [Cbuf[h]])

    v_t0 = newv("v_t0", 128, 512, BF16)
    v_t1 = newv("v_t1", 128, 512, BF16)
    vtiles = [(v_t0, B("v_t0")), (v_t1, B("v_t1"))]
    for cc in range(16):
        ov, ovb = otiles[cc % 2]
        dma("sync", ov, d_o[cc * 128:(cc + 1) * 128, :], [B("d_o")], [ovb],
            tag="ld_o")
        vv, vvb = vtiles[cc % 2]
        dma("sync", vv, d_v[cc * 128:(cc + 1) * 128, :], [B("d_v")], [vvb],
            tag="ld_v")
        op("dve", lambda e: e.tensor_copy(Cb8, C8),
           Cbuf, [Cbbuf], tag="cast_C")
        for h in range(8):
            op("pool", lambda e, h=h, cc=cc: e.tensor_scalar(
                eukv[:, h, :], k_tmv[:, cc, h * 64:(h + 1) * 64],
                eucols[:, cc * 8 + h:cc * 8 + h + 1], None, ALU.mult),
                [B("k_tm"), B("eucols")], [eukbuf[h]], tag="euk")
        for g in range(2):
            pSb = 2 + (g + cc * 2) % 2
            for hh in range(4):
                h = g * 4 + hh
                mm(bank(pSb)[:, hh * 128:(hh + 1) * 128],
                   k8v[:, h, cc * 128:(cc + 1) * 128],
                   q8v[:, h, cc * 128:(cc + 1) * 128],
                   True, True, [B("k8"), B("q8")], [pbuf[pSb]], tag="score")
            op("dve", lambda e, g=g, cc=cc, pSb=pSb: e.tensor_tensor(
                sw_sv[:, g, :], bank(pSb), Wtv[:, cc, g * 512:(g + 1) * 512],
                ALU.mult),
                [pbuf[pSb], Wtb[cc]], [swbuf[g]], tag="swmul")
            for hh in range(4):
                h = g * 4 + hh
                r = h % 2
                pW = 4 + (h + cc * 8) % 3
                colq = q8v[:, h, cc * 128:(cc + 1) * 128]
                mm(bank(pW)[:, 0:64], sw_sv[:, g, hh * 128:(hh + 1) * 128],
                   vv[:, h * 64:(h + 1) * 64], True, True,
                   [swbuf[g], vvb], [pbuf[pW]], tag="num")
                mm(bank(pW)[:, 64:128], colq, Cb8v[:, h, 0:64], True, True,
                   [B("q8"), Cbbuf], [pbuf[pW]], tag="inter")
                mm(bank(pW)[:, 128:129], sw_sv[:, g, hh * 128:(hh + 1) * 128],
                   ones_cb, True, True, [swbuf[g], B("ones_cb")], [pbuf[pW]],
                   tag="rowsum")
                mm(bank(pW)[:, 130:131], colq, Cb8v[:, h, 64:65], True, True,
                   [B("q8"), Cbbuf], [pbuf[pW]], tag="nq")
                egc = egcols[:, cc * 8 + h:cc * 8 + h + 1]
                op("act", lambda e, r=r, pW=pW: e.activation(
                    scrIv[:, r, :], bank(pW)[:, 64:131], AF.Copy),
                    [pbuf[pW]], [scrIbuf[r]], tag="evI")
                op("dve", lambda e, h=h, r=r, pW=pW, egc=egc:
                   e.scalar_tensor_tensor(
                       scrNv[:, h, :], scrIv[:, r, 0:64], egc,
                       bank(pW)[:, 0:64], ALU.mult, ALU.add),
                    [scrIbuf[r], B("egcols"), pbuf[pW]], [scrNbuf[h]],
                    tag="numc")
                op("dve", lambda e, h=h, r=r, egc=egc: e.scalar_tensor_tensor(
                    scrD[:, h:h + 1], scrIv[:, r, 66:67], egc,
                    scrIv[:, r, 64:65], ALU.mult, ALU.add),
                    [scrIbuf[r], B("egcols")], [scrDb], tag="denc")
        op("dve", lambda e: e.tensor_scalar_max(scrD[:, 8:16], scrD[:, 0:8],
                                                1.0),
           [scrDb], [scrDb], tag="denm")
        op("dve", lambda e: e.reciprocal(scrD[:, 16:24], scrD[:, 8:16]),
           [scrDb], [scrDb], tag="denr")
        for h in range(8):
            op("dve", lambda e, cc=cc, h=h, ov=ov: e.scalar_tensor_tensor(
                h_tmv_f(cc)[:, h * 64:(h + 1) * 64], scrNv[:, h, :],
                scrD[:, 16 + h:17 + h], ov[:, h * 64:(h + 1) * 64],
                ALU.mult, ALU.mult),
                [scrNbuf[h], scrDb, ovb], [htb[cc]], tag="hwr")
        for h in range(8):
            pC = 7
            op("pool", lambda e, h=h, cc=cc: e.tensor_scalar(
                C8v[:, h, :], C8v[:, h, :],
                phicols[:, cc * 8 + h:cc * 8 + h + 1], None, ALU.mult),
                [B("phicols"), Cbuf[h]], [Cbuf[h]], tag="Cphi")
            mm(bank(pC)[0:64, 0:64], eukv[:, h, :],
               vv[:, h * 64:(h + 1) * 64], True, True,
               [eukbuf[h], vvb], [pbuf[pC]], tag="Cmm")
            mm(bank(pC)[0:64, 64:65], eukv[:, h, :], ones_cb, True, True,
               [eukbuf[h], B("ones_cb")], [pbuf[pC]], tag="Cn")
            op("dve", lambda e, h=h, pC=pC: e.tensor_add(
                C8v[:, h, :], bank(pC)[0:64, 0:65], C8v[:, h, :]),
                [pbuf[pC], Cbuf[h]], [Cbuf[h]], tag="Cadd")

    if "hs" in debug_dumps:
        t = dumpt("hs", (128, 16 * 512))
        dma("sync", t[:, 0:8 * 512], h_tmA, htb[:8], [B("dump_hs")])
        dma("sync", t[:, 8 * 512:], h_tmB, htb[8:], [B("dump_hs")])

    if stop_after == "scan":
        P.finalize()
        lf = {i: c for i, c in enumerate(P.lane_count) if c}
        return nc, P, dumps, lf, stack
    mhgb = bcast_from("mhgb", d_mhg.rearrange("h d -> (h d)")[None, :], 512)
    mhbb = bcast_from("mhbb", d_mhb.rearrange("h d -> (h d)")[None, :], 512)
    rt_l0 = newv("rt_l0", 128, 512, BF16)
    rt_l1 = newv("rt_l1", 128, 512, BF16)
    rtl = [(rt_l0, B("rt_l0")), (rt_l1, B("rt_l1"))]
    sk_l0 = newv("sk_l0", 128, 512, BF16)
    sk_l1 = newv("sk_l1", 128, 512, BF16)
    skl = [(sk_l0, B("sk_l0")), (sk_l1, B("sk_l1"))]
    out_s0 = newv("out_s0", 128, 512, BF16)
    out_s1 = newv("out_s1", 128, 512, BF16)
    outs_ = [(out_s0, B("out_s0")), (out_s1, B("out_s1"))]
    for cc in range(16):
        for h in range(8):
            op("dve", lambda e, cc=cc, h=h: e.bn_stats(
                bnsc[:, 0:6], h_tmv_f(cc)[:, h * 64:(h + 1) * 64]),
                [htb[cc], B("statc")], [B("statc")], tag="bn1")
            op("dve", lambda e, h=h: e.bn_aggr(
                statc[:, 2 * h:2 * h + 2], bnsc[:, 0:6]),
                [B("statc")], [B("statc")], tag="bn2")
        op("act", lambda e: e.activation(rstdc, statc_var[:, :, 1], AF.Ln,
                                         bias=eps6_c[:, 0:1]),
           [B("statc")], [B("statc")], tag="lnv")
        op("act", lambda e: e.activation(rstdc, rstdc, AF.Exp, scale=-0.5),
           [B("statc")], [B("statc")], tag="lnr")
        for h in range(8):
            op("dve", lambda e, cc=cc, h=h: e.tensor_scalar(
                h_tmv_f(cc)[:, h * 64:(h + 1) * 64],
                h_tmv_f(cc)[:, h * 64:(h + 1) * 64],
                statc[:, 2 * h:2 * h + 1], rstdc[:, h:h + 1],
                ALU.subtract, ALU.mult),
                [B("statc"), htb[cc]], [htb[cc]], tag="hnorm")
        op("pool", lambda e, cc=cc: e.tensor_mul(h_tmv_f(cc),
                                                 h_tmv_f(cc), mhgb),
           [B("mhgb"), htb[cc]], [htb[cc]], tag="hg")
        op("pool", lambda e, cc=cc: e.tensor_add(h_tmv_f(cc),
                                                 h_tmv_f(cc), mhbb),
           [B("mhbb"), htb[cc]], [htb[cc]], tag="hb")
        rv, rvb = rtl[cc % 2]
        dma("sync", rv, d_rt[cc * 128:(cc + 1) * 128, :], [B("d_rt")], [rvb],
            tag="ld_rt")
        sv_, svb = skl[cc % 2]
        dma("sync", sv_, d_skip[cc * 128:(cc + 1) * 128, :], [B("d_skip")],
            [svb], tag="ld_sk")
        op("pool", lambda e, cc=cc, sv_=sv_: e.tensor_add(
            h_tmv_f(cc), h_tmv_f(cc), sv_),
            [svb, htb[cc]], [htb[cc]], tag="hskip")
        osl, oslb = outs_[cc % 2]
        op("dve", lambda e, cc=cc, rv=rv, osl=osl: e.tensor_tensor(
            osl, h_tmv_f(cc), rv, ALU.mult),
            [htb[cc], rvb], [oslb], tag="hout")
        dma("sync", d_outp[cc * 128:(cc + 1) * 128, :], osl, [oslb],
            [B("d_outp")], tag="st_out")

    sfree("Wt"); sfree("k_tm"); sfree("k8"); sfree("q8")
    sfree("h_tmA"); sfree("h_tmB"); sfree("euk")
    sfree("mhgb"); sfree("mhbb")
    for nm_ in ("o_t0", "o_t1", "sw_s", "scrI", "scrN", "scrD", "statc",
                "egcols", "eucols", "phicols", "C8", "Cb8",
                "rt_l0", "rt_l1", "sk_l0", "sk_l1", "out_s0", "out_s1", "v_t0", "v_t1",
                "band_bf", "band2_bf", "bupr_rb", "bv_rb", "bo_rb",
                "bskip_rb", "bul_c", "bq_c", "bk_c", "bk8_c", "bi_c",
                "bf_c", "bi15", "bf15"):
        sfree(nm_)

    bdb = bcast_from("bdb", d_bdown[None, :], 1024)
    out_fm = newv("out_fm", 128, 4 * 2048, BF16, side="hi")
    out_fmv = out_fm.rearrange("p (c t) -> p c t", c=4)
    for ic in range(4):
        dma("act", out_fmv[:, ic, :], d_outp[:, ic * 128:(ic + 1) * 128],
            [B("d_outp")], [B("out_fm")], tag="Tout", transpose=True)
    wdT = newv("wdT", 128, 4 * 1024, BF16, side="hi")
    wdTv = wdT.rearrange("p (c j) -> p c j", c=4)
    for ic in range(4):
        dma("act", wdTv[:, ic, :], d_wdownb[:, ic * 128:(ic + 1) * 128],
            [B("wdownb")], [B("wdT")], tag="Twd", transpose=True)

    y_t = newv("y_t", 128, 1024, F32)
    x1_s = newv("x1_s", 128, 1024, F32)
    for tt in range(16):
        xv, xb_ = xts[tt % 2]
        dma("sync", xv, d_x[tt * 128:(tt + 1) * 128, :], [xb_], [xb_],
            tag="ldx2")
        for dh in range(2):
            bk = nextbank()
            for ic in range(4):
                mm(bank(bk), out_fmv[:, ic, tt * 128:(tt + 1) * 128],
                   wdTv[:, ic, dh * 512:(dh + 1) * 512], ic == 0, ic == 3,
                   [B("out_fm"), B("wdT")], [pbuf[bk]], tag="y_mm")
            sl = slice(dh * 512, (dh + 1) * 512)
            op("dve", lambda e, sl=sl, bk=bk, xv=xv: e.scalar_tensor_tensor(
                y_t[:, sl], xv[:, sl], 2.0, bank(bk), ALU.mult, ALU.add),
                [xb_, pbuf[bk], B("y_t")], [B("y_t")], tag="y_ev")
        op("pool", lambda e: e.tensor_add(y_t, y_t, bdb),
           [B("y_t"), B("bdb")], [B("y_t")], tag="y_bd")
        s6, s6b = ln_stats(y_t[:, 0:512], y_t[:, 512:1024], [B("y_t")],
                           eps5_c, B("eps5_c"), slot=tt % 2)
        op("dve", lambda e, s6=s6: e.tensor_scalar(
            x1_s, y_t, s6[:, 17:18], s6[:, 23:24], ALU.subtract, ALU.mult),
            [B("y_t"), s6b], [B("x1_s")], tag="x1")
        dma("sync", d_x1[tt * 128:(tt + 1) * 128, :], x1_s,
            [B("x1_s")], [B("d_x1")], tag="st_x1")
        dma("pool", d_x1b[tt * 128:(tt + 1) * 128, :], x1_s,
            [B("x1_s")], [B("d_x1b")], tag="st_x1b")

    sfree("out_fm"); sfree("wdT"); sfree("bdb"); sfree("y_t")
    sfree("xt0"); sfree("xt1"); sfree("x1_s")

    if "x1" in debug_dumps:
        t = dumpt("x1", (S, D))
        dma("act", t, d_x1, [B("d_x1")], [B("dump_x1")])

    if stop_after == "x1":
        P.finalize()
        lf = {i: c for i, c in enumerate(P.lane_count) if c}
        return nc, P, dumps, lf, stack
    gbb = bcast_from("gbb", d_gb[None, :], 32)
    x1_fm = newv("x1_fm", 128, 8 * 2048, BF16)
    x1_fmv = x1_fm.rearrange("p (c t) -> p c t", c=8)
    for dc in range(8):
        dma("sync", x1_fmv[:, dc, :], d_x1b[:, dc * 128:(dc + 1) * 128],
            [B("d_x1b")], [B("x1_fm")], tag="Tx1", transpose=True)

    gwsb = newv("gwsb", 128, 8 * 32, BF16)
    gwv = gwsb.rearrange("p (c e) -> p c e", c=8)
    dma("pool", gwv, d_gw.rearrange("(c p) e -> p c e", p=128),
        [], [B("gwsb")], tag="ld_gw")
    cw_tm = newv("cw_tm", 128, 16 * 32, F32,
                 bufs_=[f"cw{t}" for t in range(16)])
    cwv = cw_tm.rearrange("p (c e) -> p c e", c=16)
    cwb = [B(f"cw{t}") for t in range(16)]
    cw_fm = newv("cw_fm", 32, 2048, BF16)
    gt = newv("gt", 128, 192, F32)
    g_p = gt[:, 0:32]; g_mx = gt[:, 32:33]; g_sum = gt[:, 33:34]
    g_gs = gt[:, 34:42]; g_g8 = gt[:, 42:50]; g_keepg = gt[:, 50:58]
    g_rs = gt[:, 58:59]
    g_keepe = gt[:, 64:96]; g_tt1 = gt[:, 96:128]; g_m8 = gt[:, 128:136]
    g_pm = gt[:, 136:168]
    gtb = B("gt")
    for tt in range(16):
        bk = nextbank()
        for dc in range(8):
            mm(bank(bk)[:, 0:32], x1_fmv[:, dc, tt * 128:(tt + 1) * 128],
               gwv[:, dc, :], dc == 0, dc == 7,
               [B("x1_fm"), B("gwsb")], [pbuf[bk]], tag="gate_mm")
        op("dve", lambda e, bk=bk: e.tensor_tensor(
            g_p, bank(bk)[:, 0:32], gbb, ALU.add),
            [pbuf[bk], B("gbb"), gtb], [gtb], tag="gate_b")
        op("dve", lambda e: e.tensor_reduce(g_mx, g_p, mybir.AxisListType.X,
                                            ALU.max),
           [gtb], [gtb], tag="gate_mx")
        op("dve", lambda e: e.tensor_scalar_mul(g_mx, g_mx, -1.0),
           [gtb], [gtb], tag="gate_nmx")
        op("act", lambda e: e.activation(g_pm, g_p, AF.Exp, bias=g_mx,
                                         accum_out=g_sum),
           [gtb], [gtb], tag="gate_exp")
        op("dve", lambda e: e.reciprocal(g_rs, g_sum), [gtb], [gtb],
           tag="gate_rs")
        op("dve", lambda e: e.tensor_scalar_mul(g_p, g_pm, g_rs), [gtb], [gtb],
           tag="gate_p")
        pv = g_p.rearrange("p (g j) -> p g j", g=8)
        first = True
        for a in range(4):
            for b in range(a + 1, 4):
                op("dve", lambda e, a=a, b=b: e.tensor_tensor(
                    g_tt1[:, 0:8], pv[:, :, a], pv[:, :, b], ALU.add),
                    [gtb], [gtb], tag="g2a")
                if first:
                    op("dve", lambda e: e.tensor_copy(g_gs, g_tt1[:, 0:8]),
                       [gtb], [gtb], tag="g2b")
                    first = False
                else:
                    op("dve", lambda e: e.tensor_tensor(
                        g_gs, g_gs, g_tt1[:, 0:8], ALU.max),
                        [gtb], [gtb], tag="g2c")
        op("dve", lambda e: e.max(g_g8, g_gs), [gtb], [gtb], tag="g8")
        op("dve", lambda e: e.tensor_scalar(g_keepg, g_gs, g_g8[:, 3:4], None,
                                            ALU.is_ge),
           [gtb], [gtb], tag="keepg")
        kev = g_keepe.rearrange("p (g j) -> p g j", g=8)
        for jj in range(4):
            op("dve", lambda e, jj=jj: e.tensor_copy(kev[:, :, jj], g_keepg),
               [gtb], [gtb], tag="keepe")
        op("dve", lambda e: e.tensor_scalar(g_tt1, g_keepe, 1e30, 1e30,
                                            ALU.mult, ALU.subtract),
           [gtb], [gtb], tag="gmask1")
        op("dve", lambda e: e.tensor_tensor(g_pm, g_p, g_keepe, ALU.mult),
           [gtb], [gtb], tag="gmask2")
        op("dve", lambda e: e.tensor_tensor(g_pm, g_pm, g_tt1, ALU.add),
           [gtb], [gtb], tag="gmask3")
        op("dve", lambda e: e.max(g_m8, g_pm), [gtb], [gtb], tag="gm8")
        op("dve", lambda e: e.tensor_scalar(g_tt1, g_pm, g_m8[:, 3:4],
                                            None, ALU.is_ge),
           [gtb], [gtb], tag="keep4")
        op("dve", lambda e, tt=tt: e.tensor_tensor(
            cwv[:, tt, :], g_p, g_tt1, ALU.mult),
            [gtb], [cwb[tt]], tag="cwt")
        bk2 = nextbank()
        P.stepw("pe", lambda e, tt=tt, bk2=bk2: e.transpose(
            bank(bk2)[0:32, 0:128], cwv[:, tt, :], c_ident),
            reads=[cwb[tt], B("c_ident")], writes=[pbuf[bk2]], tag="cwT")
        op("dve", lambda e, tt=tt, bk2=bk2: e.tensor_copy(
            cw_fm[0:32, tt * 128:(tt + 1) * 128], bank(bk2)[0:32, 0:128]),
            [pbuf[bk2]], [B("cw_fm")], tag="cwTe")

    dma("sync", d_cw, cw_fm, [B("cw_fm")], [B("d_cw")], tag="st_cw")

    if "cw" in debug_dumps:
        t = dumpt("cw", (128, 512))
        dma("sync", t, cw_tm, cwb, [B("dump_cw")])

    if stop_after == "gate":
        P.finalize()
        lf = {i: c for i, c in enumerate(P.lane_count) if c}
        return nc, P, dumps, lf, stack
    sfree("cw_tm"); sfree("gt"); sfree("gwsb")
    b2sb = newv("b2sb", 32, 1024, BF16)
    dma("pool", b2sb, d_b2, [], [B("b2sb")], tag="ld_b2")
    y_sb = newv("y_sb", 128, 16 * 1024, F32,
                bufs_=[f"y{t}" for t in range(16)])
    y_sbv = y_sb.rearrange("p (c j) -> p c j", c=16)
    ytb = [B(f"y{t}") for t in range(16)]

    w1s = [newv("w1s0", 128, 4096, BF16), newv("w1s1", 128, 4096, BF16)]
    w3s = [newv("w3s0", 128, 4096, BF16), newv("w3s1", 128, 4096, BF16)]
    w2s = [newv("w2s0", 128, 4096, BF16), newv("w2s1", 128, 4096, BF16)]
    w1b = [B("w1s0"), B("w1s1")]; w3b = [B("w3s0"), B("w3s1")]
    w2b = [B("w2s0"), B("w2s1")]
    hh_fm = [newv("hh0", 128, 4 * 2048, BF16), newv("hh1", 128, 4 * 2048, BF16)]
    hh_fmv = [t.rearrange("p (c j) -> p c j", c=4) for t in hh_fm]
    hhb = [B("hh0"), B("hh1")]
    s1_s = newv("s1_s", 128, 2 * 512, F32, bufs_=["s10", "s11"])
    s1_sv = s1_s.rearrange("p (r j) -> p r j", r=2)
    s1b = [B("s10"), B("s11")]
    s4_s = newv("s4_s", 128, 2 * 512, BF16, bufs_=["s40", "s41"])
    s4_sv = s4_s.rearrange("p (r j) -> p r j", r=2)
    s4b = [B("s40"), B("s41")]
    cwrow = [newv("cwrow0", 1, 2048, BF16)] * 2
    cwrb = [B("cwrow0")] * 2
    cwbt = [newv("cwb0", 128, 2048, BF16)] * 2
    cwbb = [B("cwb0")] * 2

    b1T = newv("b1T", 128, 128, F32)
    b3T = newv("b3T", 128, 128, F32)
    sb1T = newv("sb1T", 128, 8, F32)
    sb3T = newv("sb3T", 128, 8, F32)
    bsb = newv("bsb", 32, 512, F32, side="hi")
    for (dsrc, bT, bTn) in ((d_b1, b1T, "b1T"), (d_b3, b3T, "b3T")):
        dma("sync", bsb[0:32, 0:512], dsrc, [B("bsb")], [B("bsb")],
            tag="ld_" + bTn)
        for ic in range(4):
            bk = nextbank()
            P.stepw("pe", lambda e, ic=ic, bk=bk: e.transpose(
                bank(bk)[:, 0:32], bsb[0:32, ic * 128:(ic + 1) * 128],
                c_ident[0:32, 0:32]),
                reads=[B("bsb"), B("c_ident")], writes=[pbuf[bk]], tag="Tb")
            op("act", lambda e, ic=ic, bk=bk, bT=bT: e.activation(
                bT[:, ic * 32:(ic + 1) * 32], bank(bk)[:, 0:32], AF.Copy),
                [pbuf[bk]], [B(bTn)], tag="Tbe")
    for (dsrc, bT, bTn) in ((d_sb1, sb1T, "sb1T"), (d_sb3, sb3T, "sb3T")):
        dma("sync", bsb[0:2, 0:512].rearrange("h c -> h c"),
            dsrc.rearrange("o (h c) -> (o h) c", h=2), [B("bsb")], [B("bsb")],
            tag="ld_" + bTn)
        for ic in range(4):
            bk = nextbank()
            P.stepw("pe", lambda e, ic=ic, bk=bk: e.transpose(
                bank(bk)[:, 0:2], bsb[0:2, ic * 128:(ic + 1) * 128],
                c_ident[0:2, 0:2]),
                reads=[B("bsb"), B("c_ident")], writes=[pbuf[bk]], tag="Tsb")
            op("act", lambda e, ic=ic, bk=bk, bT=bT: e.activation(
                bT[:, ic * 2:ic * 2 + 2], bank(bk)[:, 0:2], AF.Copy),
                [pbuf[bk]], [B(bTn)], tag="Tsbe")
    sfree("bsb")

    def exp_srcs(e):
        if e < 32:
            return (d_w1[e], d_w3[e], d_w2[e])
        hh = e - 32
        return (d_sw1[:, hh * 512:(hh + 1) * 512],
                d_sw3[:, hh * 512:(hh + 1) * 512],
                d_sw2[hh * 512:(hh + 1) * 512, :])

    def load_expert(e):
        sl = e % 2
        s_w1, s_w3, s_w2 = exp_srcs(e)
        dma("pool", w1s[sl].rearrange("p (c i) -> p c i", c=8),
            s_w1.rearrange("(c p) i -> p c i", p=128), [], [w1b[sl]],
            tag=f"ldw1_{e}")
        dma("pool", w3s[sl].rearrange("p (c i) -> p c i", c=8),
            s_w3.rearrange("(c p) i -> p c i", p=128), [], [w3b[sl]],
            tag=f"ldw3_{e}")
        dma("pool", w2s[sl].rearrange("p (c i) -> p c i", c=4),
            s_w2.rearrange("(c p) i -> p c i", p=128), [], [w2b[sl]],
            tag=f"ldw2_{e}")

    def h_phase(e):
        sl = e % 2
        w1v = w1s[sl].rearrange("p (c i) -> p c i", c=8)
        w3v = w3s[sl].rearrange("p (c i) -> p c i", c=8)
        if e < 32:
            dma("sync", cwrow[sl], d_cw[None, e, :], [B("d_cw")], [cwrb[sl]],
                tag=f"ldcw_{e}")
            for tf in range(4):
                bk = nextbank()
                mm(bank(bk), ones_rb, cwrow[sl][0:1, tf * 512:(tf + 1) * 512],
                   True, True, [B("ones_rb"), cwrb[sl]], [pbuf[bk]], tag="cwbm")
                op("act", lambda e_, sl=sl, tf=tf, bk=bk: e_.activation(
                    cwbt[sl][:, tf * 512:(tf + 1) * 512], bank(bk), AF.Copy),
                    [pbuf[bk]], [cwbb[sl]], tag="cwbe")
        for ic in range(4):
            if e < 32:
                b1c = b1T[:, ic * 32 + e:ic * 32 + e + 1]
                b3c = b3T[:, ic * 32 + e:ic * 32 + e + 1]
                b1cb, b3cb = B("b1T"), B("b3T")
            else:
                hx = e - 32
                b1c = sb1T[:, ic * 2 + hx:ic * 2 + hx + 1]
                b3c = sb3T[:, ic * 2 + hx:ic * 2 + hx + 1]
                b1cb, b3cb = B("sb1T"), B("sb3T")
            for tf in range(4):
                r = (ic * 4 + tf) % 2
                b1k = nextbank()
                b3k = nextbank()
                for dc in range(8):
                    mm(bank(b1k), w1v[:, dc, ic * 128:(ic + 1) * 128],
                       x1_fmv[:, dc, tf * 512:(tf + 1) * 512], dc == 0, dc == 7,
                       [B("x1_fm"), w1b[sl]], [pbuf[b1k]], tag="h1mm")
                for dc in range(8):
                    mm(bank(b3k), w3v[:, dc, ic * 128:(ic + 1) * 128],
                       x1_fmv[:, dc, tf * 512:(tf + 1) * 512], dc == 0, dc == 7,
                       [B("x1_fm"), w3b[sl]], [pbuf[b3k]], tag="h3mm")
                op("act", lambda e_, b1k=b1k, r=r, b1c=b1c: e_.activation(
                    s1_sv[:, r, :], bank(b1k), AF.Silu, bias=b1c),
                    [pbuf[b1k], b1cb], [s1b[r]], tag="silu")
                if e < 32:
                    op("dve", lambda e_, b3k=b3k, r=r, b3c=b3c:
                       e_.scalar_tensor_tensor(
                           s4_sv[:, r, :], bank(b3k), b3c, s1_sv[:, r, :],
                           ALU.add, ALU.mult),
                       [pbuf[b3k], s1b[r], b3cb], [s4b[r]], tag="h3c")
                    op("pool", lambda e_, sl=sl, ic=ic, tf=tf, r=r:
                       e_.tensor_mul(
                           hh_fmv[sl][:, ic, tf * 512:(tf + 1) * 512],
                           s4_sv[:, r, :],
                           cwbt[sl][:, tf * 512:(tf + 1) * 512]),
                       [s4b[r], cwbb[sl]], [hhb[sl]], tag="hhat")
                else:
                    op("dve", lambda e_, b3k=b3k, r=r, b3c=b3c, sl=sl, ic=ic,
                       tf=tf: e_.scalar_tensor_tensor(
                           hh_fmv[sl][:, ic, tf * 512:(tf + 1) * 512],
                           bank(b3k), b3c, s1_sv[:, r, :],
                           ALU.add, ALU.mult),
                       [pbuf[b3k], s1b[r], b3cb], [hhb[sl]], tag="hhat_s")

    def w2_phase(es, first_group):
        for tt in range(16):
            for dh in range(2):
                bk = nextbank()
                nmm = len(es) * 4 + (1 if first_group else 0)
                i = 0
                if first_group:
                    mm(bank(bk), cw_fm[0:32, tt * 128:(tt + 1) * 128],
                       b2sb[:, dh * 512:(dh + 1) * 512], True, False,
                       [B("cw_fm"), B("b2sb")], [pbuf[bk]], tag="b2cw")
                    i = 1
                for e in es:
                    sl = e % 2
                    w2v = w2s[sl].rearrange("p (c j) -> p c j", c=4)
                    for ic in range(4):
                        mm(bank(bk), hh_fmv[sl][:, ic, tt * 128:(tt + 1) * 128],
                           w2v[:, ic, dh * 512:(dh + 1) * 512],
                           i == 0, i == nmm - 1,
                           [hhb[sl], w2b[sl]], [pbuf[bk]], tag="w2mm")
                        i += 1
                sl2 = slice(dh * 512, (dh + 1) * 512)
                if first_group:
                    op("act", lambda e_, tt=tt, sl2=sl2, bk=bk: e_.activation(
                        y_sbv[:, tt, sl2], bank(bk), AF.Copy),
                        [pbuf[bk]], [ytb[tt]], tag="yw")
                else:
                    op("dve", lambda e_, tt=tt, sl2=sl2, bk=bk: e_.tensor_tensor(
                        y_sbv[:, tt, sl2], bank(bk), y_sbv[:, tt, sl2], ALU.add),
                        [pbuf[bk], ytb[tt]], [ytb[tt]], tag="yadd")

    load_expert(0)
    load_expert(1)
    h_phase(0)
    h_phase(1)
    first = True
    for g0 in range(0, NEX, 2):
        w2_phase([g0, g0 + 1], first)
        first = False
        if g0 + 2 < NEX:
            load_expert(g0 + 2)
            load_expert(g0 + 3)
            h_phase(g0 + 2)
            h_phase(g0 + 3)

    for nm_ in ("w1s0", "w1s1", "w3s0", "w3s1", "w2s0", "w2s1", "hh0", "hh1",
                "s1_s", "s4_s", "cwrow0", "cwb0",
                "b1T", "b3T", "sb1T", "sb3T",
                "cw_fm", "gbb", "b2sb", "x1_fm"):
        sfree(nm_)
    sb2b = bcast_from("sb2b", d_sb2[None, :], 1024)
    x2_s = newv("x2_s", 128, 1024, F32)
    xf2_0 = newv("xf2_0", 128, 1024, F32)
    xf2_1 = newv("xf2_1", 128, 1024, F32)
    xf2s = [(xf2_0, B("xf2_0")), (xf2_1, B("xf2_1"))]
    out_sids = []
    for tt in range(16):
        xv, xb_ = xf2s[tt % 2]
        dma("sync", xv, d_x1[tt * 128:(tt + 1) * 128, :], [B("d_x1"), xb_],
            [xb_], tag="ld_x1f")
        op("pool", lambda e, tt=tt, xv=xv: e.tensor_add(
            y_sbv[:, tt, :], y_sbv[:, tt, :], xv),
            [xb_, ytb[tt]], [ytb[tt]], tag="fadd1")
        op("pool", lambda e, tt=tt: e.tensor_add(
            y_sbv[:, tt, :], y_sbv[:, tt, :], sb2b),
            [B("sb2b"), ytb[tt]], [ytb[tt]], tag="fadd2")
        s6, s6b = ln_stats(y_sbv[:, tt, 0:512], y_sbv[:, tt, 512:1024],
                           [ytb[tt]], eps5_c, B("eps5_c"), slot=tt % 2)
        op("dve", lambda e, tt=tt, s6=s6: e.tensor_scalar(
            x2_s, y_sbv[:, tt, :], s6[:, 17:18], s6[:, 23:24],
            ALU.subtract, ALU.mult),
            [ytb[tt], s6b], [B("x2_s")], tag="x2")
        sid = dma("sync", d_final[tt * 128:(tt + 1) * 128, :], x2_s,
                  [B("x2_s")], [B("d_final")], tag="st_x2")
        out_sids.append(sid)

    P.finalize()
    lane_final = {i: c for i, c in enumerate(P.lane_count) if c}
    return nc, P, dumps, lane_final, stack


_CONSTS = _host_consts()


def _weights_dict(kw):
    return dict(
        ln_g=kw["L0_attn_ln_g"], ln_b=kw["L0_attn_ln_b"],
        mh_g=kw["L0_attn_mh_g"], mh_b=kw["L0_attn_mh_b"],
        wupl=kw["L0_attn_W_up_l"], bupl=kw["L0_attn_b_up_l"],
        wupr=kw["L0_attn_W_up_r"], bupr=kw["L0_attn_b_up_r"],
        wdown=kw["L0_attn_W_down"], bdown=kw["L0_attn_b_down"],
        convw=kw["L0_attn_conv_w"],
        convb=np.asarray(kw["L0_attn_conv_b"], np.float32).reshape(1),
        wskip=kw["L0_attn_W_skip"], bskip=kw["L0_attn_b_skip"],
        wi=kw["L0_attn_W_i"],
        bi=np.asarray(kw["L0_attn_b_i"], np.float32).reshape(H, 1),
        wf=kw["L0_attn_W_f"],
        bf=np.asarray(kw["L0_attn_b_f"], np.float32).reshape(H, 1),
        wo=kw["L0_attn_W_o"], bo=kw["L0_attn_b_o"],
        wq=kw["L0_attn_W_q"], bq=kw["L0_attn_b_q"],
        wk=kw["L0_attn_W_k"], bk=kw["L0_attn_b_k"],
        wv=kw["L0_attn_W_v"], bv=kw["L0_attn_b_v"],
        gw=kw["L0_ffn_gw"], gb=kw["L0_ffn_gb"],
        w1=kw["L0_ffn_w1"], b1=kw["L0_ffn_b1"],
        w3=kw["L0_ffn_w3"], b3=kw["L0_ffn_b3"],
        w2=kw["L0_ffn_w2"], b2=kw["L0_ffn_b2"],
        sw1=kw["L0_ffn_sw1"],
        sb1=np.asarray(kw["L0_ffn_sb1"], np.float32).reshape(1, -1),
        sw3=kw["L0_ffn_sw3"],
        sb3=np.asarray(kw["L0_ffn_sb3"], np.float32).reshape(1, -1),
        sw2=kw["L0_ffn_sw2"], sb2=kw["L0_ffn_sb2"],
    )


def _run(kw, core_ids, debug_dumps=(), stop_after=None):
    nc, P, dumps, lane_final, stack = build_program(debug_dumps=debug_dumps,
                                                    stop_after=stop_after)
    with stack:
        with ExitStack() as es2:
            esems = {s: es2.enter_context(nc.semaphore(f"es_{s}"))
                     for s in Prog.STREAMS}
            lsems = [es2.enter_context(nc.semaphore(f"lane{i}"))
                     for i in range(P.n_lanes)]
            with nc.Block() as block:
                @block.tensor
                def _(eng):
                    _emit(P, "pe", eng, esems, lsems)

                @block.vector
                def _(eng):
                    _emit(P, "dve", eng, esems, lsems)

                @block.scalar
                def _(eng):
                    _emit(P, "act", eng, esems, lsems)

                @block.gpsimd
                def _(eng):
                    _emit(P, "pool", eng, esems, lsems)

                @block.sync
                def _(eng):
                    _emit(P, "sync", eng, esems, lsems)
                    for ln, pos in sorted(lane_final.items()):
                        eng.wait_ge(lsems[ln], pos * 16)

            x_full = np.ascontiguousarray(
                np.asarray(kw["L0_attn_x"], dtype=np.float32))
            W = {k: np.ascontiguousarray(np.asarray(v, np.float32))
                 for k, v in _weights_dict(kw).items()}
            in_maps = []
            for c in core_ids:
                m = dict(_CONSTS)
                m["x"] = x_full[c]
                m.update(W)
                in_maps.append(m)
            res = run_bass_kernel_spmd(nc, in_maps, core_ids=list(core_ids))
    return res


def kernel(**kw):
    res = _run(kw, list(range(8)))
    return np.stack([np.asarray(r["x2"], np.float32) for r in res.results])
